# version 17
# baseline (speedup 1.0000x reference)
import numpy as np
import concourse.bass as bass
import concourse.mybir as mybir
from concourse.bass_utils import run_bass_kernel_spmd

F32 = mybir.dt.float32
BF16 = mybir.dt.bfloat16
I8 = mybir.dt.int8
AF = mybir.ActivationFunctionType
ALU = mybir.AluOpType
AX = mybir.AxisListType

B = 2
S = 2048
D = 1024
KC = 8
NH = 16
HD = 64
DI = 2816
IC = 22
FT = 512
NT = S // FT
KT = S // 128
CT = B * NT

ENGINES = ("sync", "gpsimd", "vector", "scalar", "tensor")


class Sem:
    def __init__(self, h):
        self.h = h
        self.n = 0

    def inc(self, k=1):
        self.n += k
        return self.n


def build_nc():
    nc = bass.Bass("TRN2", target_bir_lowering=False)

    DT = lambda n, sh, dt=F32, kind="ExternalInput": nc.dram_tensor(n, sh, dt, kind=kind)
    xT = DT("xT", [B * D, S])
    wqkvT = DT("wqkvT", [D, 3 * D])
    woutT = DT("woutT", [D, D])
    wg1T = DT("wg1T", [D, HD])
    wg2T = DT("wg2T", [HD, D])
    wgTd = DT("wgT", [D, DI])
    wuTd = DT("wuT", [D, DI])
    wdTd = DT("wdT", [DI, D])
    wg_nat = DT("wg_nat", [DI, D])
    wu_nat = DT("wu_nat", [DI, D])
    wd_nat = DT("wd_nat", [D, DI])
    wmodd = DT("wmod", [D, 6 * D])
    cTd = DT("cT", [128, 2 * KC])
    bq_d = DT("bq", [128, KC])
    bv_row_d = DT("bv_row", [1, D])
    bg1_d = DT("bg1", [HD, 1])
    bg2_d = DT("bg2", [128, KC])
    bout_d = DT("bout", [128, KC])
    lng_d = DT("lng", [128, KC])
    lnb_d = DT("lnb", [128, KC])
    clip_d = DT("clip2", [1, 2])
    import os as _os0
    KDBG = bool(int(_os0.environ.get("KDBG", "0")))
    _K1 = "ExternalOutput" if KDBG else "Internal"
    out_d = DT("out", [B * D, S], F32, "ExternalOutput")
    x2T = DT("x2T", [B * D, S], F32, _K1)
    tsp = DT("tsp", [DI, B * S], BF16, _K1)
    cc1i = DT("cc1i", [1, 16], F32, "Internal")
    cc1o = DT("cc1o", [1, 16], F32, "Internal")
    cc2i = DT("cc2i", [1, 16], F32, "Internal")
    cc2o = DT("cc2o", [1, 16], F32, "Internal")

    id_dram = nc.inline_tensor(np.eye(128, dtype=np.float32), "ident")
    onescol_dram = nc.inline_tensor(np.ones((128, 1), np.float32), "onescol")
    onesrow_dram = nc.inline_tensor(np.ones((1, 128), np.float32), "onesrow")

    A = lambda name, shape, dt=F32: nc.alloc_sbuf_tensor(name, shape, dt)
    id_f = A("id_f", [128, 128])
    onescol_bf = A("onescol_bf", [128, 1], BF16)
    onesrow_f = A("onesrow_f", [1, 128])
    onesrow_bf = A("onesrow_bf", [1, 128], BF16)
    mod6 = A("mod6", [128, 6 * 2 * KC])
    sa_v = A("sa_v", [128, 2 * KC])
    ca1_v = A("ca1_v", [128, 2 * KC])
    gap_v = A("gap_v", [128, 2 * KC])
    sb_v = A("sb_v", [128, 2 * KC])
    cb1_v = A("cb1_v", [128, 2 * KC])
    gbp_v = A("gbp_v", [128, 2 * KC])
    gabo_v = A("gabo_v", [128, 2 * KC])
    sc16a = A("sc16a", [128, 2 * KC])
    sc16b = A("sc16b", [128, 2 * KC])
    bq_v = A("bq_v", [128, KC])
    bg2_v = A("bg2_v", [128, KC])
    bout_v = A("bout_v", [128, KC])
    lng_v = A("lng_v", [128, KC])
    lnb_v = A("lnb_v", [128, KC])
    bg1_v = A("bg1_v", [HD, 1])
    c_sb = A("c_sb", [128, 2 * KC])
    c_bf = A("c_bf", [128, 2 * KC], BF16)
    clip_v = A("clip_v", [1, 2])
    bvbc = A("bvbc", [128, D])
    wrk = A("wrk", [128, 16])
    wrow = A("wrow", [1, 32])
    den_v = A("den_v", [128, 2])
    rmxg = A("rmxg", [128, IC])
    rmxu = A("rmxu", [128, IC])
    rmxd = A("rmxd", [128, KC])
    tmall = A("tmall", [128, IC * CT])
    sgc_g = A("sgc_g", [128, IC])
    susx = A("susx", [128, IC])
    qsc = A("qsc", [128, IC])
    comb = A("comb", [128, 2 * KC])
    tmaxr = A("tmaxr", [128, IC])
    cq22 = A("cq22", [128, IC])
    cq8 = A("cq8", [128, KC])
    sc8 = A("sc8", [128, KC])
    ccs_sb = A("ccs_sb", [1, 16])
    cv = A("cv", [1, 4])
    ccr_sb = A("ccr_sb", [1, 16])

    WK_MRUN, WK_NEGM, WK_HMAX, WK_TMP, WK_SXB, WK_RSX, WK_SDB, WK_RSD, WK_TMX = range(9)
    WR_M, WR_NEGM, WR_HMAX, WR_SIG, WR_SX, WR_RSX, WR_TMX, WR_SIGD, WR_SD, WR_RSD = range(10)

    X0 = (nc.sbuf_base + 31) & ~31
    TOP = nc.sbuf_top

    def AT(name, shape, dt, off):
        assert off % 32 == 0, name
        nbytes = int(np.prod(shape[1:])) * mybir.dt.size(dt)
        assert X0 <= off and off + nbytes <= TOP, (name, off, nbytes, X0, TOP)
        return nc.alloc_sbuf_tensor_at(name, shape, dt, offset=off)

    K = 1024
    Z = X0
    t1a = AT("t1a", [128, FT], BF16, Z + 0 * K)
    t1b = AT("t1b", [128, FT], BF16, Z + 1 * K)
    t2a = AT("t2a", [128, FT], BF16, Z + 2 * K)
    t2b = AT("t2b", [128, FT], BF16, Z + 3 * K)
    sqa = AT("sqa", [128, FT], BF16, Z + 4 * K)
    sqb = AT("sqb", [128, FT], BF16, Z + 5 * K)
    rb_a = AT("rb_a", [128, FT], BF16, Z + 6 * K)
    mb_a = AT("mb_a", [128, FT], BF16, Z + 7 * K)
    eka = AT("eka", [128, D], BF16, Z + 8 * K)
    ekb = AT("ekb", [128, D], BF16, Z + 10 * K)
    muT = AT("muT", [1, FT], F32, Z + 12 * K)
    e2T = AT("e2T", [1, FT], F32, Z + 14 * K)
    tmpT = AT("tmpT", [1, FT], F32, Z + 16 * K)
    varT = AT("varT", [1, FT], F32, Z + 18 * K)
    rT = AT("rT", [1, FT], F32, Z + 20 * K)
    mrT = AT("mrT", [1, FT], F32, Z + 22 * K)
    rT_bf = AT("rT_bf", [1, FT], BF16, Z + 24 * K)
    mrT_bf = AT("mrT_bf", [1, FT], BF16, Z + 25 * K)
    bvrow_sb = AT("bvrow_sb", [1, D], F32, Z + 26 * K)

    Z1 = Z + 31 * K

    wkv_bf = AT("wkv_bf", [128, KC * 2 * D], BF16, Z1)
    wq_bf = AT("wq_bf", [128, KC * D], BF16, Z1)
    wout_bf = AT("wout_bf", [128, KC * D], BF16, Z1 + 32 * K)
    wg1_bf = AT("wg1_bf", [128, KC * HD], BF16, Z1 + 48 * K)
    wg2_bf = AT("wg2_bf", [HD, D], BF16, Z1 + 49 * K)
    xh_bf = AT("xh_bf", [128, KC * S], BF16, Z1 + 52 * K)
    R1 = Z1 + 84 * K
    k_bf = AT("k_bf", [128, KT * D], BF16, R1)
    q_bf = AT("q_bf", [128, KC * FT], BF16, R1)
    y_bf = AT("y_bf", [128, KC * FT], BF16, R1 + 8 * K)
    z_bf = AT("z_bf", [128, KC * FT], BF16, R1 + 16 * K)
    gate_bf = AT("gate_bf", [128, KC * FT], BF16, R1 + 24 * K)
    R2 = Z1 + 116 * K
    v_sb = AT("v_sb", [128, KT * NH * 65], BF16, R2)
    e_f = AT("e_f", [128, FT], F32, R2)
    xf_a = AT("xf_a", [128, FT], F32, R2 + 2 * K)
    xf_b = AT("xf_b", [128, FT], F32, R2 + 4 * K)
    x2f_a = AT("x2f_a", [128, FT], F32, R2 + 6 * K)
    x2f_b = AT("x2f_b", [128, FT], F32, R2 + 8 * K)
    gh_bf = AT("gh_bf", [HD, FT], BF16, R2 + 10 * K)
    kvtmp = AT("kvtmp", [128, HD], F32, R2 + 12 * K)
    kv_bfs = AT("kv_bfs", [128, 8 * HD], BF16, R2 + 13 * K)
    assert R2 + 34 * K <= TOP

    wmod_bf = AT("wmod_bf", [128, KC * 6 * D], BF16, Z1)
    wm32 = AT("wm32", [128, KC * 128], F32, Z1 + 97 * K)
    wm32b = AT("wm32b", [128, KC * 1024], F32, Z1)

    x2t_bf = AT("x2t_bf", [128, KC * FT], BF16, Z1)
    x2t_bf2 = AT("x2t_bf2", [128, KC * FT], BF16, Z1 + 8 * K)
    xq_i8 = AT("xq_i8", [128, KC * B * S], I8, Z1)
    h2_bf = AT("h2_bf", [128, KC * B * S], BF16, Z1 + 33 * K)

    wgq_bf = AT("wgq_bf", [128, KC * DI], BF16, Z1 + 33 * K)
    wuq_bf = AT("wuq_bf", [128, KC * DI], BF16, Z1 + 77 * K)
    cbc_f = AT("cbc_f", [128, DI], F32, Z1 + 121 * K)
    wtf_f = AT("wtf_f", [128, DI], F32, Z1 + 132 * K)
    crow = AT("crow", [1, DI], F32, Z1 + 132 * K)
    cqT22 = AT("cqT22", [IC, 128], F32, Z1 + 143 * K)
    wq8b = AT("wq8b", [128, DI], I8, Z1 + 144 * K)
    natq = AT("natq", [128, D], F32, Z1 + 147 * K)
    xqb = AT("xqb", [128, KC * FT], BF16, Z1 + 151 * K)
    gs_a = AT("gs_a", [128, FT], BF16, Z1 + 159 * K)
    gs_b = AT("gs_b", [128, FT], BF16, Z1 + 160 * K)
    up_a = AT("up_a", [128, FT], BF16, Z1 + 161 * K)
    up_b = AT("up_b", [128, FT], BF16, Z1 + 162 * K)
    tb_a = AT("tb_a", [128, FT], BF16, Z1 + 163 * K)
    tb_b = AT("tb_b", [128, FT], BF16, Z1 + 164 * K)
    assert Z1 + 165 * K <= TOP, (Z1 + 165 * K, TOP)

    wdq_bf = AT("wdq_bf", [128, IC * D], BF16, Z1)
    tqb_a = AT("tqb_a", [128, IC * FT], BF16, Z1 + 44 * K)
    tqb_b = AT("tqb_b", [128, IC * FT], BF16, Z1 + 66 * K)
    cbcd_f = AT("cbcd_f", [128, D], F32, Z1 + 88 * K)
    wtd_f = AT("wtd_f", [128, DI], F32, Z1 + 92 * K)
    natd = AT("natd", [128, DI], F32, Z1 + 103 * K)
    crowd = AT("crowd", [1, D], F32, Z1 + 114 * K)
    cqT8 = AT("cqT8", [KC, 128], F32, Z1 + 118 * K)
    wd8b = AT("wd8b", [128, D], I8, Z1 + 119 * K)
    tq8_a = AT("tq8_a", [128, FT], I8, Z1 + 120 * K)
    tq8_b = AT("tq8_b", [128, FT], I8, Z1 + 121 * K)
    of_a = AT("of_a", [128, FT], F32, Z1 + 122 * K)
    of_b = AT("of_b", [128, FT], F32, Z1 + 124 * K)
    trd_a = AT("trd_a", [128, FT], BF16, Z1 + 126 * K)
    trd_b = AT("trd_b", [128, FT], BF16, Z1 + 127 * K)
    dtmp_a = AT("dtmp_a", [128, FT], F32, Z1 + 128 * K)
    dtmp_b = AT("dtmp_b", [128, FT], F32, Z1 + 130 * K)
    xfd_a = AT("xfd_a", [128, FT], F32, Z1 + 132 * K)
    xfd_b = AT("xfd_b", [128, FT], F32, Z1 + 134 * K)

    pb = [nc.alloc_psum_tensor(f"pb{i}", [128, 512], F32) for i in range(8)]

    import contextlib
    es = contextlib.ExitStack()
    SEM = lambda n: Sem(es.enter_context(nc.semaphore(n)))
    LD, GD, STm = SEM("LD"), SEM("GD"), SEM("STm")
    PEs, VEs, SEs, GEs = SEM("PEs"), SEM("VEs"), SEM("SEs"), SEM("GEs")
    CCs, PHB = SEM("CCs"), SEM("PHB")
    ESEM = {"vector": VEs, "scalar": SEs, "tensor": PEs, "gpsimd": GEs,
            "sync": None}
    DSEM = {"sync": LD, "gpsimd": GD}

    prog = {en: [] for en in ENGINES}

    def E(en, f):
        prog[en].append(f)

    def emit(en, f, inc=False):
        if not inc:
            E(en, f)
            return None
        s = ESEM[en]
        v = s.inc(1)
        E(en, lambda e, ff=f, hh=s.h: ff(e).then_inc(hh, 1))
        return v

    def wait(en, sem, val):
        if val is None:
            return
        E(en, lambda e, hh=sem.h, v=val: e.wait_ge(hh, v))

    def dma(queue, out, in_, sem=None):
        s = sem if sem is not None else DSEM[queue]
        v = s.inc(16)
        E(queue, lambda e, o=out, i=in_, hh=s.h: e.dma_start(out=o, in_=i).then_inc(hh, 16))
        return v

    def bar():
        tgt = PHB.inc(5)
        for en in ENGINES:
            E(en, lambda e, hh=PHB.h: e.nop().then_inc(hh, 1))
            E(en, lambda e, hh=PHB.h, t=tgt: e.wait_ge(hh, t))

    freemap = {}

    def guard(en, name):
        if name in freemap:
            s, v = freemap.pop(name)
            wait(en, s, v)

    def release(name, sem, val):
        freemap[name] = (sem, val)

    def dbg(name, tens):
        if not KDBG:
            return
        sh = list(tens.shape)
        d = nc.dram_tensor("dbg_" + name, sh, tens.dtype, kind="ExternalOutput")
        dma("gpsimd", d[:, :] if len(sh) == 2 else d[:], tens[:])

    import os as _os
    _klim = int(_os.environ.get("KPHASE", "999"))
    class _StopSched(Exception):
        pass
    _phctr = [0]
    _orig_bar = bar
    def bar():
        _orig_bar()
        _phctr[0] += 1
        if _phctr[0] >= _klim:
            raise _StopSched()

    def _sched():
        def qw_rowmax(nat_d, rmx, oc_n, w_in, bufs):
            for oc in range(oc_n):
                nb, key = bufs[oc % len(bufs)]
                guard("sync", key)
                vld = dma("sync", nb[:, 0:w_in], nat_d[oc * 128:(oc + 1) * 128, :])
                wait("vector", LD, vld)
                vr = emit("vector", lambda e, oc=oc, nb=nb, rm=rmx, w_in=w_in: e.reduce_max(
                    rm[:, oc:oc + 1], nb[:, 0:w_in], AX.X, apply_absolute_value=True), inc=True)
                release(key, VEs, vr)

        dma("sync", id_f[:], id_dram[:, :])
        dma("sync", c_sb[:], cTd[:, :])
        dma("sync", bq_v[:], bq_d[:, :])
        dma("sync", bg2_v[:], bg2_d[:, :])
        dma("sync", bout_v[:], bout_d[:, :])
        dma("sync", lng_v[:], lng_d[:, :])
        dma("sync", lnb_v[:], lnb_d[:, :])
        dma("sync", bg1_v[:], bg1_d[:, :])
        dma("sync", clip_v[:], clip_d[:, :])
        dma("sync", onesrow_f[:], onesrow_dram[:, :])
        dma("sync", bvrow_sb[:], bv_row_d[:, :])
        dma("gpsimd", onescol_bf[:], onescol_dram[:, :])
        dma("gpsimd", onesrow_bf[:], onesrow_dram[:, :])
        for kk in range(KC):
            dma("gpsimd", wmod_bf[:, kk * 6 * D:(kk + 1) * 6 * D],
                wmodd[kk * 128:(kk + 1) * 128, :])
        v_ld0, v_gd0 = LD.n, GD.n
        bar()
        for en in ("vector", "scalar", "tensor"):
            wait(en, LD, v_ld0)
            wait(en, GD, v_gd0)

        emit("vector", lambda e: e.memset(ccs_sb[:], 0.0), inc=True)
        emit("vector", lambda e: e.memset(cv[0:1, 0:1], 1.0 / D), inc=True)
        emit("vector", lambda e: e.memset(cv[0:1, 1:2], 1e-5), inc=True)
        vcv = emit("vector", lambda e: e.memset(cv[0:1, 2:3], -1.0), inc=True)
        vsc = emit("scalar", lambda e: e.activation(c_sb[:], c_sb[:], AF.Silu), inc=True)
        wait("vector", SEs, vsc)
        vcb = emit("vector", lambda e: e.tensor_copy(c_bf[:], c_sb[:]), inc=True)
        for j in range(2):
            emit("tensor", lambda e, j=j: e.matmul(
                pb[2 + j][:, 0:512], onesrow_f[0:1, :],
                bvrow_sb[0:1, j * 512:(j + 1) * 512], start=True, stop=True), inc=True)
        vpe_bv = PEs.n
        wait("vector", PEs, vpe_bv)
        for j in range(2):
            vv = emit("vector", lambda e, j=j: e.tensor_copy(
                bvbc[:, j * 512:(j + 1) * 512], pb[2 + j][:, 0:512]), inc=True)
            release(f"ps{2 + j}", VEs, vv)
        wait("tensor", VEs, vcb)
        for mc in [m for m in range(48) if not (16 <= m < 24 or 40 <= m < 48)]:
            bk = mc % 2
            guard("tensor", f"ps{bk}")
            for kk in range(KC):
                emit("tensor", lambda e, mc=mc, kk=kk, bk=bk: e.matmul(
                    pb[bk][:, 0:2],
                    wmod_bf[:, kk * 6 * D + mc * 128:kk * 6 * D + mc * 128 + 128],
                    c_bf[:, kk * 2:kk * 2 + 2],
                    start=(kk == 0), stop=(kk == KC - 1)), inc=(kk == KC - 1))
            vmm = PEs.n
            wait("vector", PEs, vmm)
            Vv, ff = mc // 8, mc % 8
            for b_ in range(2):
                vv = emit("vector", lambda e, bk=bk, b_=b_, Vv=Vv, ff=ff: e.tensor_copy(
                    mod6[:, Vv * 16 + b_ * 8 + ff:Vv * 16 + b_ * 8 + ff + 1],
                    pb[bk][:, b_:b_ + 1]), inc=True)
            release(f"ps{bk}", VEs, vv)
        release("wm32b", PEs, PEs.n)
        for blk, mc0 in enumerate((16, 40)):
            vld = 0
            for kk in range(KC):
                if kk == 0:
                    guard("sync", "wm32b")
                vld = dma("sync", wm32b[:, kk * 1024:(kk + 1) * 1024],
                          wmodd[kk * 128:(kk + 1) * 128, mc0 * 128:(mc0 + 8) * 128])
            wait("tensor", LD, vld)
            for j in range(8):
                mc = mc0 + j
                bk = j % 2
                guard("tensor", f"ps{bk}")
                for kk in range(KC):
                    emit("tensor", lambda e, kk=kk, bk=bk, j=j: e.matmul(
                        pb[bk][:, 0:2],
                        wm32b[:, kk * 1024 + j * 128:kk * 1024 + j * 128 + 128],
                        c_sb[:, kk * 2:kk * 2 + 2],
                        start=(kk == 0), stop=(kk == KC - 1)), inc=(kk == KC - 1))
                vmm = PEs.n
                wait("vector", PEs, vmm)
                Vv, ff = mc // 8, mc % 8
                for b_ in range(2):
                    vv = emit("vector", lambda e, bk=bk, b_=b_, Vv=Vv, ff=ff: e.tensor_copy(
                        mod6[:, Vv * 16 + b_ * 8 + ff:Vv * 16 + b_ * 8 + ff + 1],
                        pb[bk][:, b_:b_ + 1]), inc=True)
                release(f"ps{bk}", VEs, vv)
            release("wm32b", PEs, PEs.n)
        emit("vector", lambda e: e.drain())
        emit("vector", lambda e: e.tensor_copy(sa_v[:], mod6[:, 0:16]), inc=True)
        emit("vector", lambda e: e.tensor_scalar(ca1_v[:], mod6[:, 16:32], 1.0, None, ALU.add), inc=True)
        emit("vector", lambda e: e.tensor_copy(sb_v[:], mod6[:, 48:64]), inc=True)
        emit("vector", lambda e: e.tensor_scalar(cb1_v[:], mod6[:, 64:80], 1.0, None, ALU.add), inc=True)
        for (src0, dstv) in ((32, gap_v), (80, gbp_v)):
            emit("vector", lambda e, s0=src0: e.tensor_single_scalar(
                sc16a[:], mod6[:, s0:s0 + 16], 0.0, ALU.is_ge), inc=True)
            emit("vector", lambda e: e.drain())
            emit("vector", lambda e: e.tensor_scalar(
                sc16b[:], sc16a[:], 2.0, -1.0, ALU.mult, ALU.add), inc=True)
            emit("vector", lambda e, s0=src0: e.tensor_scalar(
                sc16a[:], mod6[:, s0:s0 + 16], -1.0, 0.25, ALU.mult, ALU.max), inc=True)
            emit("vector", lambda e: e.drain())
            emit("vector", lambda e, s0=src0: e.tensor_tensor(
                sc16a[:], sc16a[:], mod6[:, s0:s0 + 16], ALU.max), inc=True)
            emit("vector", lambda e: e.drain())
            emit("vector", lambda e, d=dstv: e.tensor_tensor(
                d[:], sc16a[:], sc16b[:], ALU.mult), inc=True)
            emit("vector", lambda e: e.drain())
        for b_ in range(2):
            emit("vector", lambda e, b_=b_: e.tensor_tensor(
                gabo_v[:, b_ * 8:b_ * 8 + 8], gap_v[:, b_ * 8:b_ * 8 + 8], bout_v[:],
                ALU.mult), inc=True)
        emit("vector", lambda e: e.drain())
        bar()
        for nm, tv in (("mod6", mod6), ("sa", sa_v), ("ca1", ca1_v), ("gap", gap_v),
                       ("sb", sb_v), ("cb1", cb1_v), ("gbp", gbp_v), ("gabo", gabo_v),
                       ("csilu", c_sb), ("bvbc", bvbc)):
            dbg(nm, tv)
        bar()

        for kk in range(KC):
            dma("gpsimd", wkv_bf[:, kk * 2 * D:(kk + 1) * 2 * D],
                wqkvT[kk * 128:(kk + 1) * 128, D:3 * D])
        for kk in range(KC):
            dma("gpsimd", wout_bf[:, kk * D:(kk + 1) * D], woutT[kk * 128:(kk + 1) * 128, :])
        for kk in range(KC):
            dma("gpsimd", wg1_bf[:, kk * HD:(kk + 1) * HD], wg1T[kk * 128:(kk + 1) * 128, :])
        v_wload = dma("gpsimd", wg2_bf[:], wg2T[:, :])

        def ln_tile(srcs, out_writer, chunk_wait=None):
            guard("tensor", "ps0")
            guard("tensor", "ps1")
            for c in range(KC):
                sq = sqa if c % 2 == 0 else sqb
                guard("scalar", f"sq{c % 2}")
                if chunk_wait is not None:
                    w = chunk_wait(c)
                    if w is not None:
                        wait("scalar", *w)
                        wait("vector", *w)
                vsq = emit("scalar", lambda e, s=srcs[c], q=sq: e.activation(
                    q[:], s, AF.Square), inc=True)
                wait("tensor", SEs, vsq)
                emit("tensor", lambda e, s=srcs[c], c=c: e.matmul(
                    pb[0][0:1, 0:512], onescol_bf[:, 0:1], s,
                    start=(c == 0), stop=(c == KC - 1)), inc=(c == KC - 1))
                vq = emit("tensor", lambda e, q=sq, c=c: e.matmul(
                    pb[1][0:1, 0:512], onescol_bf[:, 0:1], q[:],
                    start=(c == 0), stop=(c == KC - 1)), inc=True)
                release(f"sq{c % 2}", PEs, vq)
            vstat = PEs.n
            wait("scalar", PEs, vstat)
            emit("scalar", lambda e: e.activation(muT[:], pb[0][0:1, 0:512], AF.Identity,
                                                  scale=cv[0:1, 0:1]), inc=True)
            v1 = emit("scalar", lambda e: e.activation(e2T[:], pb[1][0:1, 0:512], AF.Identity,
                                                       scale=cv[0:1, 0:1]), inc=True)
            release("ps0", SEs, v1)
            release("ps1", SEs, v1)
            wait("vector", SEs, v1)
            emit("vector", lambda e: e.tensor_tensor(tmpT[:], muT[:], muT[:], ALU.mult), inc=True)
            v2 = emit("vector", lambda e: e.tensor_tensor(varT[:], e2T[:], tmpT[:], ALU.subtract), inc=True)
            wait("scalar", VEs, v2)
            v3 = emit("scalar", lambda e: e.activation(tmpT[:], varT[:], AF.Sqrt,
                                                       bias=cv[0:1, 1:2]), inc=True)
            wait("vector", SEs, v3)
            emit("vector", lambda e: e.reciprocal(rT[:], tmpT[:]), inc=True)
            emit("vector", lambda e: e.tensor_tensor(mrT[:], muT[:], rT[:], ALU.mult), inc=True)
            guard("vector", "rtbf")
            emit("vector", lambda e: e.tensor_copy(rT_bf[:], rT[:]), inc=True)
            v4 = emit("vector", lambda e: e.tensor_copy(mrT_bf[:], mrT[:]), inc=True)
            guard("tensor", "ps2")
            guard("tensor", "ps3")
            wait("tensor", VEs, v4)
            emit("tensor", lambda e: e.matmul(pb[2][:, 0:512], onesrow_bf[0:1, :], rT_bf[0:1, :],
                                              start=True, stop=True), inc=True)
            v5 = emit("tensor", lambda e: e.matmul(pb[3][:, 0:512], onesrow_bf[0:1, :], mrT_bf[0:1, :],
                                                   start=True, stop=True), inc=True)
            release("rtbf", PEs, v5)
            wait("vector", PEs, v5)
            guard("vector", "rbmb")
            emit("vector", lambda e: e.tensor_copy(rb_a[:], pb[2][:, 0:512]), inc=True)
            v6 = emit("vector", lambda e: e.tensor_copy(mb_a[:], pb[3][:, 0:512]), inc=True)
            release("ps2", VEs, v6)
            release("ps3", VEs, v6)
            lastv = None
            for c in range(KC):
                t1 = t1a if c % 2 == 0 else t1b
                t2 = t2a if c % 2 == 0 else t2b
                guard("vector", f"t12{c % 2}")
                emit("vector", lambda e, s=srcs[c], t=t1: e.tensor_tensor(
                    t[:], s, rb_a[:], ALU.mult), inc=True)
                v7 = emit("vector", lambda e, t=t1, u=t2: e.tensor_tensor(
                    u[:], t[:], mb_a[:], ALU.subtract), inc=True)
                en, fn = out_writer(c, t2)
                wait(en, VEs, v7)
                v8 = emit(en, fn, inc=True)
                release(f"t12{c % 2}", ESEM[en], v8)
                lastv = (ESEM[en], v8)
            release("rbmb", *lastv)
            bar()
            return lastv

        def xh_sl(c, t0, n):
            return xh_bf[:, c * S + t0:c * S + t0 + n]

        v_x2store = None
        for s in range(B):
            guard("gpsimd", "xh")
            xdmav = []
            for c in range(KC):
                xdmav.append(dma("gpsimd", xh_bf[:, c * S:(c + 1) * S],
                                 xT[s * D + c * 128:s * D + (c + 1) * 128, :]))
            if s == 1:
                guard("gpsimd", "Wreg")
                for kk in range(KC):
                    dma("gpsimd", wkv_bf[:, kk * 2 * D:(kk + 1) * 2 * D],
                        wqkvT[kk * 128:(kk + 1) * 128, D:3 * D])
            v_wkv = GD.n if s == 1 else v_wload
            emit("vector", lambda e: e.memset(wrk[:, WK_MRUN:WK_MRUN + 1], -1e30), inc=True)
            emit("vector", lambda e: e.memset(
                v_sb[:, :].rearrange("p (x e) -> p x e", e=65)[:, :, 64:65], 1.0), inc=True)
            wait("tensor", GD, v_wkv)

            def emit_kv(kt):
                for half, bks in ((0, (4, 5)), (1, (6, 7))):
                    for ncnk in range(2):
                        bk = bks[ncnk]
                        guard("tensor", f"ps{bk}")
                        for c in range(KC):
                            emit("tensor", lambda e, kt=kt, c=c, h=half, ncnk=ncnk, bk=bk: e.matmul(
                                pb[bk][:, 0:512],
                                xh_bf[:, c * S + kt * 128:c * S + kt * 128 + 128],
                                wkv_bf[:, c * 2 * D + h * D + ncnk * 512:
                                       c * 2 * D + h * D + ncnk * 512 + 512],
                                start=(c == 0), stop=(c == KC - 1)),
                                inc=(c == KC - 1))
                vkv = PEs.n
                wait("vector", PEs, vkv)
                for j, bk in enumerate((4, 5)):
                    vv = emit("vector", lambda e, kt=kt, j=j, bk=bk: e.tensor_copy(
                        k_bf[:, kt * D + j * 512:kt * D + j * 512 + 512],
                        pb[bk][:, 0:512]), inc=True)
                    release(f"ps{bk}", VEs, vv)
                for j, bk in enumerate((6, 7)):
                    vv = emit("vector", lambda e, kt=kt, j=j, bk=bk: e.tensor_copy(
                        v_sb[:, kt * 1040 + j * 520:kt * 1040 + (j + 1) * 520]
                            .rearrange("p (h e) -> p h e", e=65)[:, :, 0:64],
                        pb[bk][:, 0:512].rearrange("p (h e) -> p h e", e=64)), inc=True)
                    release(f"ps{bk}", VEs, vv)
                emit("vector", lambda e, kt=kt: e.reduce_max(
                    wrk[:, WK_TMP:WK_TMP + 1], k_bf[:, kt * D:(kt + 1) * D],
                    AX.X), inc=True)
                emit("vector", lambda e: e.drain())
                emit("vector", lambda e: e.tensor_tensor(
                    wrk[:, WK_MRUN:WK_MRUN + 1], wrk[:, WK_MRUN:WK_MRUN + 1],
                    wrk[:, WK_TMP:WK_TMP + 1], ALU.max), inc=True)
                emit("vector", lambda e: e.drain())

            for t in range(NT):
                def w_h(s=s, t=t):
                    def w(c, t2buf):
                        return ("scalar", lambda e, c=c, u=t2buf: e.activation(
                            xh_sl(c, t * FT, FT), u[:], AF.Identity,
                            scale=ca1_v[:, s * KC + c:s * KC + c + 1],
                            bias=sa_v[:, s * KC + c:s * KC + c + 1]))
                    return w
                ln_tile([xh_sl(c, t * FT, FT) for c in range(KC)], w_h(),
                        chunk_wait=lambda c, vs=tuple(xdmav): (GD, vs[c]))
                for kt in range(4 * t, 4 * t + 4):
                    emit_kv(kt)
            bar()
            if s == 0:
                dbg("h_s0", xh_bf)
                bar()

            bar()
            if s == 0:
                dbg("k_s0", k_bf)
                dbg("v_s0", v_sb)
                bar()

            emit("tensor", lambda e: e.transpose(
                pb[4][0:1, 0:128], wrk[:, WK_MRUN:WK_MRUN + 1], id_f[:]), inc=True)
            vt = PEs.n
            wait("vector", PEs, vt)
            vv = emit("vector", lambda e: e.reduce_max(
                wrow[0:1, WR_M:WR_M + 1], pb[4][0:1, 0:128], AX.X), inc=True)
            wait("scalar", VEs, vv)
            vsc = emit("scalar", lambda e: e.activation(
                wrow[0:1, WR_NEGM:WR_NEGM + 1], wrow[0:1, WR_M:WR_M + 1],
                AF.Identity, scale=cv[0:1, 2:3]), inc=True)
            wait("tensor", SEs, vsc)
            emit("tensor", lambda e: e.matmul(
                pb[4][:, 0:1], onesrow_f[0:1, :], wrow[0:1, WR_NEGM:WR_NEGM + 1],
                start=True, stop=True), inc=True)
            vt2 = PEs.n
            wait("vector", PEs, vt2)
            vnm = emit("vector", lambda e: e.tensor_copy(
                wrk[:, WK_NEGM:WK_NEGM + 1], pb[4][:, 0:1]), inc=True)
            bar()

            for kt in range(KT):
                ek = eka if kt % 2 == 0 else ekb
                guard("scalar", f"ek{kt % 2}")
                vek = emit("scalar", lambda e, kt=kt, k=ek: e.activation(
                    k[:], k_bf[:, kt * D:(kt + 1) * D], AF.Exp,
                    bias=wrk[:, WK_NEGM:WK_NEGM + 1]), inc=True)
                wait("tensor", SEs, vek)
                for p in range(8):
                    if kt == 0:
                        guard("tensor", f"ps{p}")
                    emit("tensor", lambda e, kt=kt, p=p, k=ek: e.matmul(
                        pb[p][:, 0:130],
                        k[:, p * 128:(p + 1) * 128],
                        v_sb[:, kt * 1040 + p * 130:kt * 1040 + (p + 1) * 130],
                        start=(kt == 0), stop=(kt == KT - 1)),
                        inc=(p == 7))
                release(f"ek{kt % 2}", PEs, PEs.n)
            vkvacc = PEs.n
            wait("vector", PEs, vkvacc)
            guard("vector", "kv_bfs")
            for p in range(8):
                emit("vector", lambda e, p=p: e.tensor_copy(
                    den_v[0:64, 0:1], pb[p][0:64, 64:65]), inc=True)
                emit("vector", lambda e, p=p: e.tensor_copy(
                    den_v[64:128, 0:1], pb[p][64:128, 129:130]), inc=True)
                emit("vector", lambda e: e.drain())
                emit("vector", lambda e, p=p: e.scalar_tensor_tensor(
                    kvtmp[0:64, :], bvbc[0:64, p * 128:p * 128 + 64],
                    den_v[0:64, 0:1], pb[p][0:64, 0:64], ALU.mult, ALU.add), inc=True)
                emit("vector", lambda e, p=p: e.scalar_tensor_tensor(
                    kvtmp[64:128, :], bvbc[64:128, p * 128 + 64:p * 128 + 128],
                    den_v[64:128, 0:1], pb[p][64:128, 65:129], ALU.mult, ALU.add), inc=True)
                emit("vector", lambda e: e.drain())
                vv = emit("vector", lambda e: e.reciprocal(den_v[:, 1:2], den_v[:, 0:1]), inc=True)
                emit("vector", lambda e: e.drain())
                release(f"ps{p}", VEs, vv)
                wait("scalar", VEs, vv)
                vkvp = emit("scalar", lambda e, p=p: e.activation(
                    kv_bfs[:, p * HD:(p + 1) * HD], kvtmp[:, :], AF.Identity,
                    scale=den_v[:, 1:2]), inc=True)
            bar()
            if s == 0:
                dbg("kv_s0", kv_bfs)
                dbg("wrk_s0", wrk)
                bar()

            guard("gpsimd", "Wreg")
            for kk in range(KC):
                dma("gpsimd", wq_bf[:, kk * D:(kk + 1) * D],
                    wqkvT[kk * 128:(kk + 1) * 128, 0:D])
            v_wq = GD.n
            wait("tensor", GD, v_wq)
            for t in range(NT):
                for oc in range(KC):
                    bk = oc % 2
                    guard("tensor", f"ps{bk}")
                    for c in range(KC):
                        emit("tensor", lambda e, oc=oc, c=c, t=t, bk=bk: e.matmul(
                            pb[bk][:, 0:512],
                            wq_bf[:, c * D + oc * 128:c * D + oc * 128 + 128],
                            xh_sl(c, t * FT, FT),
                            start=(c == 0), stop=(c == KC - 1)), inc=(c == KC - 1))
                    vq = PEs.n
                    wait("scalar", PEs, vq)
                    vqe = emit("scalar", lambda e, oc=oc, bk=bk: e.activation(
                        q_bf[:, oc * FT:(oc + 1) * FT], pb[bk][:, 0:512], AF.Silu,
                        bias=bq_v[:, oc:oc + 1]), inc=True)
                    release(f"ps{bk}", SEs, vqe)
                vqall = SEs.n
                wait("tensor", SEs, vqall)
                for oc in range(KC):
                    bk = 2 + oc % 2
                    guard("tensor", f"ps{bk}")
                    emit("tensor", lambda e, oc=oc, bk=bk: e.matmul(
                        pb[bk][0:64, 0:512], kv_bfs[0:64, oc * HD:(oc + 1) * HD],
                        q_bf[0:64, oc * FT:(oc + 1) * FT],
                        start=True, stop=True), inc=True)
                    vy = emit("tensor", lambda e, oc=oc, bk=bk: e.matmul(
                        pb[bk][64:128, 0:512], kv_bfs[64:128, oc * HD:(oc + 1) * HD],
                        q_bf[64:128, oc * FT:(oc + 1) * FT],
                        start=True, stop=True, tile_position=(64, 64)), inc=True)
                    wait("scalar", PEs, vy)
                    vye = emit("scalar", lambda e, oc=oc, bk=bk: e.activation(
                        y_bf[:, oc * FT:(oc + 1) * FT], pb[bk][:, 0:512], AF.Identity), inc=True)
                    release(f"ps{bk}", SEs, vye)
                vyall = SEs.n

                guard("tensor", "ps4")
                for c in range(KC):
                    emit("tensor", lambda e, c=c, t=t: e.matmul(
                        pb[4][0:64, 0:512], wg1_bf[:, c * HD:(c + 1) * HD],
                        xh_sl(c, t * FT, FT),
                        start=(c == 0), stop=(c == KC - 1)), inc=(c == KC - 1))
                vgh = PEs.n
                wait("scalar", PEs, vgh)
                vghe = emit("scalar", lambda e: e.activation(
                    gh_bf[:, :], pb[4][0:64, 0:512], AF.Identity,
                    bias=bg1_v[:, 0:1]), inc=True)
                release("ps4", SEs, vghe)
                def w_z(c, t2buf):
                    return ("scalar", lambda e, c=c, u=t2buf: e.activation(
                        z_bf[:, c * FT:(c + 1) * FT], u[:], AF.Identity,
                        scale=lng_v[:, c:c + 1], bias=lnb_v[:, c:c + 1]))
                ln_tile([y_bf[:, c * FT:(c + 1) * FT] for c in range(KC)], w_z,
                        chunk_wait=lambda c, v=vyall: (SEs, v))

                wait("tensor", SEs, vghe)
                for oc in range(KC):
                    bk = 5 + oc % 2
                    guard("tensor", f"ps{bk}")
                    vg = emit("tensor", lambda e, oc=oc, bk=bk: e.matmul(
                        pb[bk][:, 0:512], wg2_bf[0:64, oc * 128:(oc + 1) * 128],
                        gh_bf[0:64, :], start=True, stop=True), inc=True)
                    wait("scalar", PEs, vg)
                    vge = emit("scalar", lambda e, oc=oc, bk=bk: e.activation(
                        gate_bf[:, oc * FT:(oc + 1) * FT], pb[bk][:, 0:512], AF.Sigmoid,
                        bias=bg2_v[:, oc:oc + 1]), inc=True)
                    release(f"ps{bk}", SEs, vge)
                    wait("vector", SEs, vge)
                    emit("vector", lambda e, oc=oc: e.tensor_tensor(
                        y_bf[:, oc * FT:(oc + 1) * FT], z_bf[:, oc * FT:(oc + 1) * FT],
                        gate_bf[:, oc * FT:(oc + 1) * FT], ALU.mult), inc=True)
                vgated = VEs.n
                wait("tensor", VEs, vgated)
                for oc in range(KC):
                    bk = oc % 2
                    xb = xf_a if oc % 2 == 0 else xf_b
                    guard("sync", f"xf{oc % 2}")
                    vxf = dma("sync", xb[:], xT[s * D + oc * 128:s * D + oc * 128 + 128,
                                                t * FT:(t + 1) * FT])
                    guard("tensor", f"ps{bk}")
                    for c in range(KC):
                        emit("tensor", lambda e, oc=oc, c=c, bk=bk: e.matmul(
                            pb[bk][:, 0:512],
                            wout_bf[:, c * D + oc * 128:c * D + oc * 128 + 128],
                            y_bf[:, c * FT:(c + 1) * FT],
                            start=(c == 0), stop=(c == KC - 1)), inc=(c == KC - 1))
                    vyo = PEs.n
                    wait("scalar", PEs, vyo)
                    guard("scalar", "e_f")
                    vef = emit("scalar", lambda e, s=s, oc=oc, bk=bk: e.activation(
                        e_f[:], pb[bk][:, 0:512], AF.Identity,
                        scale=gap_v[:, s * KC + oc:s * KC + oc + 1],
                        bias=gabo_v[:, s * KC + oc:s * KC + oc + 1]), inc=True)
                    release(f"ps{bk}", SEs, vef)
                    x2b = x2f_a if oc % 2 == 0 else x2f_b
                    guard("vector", f"x2f{oc % 2}")
                    wait("vector", SEs, vef)
                    wait("vector", LD, vxf)
                    vx2 = emit("vector", lambda e, xb=xb, x2b=x2b: e.tensor_tensor(
                        x2b[:], e_f[:], xb[:], ALU.add), inc=True)
                    release("e_f", VEs, vx2)
                    release(f"xf{oc % 2}", VEs, vx2)
                    wait("gpsimd", VEs, vx2)
                    vst = dma("gpsimd", x2T[s * D + oc * 128:s * D + oc * 128 + 128,
                                            t * FT:(t + 1) * FT], x2b[:])
                    release(f"x2f{oc % 2}", GD, vst)
            v_x2store = GD.n
            vqlast = PEs.n
            release("Wreg", PEs, vqlast)
            release("xh", PEs, vqlast)
            release("kv_bfs", PEs, vqlast)
            bar()

        qw_rowmax(wg_nat, rmxg, IC, D, [(natq, "natb"), (wtf_f, "natb2")])
        qw_rowmax(wu_nat, rmxu, IC, D, [(natq, "natb"), (wtf_f, "natb2")])
        qw_rowmax(wd_nat, rmxd, KC, DI, [(wtf_f, "natb2"), (cbc_f, "natbc")])
        emit("vector", lambda e: e.memset(wrk[:, WK_HMAX:WK_HMAX + 1], 0.0), inc=True)
        wait("gpsimd", GD, v_x2store)
        for ct in range(CT):
            s, t = ct // NT, ct % NT
            x2t = x2t_bf if ct % 2 == 0 else x2t_bf2
            x2v = []
            for c in range(KC):
                guard("gpsimd", f"x2t{ct % 2}")
                x2v.append(dma("gpsimd", x2t[:, c * FT:(c + 1) * FT],
                               x2T[s * D + c * 128:s * D + c * 128 + 128,
                                   t * FT:(t + 1) * FT]))

            def w_h2(c, t2buf, s=s, ct=ct):
                return ("scalar", lambda e, c=c, u=t2buf: e.activation(
                    h2_bf[:, c * B * S + ct * FT:c * B * S + (ct + 1) * FT], u[:], AF.Identity,
                    scale=cb1_v[:, s * KC + c:s * KC + c + 1],
                    bias=sb_v[:, s * KC + c:s * KC + c + 1]))
            lv = ln_tile([x2t[:, c * FT:(c + 1) * FT] for c in range(KC)], w_h2,
                         chunk_wait=lambda c, vs=tuple(x2v): (GD, vs[c]))
            release(f"x2t{ct % 2}", *lv)
        vh2 = SEs.n
        wait("vector", SEs, vh2)
        for c in range(KC):
            emit("vector", lambda e, c=c: e.reduce_max(
                wrk[:, WK_TMP:WK_TMP + 1], h2_bf[:, c * B * S:(c + 1) * B * S],
                AX.X, apply_absolute_value=True), inc=True)
            emit("vector", lambda e: e.drain())
            emit("vector", lambda e: e.tensor_tensor(
                wrk[:, WK_HMAX:WK_HMAX + 1], wrk[:, WK_HMAX:WK_HMAX + 1],
                wrk[:, WK_TMP:WK_TMP + 1], ALU.max), inc=True)
            emit("vector", lambda e: e.drain())
        bar()

        def collapse_and_cc(src_col, wr_dst, cci, cco):
            guard("tensor", "ps4")
            emit("tensor", lambda e, sc=src_col: e.transpose(
                pb[4][0:1, 0:128], wrk[:, sc:sc + 1], id_f[:]), inc=True)
            vt = PEs.n
            wait("vector", PEs, vt)
            vv = emit("vector", lambda e, d=wr_dst: e.reduce_max(
                wrow[0:1, d:d + 1], pb[4][0:1, 0:128], AX.X), inc=True)
            release("ps4", VEs, vv)
            emit("vector", lambda e: e.drain())
            vcp = emit("vector", lambda e, d=wr_dst: e.tensor_copy(
                ccs_sb[0:1, 0:1], wrow[0:1, d:d + 1]), inc=True)
            wait("gpsimd", VEs, vcp)
            vdm = dma("gpsimd", cci[0:1, :], ccs_sb[0:1, :])
            wait("gpsimd", GD, vdm)
            ccv = CCs.inc(1)
            E("gpsimd", lambda e, i=cci, o=cco: e.collective_compute(
                "AllReduce", ALU.max, replica_groups=[list(range(8))],
                ins=[i.ap()], outs=[o.ap()]).then_inc(CCs.h, 1))
            wait("gpsimd", CCs, ccv)
            return dma("gpsimd", ccr_sb[0:1, :], cco[0:1, :])

        vcc1 = collapse_and_cc(WK_HMAX, WR_HMAX, cc1i, cc1o)
        vsg = emit("scalar", lambda e: e.activation(
            wrow[0:1, WR_SIG:WR_SIG + 1], clip_v[0:1, 0:1], AF.Sigmoid), inc=True)
        wait("vector", GD, vcc1)
        wait("vector", SEs, vsg)
        emit("vector", lambda e: e.tensor_tensor(
            wrow[0:1, WR_SX:WR_SX + 1], wrow[0:1, WR_SIG:WR_SIG + 1],
            ccr_sb[0:1, 0:1], ALU.mult), inc=True)
        emit("vector", lambda e: e.drain())
        emit("vector", lambda e: e.tensor_scalar(
            wrow[0:1, WR_SX:WR_SX + 1], wrow[0:1, WR_SX:WR_SX + 1],
            1.0 / 127.0, None, ALU.mult), inc=True)
        emit("vector", lambda e: e.drain())
        vrsx = emit("vector", lambda e: e.reciprocal(
            wrow[0:1, WR_RSX:WR_RSX + 1], wrow[0:1, WR_SX:WR_SX + 1]), inc=True)
        emit("vector", lambda e: e.drain())
        guard("tensor", "ps4")
        wait("tensor", VEs, vrsx)
        emit("tensor", lambda e: e.matmul(
            pb[4][:, 0:2], onesrow_f[0:1, :], wrow[0:1, WR_SX:WR_SX + 2],
            start=True, stop=True), inc=True)
        vbx = PEs.n
        wait("vector", PEs, vbx)
        vsx2 = emit("vector", lambda e: e.tensor_copy(
            wrk[:, WK_SXB:WK_SXB + 2], pb[4][:, 0:2]), inc=True)
        release("ps4", VEs, vsx2)
        wait("scalar", VEs, vsx2)
        for c in range(KC):
            emit("scalar", lambda e, c=c: e.activation(
                xq_i8[:, c * B * S:(c + 1) * B * S], h2_bf[:, c * B * S:(c + 1) * B * S],
                AF.Identity, scale=wrk[:, WK_RSX:WK_RSX + 1]), inc=True)
        bar()

        for nm, tv in (("h2", h2_bf), ("xq", xq_i8), ("wrow1", wrow), ("wrk1", wrk)):
            dbg(nm, tv)
        bar()
        def quant_weight(wT_d, rmx, wq_sb, n_out, n_in_chunks, cq, cqT_sb,
                         crow_sb, cbc_sb, wtf, w8):
            oc_n = n_out // 128
            emit("vector", lambda e: e.drain())
            emit("vector", lambda e, rm=rmx, c=cq: e.reciprocal(c[:, 0:oc_n], rm[:, 0:oc_n]), inc=True)
            emit("vector", lambda e: e.drain())
            vcq = emit("vector", lambda e, c=cq: e.tensor_scalar(
                c[:, 0:oc_n], c[:, 0:oc_n], 127.0, None, ALU.mult), inc=True)
            emit("vector", lambda e: e.drain())
            guard("tensor", "ps4")
            wait("tensor", VEs, vcq)
            emit("tensor", lambda e, c=cq, o=oc_n: e.transpose(
                pb[4][0:o, 0:128], c[:, 0:o], id_f[:]), inc=True)
            vtp = PEs.n
            wait("vector", PEs, vtp)
            vct = emit("vector", lambda e, ct_=cqT_sb, o=oc_n: e.tensor_copy(
                ct_[0:o, :], pb[4][0:o, 0:128]), inc=True)
            release("ps4", VEs, vct)
            wait("sync", VEs, vct)
            guard("sync", "wtf")
            vcr = dma("sync", crow_sb[0:1, 0:n_out], cqT_sb[0:oc_n, :])
            wait("tensor", LD, vcr)
            nbk = (n_out + 511) // 512
            for j in range(nbk):
                w_ = min(512, n_out - j * 512)
                bk = 4 + (j % 2)
                guard("tensor", f"ps{bk}")
                vbc = emit("tensor", lambda e, j=j, bk=bk, cr=crow_sb, w_=w_: e.matmul(
                    pb[bk][:, 0:w_], onesrow_f[0:1, :], cr[0:1, j * 512:j * 512 + w_],
                    start=True, stop=True), inc=True)
                wait("vector", PEs, vbc)
                vcb2 = emit("vector", lambda e, j=j, bk=bk, cb=cbc_sb, w_=w_: e.tensor_copy(
                    cb[:, j * 512:j * 512 + w_], pb[bk][:, 0:w_]), inc=True)
                release(f"ps{bk}", VEs, vcb2)
            vcbc = VEs.n
            release("wtf", PEs, PEs.n)
            wait("vector", VEs, vcbc)
            for kk in range(n_in_chunks):
                guard("sync", "wtf")
                vld = dma("sync", wtf[:, 0:n_out], wT_d[kk * 128:(kk + 1) * 128, :])
                wait("vector", LD, vld)
                guard("vector", "w8")
                v8 = emit("vector", lambda e, w=wtf, cb=cbc_sb, o=n_out, x8=w8: e.scalar_tensor_tensor(
                    x8[:, 0:o], w[:, 0:o], 1.0, cb[:, 0:o], ALU.mult, ALU.mult), inc=True)
                release("wtf", VEs, v8)
                if kk % 2 == 0:
                    wait("scalar", VEs, v8)
                    vq = emit("scalar", lambda e, kk=kk, x8=w8, o=n_out, wq=wq_sb: e.activation(
                        wq[:, kk * o:(kk + 1) * o], x8[:, 0:o], AF.Identity), inc=True)
                    release("w8", SEs, vq)
                else:
                    vq = emit("vector", lambda e, kk=kk, x8=w8, o=n_out, wq=wq_sb: e.tensor_copy(
                        wq[:, kk * o:(kk + 1) * o], x8[:, 0:o]), inc=True)
                    release("w8", VEs, vq)
            wait("scalar", VEs, VEs.n)
            return SEs.n

        v_gq = quant_weight(wgTd, rmxg, wgq_bf, DI, KC, cq22, cqT22,
                            crow, cbc_f, wtf_f, wq8b)
        v_uq = quant_weight(wuTd, rmxu, wuq_bf, DI, KC, cq22, cqT22,
                            crow, cbc_f, wtf_f, wq8b)
        wait("vector", VEs, vsx2)
        emit("vector", lambda e: e.drain())
        emit("vector", lambda e: e.tensor_scalar(
            sgc_g[:], rmxg[:], wrk[:, WK_SXB:WK_SXB + 1], 1.0 / 127.0,
            ALU.mult, ALU.mult), inc=True)
        emit("vector", lambda e: e.tensor_scalar(
            susx[:], rmxu[:], wrk[:, WK_SXB:WK_SXB + 1], 1.0 / 127.0,
            ALU.mult, ALU.mult), inc=True)
        emit("vector", lambda e: e.memset(tmaxr[:], 0.0), inc=True)
        bar()

        wait("tensor", SEs, v_uq)
        v_spill_all = None
        for ct in range(CT):
            guard("vector", "xqb")
            for c in range(KC):
                emit("vector", lambda e, c=c, ct=ct: e.tensor_copy(
                    xqb[:, c * FT:(c + 1) * FT],
                    xq_i8[:, c * B * S + ct * FT:c * B * S + (ct + 1) * FT]),
                    inc=(c == KC - 1))
            vxq = VEs.n
            wait("tensor", VEs, vxq)
            for ic in range(IC):
                gbk = ic % 2
                ubk = 2 + ic % 2
                guard("tensor", f"ps{gbk}")
                for c in range(KC):
                    emit("tensor", lambda e, ic=ic, c=c, bk=gbk: e.matmul(
                        pb[bk][:, 0:512],
                        wgq_bf[:, c * DI + ic * 128:c * DI + ic * 128 + 128],
                        xqb[:, c * FT:(c + 1) * FT],
                        start=(c == 0), stop=(c == KC - 1)), inc=(c == KC - 1))
                vgmm = PEs.n
                guard("tensor", f"ps{ubk}")
                for c in range(KC):
                    emit("tensor", lambda e, ic=ic, c=c, bk=ubk: e.matmul(
                        pb[bk][:, 0:512],
                        wuq_bf[:, c * DI + ic * 128:c * DI + ic * 128 + 128],
                        xqb[:, c * FT:(c + 1) * FT],
                        start=(c == 0), stop=(c == KC - 1)), inc=(c == KC - 1))
                vumm = PEs.n
                gs = gs_a if ic % 2 == 0 else gs_b
                up = up_a if ic % 2 == 0 else up_b
                tb = tb_a if ic % 2 == 0 else tb_b
                guard("scalar", f"gs{ic % 2}")
                wait("scalar", PEs, vgmm)
                vgse = emit("scalar", lambda e, ic=ic, bk=gbk, g=gs: e.activation(
                    g[:], pb[bk][:, 0:512], AF.Silu,
                    scale=sgc_g[:, ic:ic + 1]), inc=True)
                release(f"ps{gbk}", SEs, vgse)
                guard("scalar", f"up{ic % 2}")
                wait("scalar", PEs, vumm)
                vupe = emit("scalar", lambda e, bk=ubk, u=up: e.activation(
                    u[:], pb[bk][:, 0:512], AF.Identity), inc=True)
                release(f"ps{ubk}", SEs, vupe)
                guard("vector", f"tb{ic % 2}")
                guard("vector", f"tbS{ic % 2}")
                wait("vector", SEs, vupe)
                vtb = emit("vector", lambda e, g=gs, u=up, t=tb: e.tensor_tensor(
                    t[:], g[:], u[:], ALU.mult), inc=True)
                release(f"gs{ic % 2}", VEs, vtb)
                release(f"up{ic % 2}", VEs, vtb)
                wait("sync", VEs, vtb)
                vsp = dma("sync", tsp[ic * 128:(ic + 1) * 128, ct * FT:(ct + 1) * FT],
                          tb[:], sem=STm)
                release(f"tbS{ic % 2}", STm, vsp)
                vgm = emit("vector", lambda e, t=tb, ic=ic, ct=ct: e.reduce_max(
                    tmall[:, ic * CT + ct:ic * CT + ct + 1], t[:], AX.X,
                    apply_absolute_value=True), inc=True)
                release(f"tb{ic % 2}", VEs, vgm)
            release("xqb", PEs, PEs.n)
        v_spill_all = STm.n
        bar()
        for nm, tv in (("wgq", wgq_bf), ("wuq", wuq_bf), ("xqb", xqb), ("gsb", gs_b),
                       ("upb", up_b), ("tbb", tb_b), ("cbc", cbc_f), ("tmall", tmall),
                       ("cq22", cq22), ("cqT22", cqT22)):
            dbg(nm, tv)
        bar()

        emit("vector", lambda e: e.drain())
        for ic in range(IC):
            emit("vector", lambda e, ic=ic: e.reduce_max(
                tmaxr[:, ic:ic + 1], tmall[:, ic * CT:(ic + 1) * CT], AX.X), inc=True)
        emit("vector", lambda e: e.drain())
        emit("vector", lambda e: e.tensor_tensor(tmaxr[:], tmaxr[:], susx[:], ALU.mult), inc=True)
        emit("vector", lambda e: e.drain())
        vtm = emit("vector", lambda e: e.reduce_max(
            wrk[:, WK_TMX:WK_TMX + 1], tmaxr[:], AX.X), inc=True)
        emit("vector", lambda e: e.drain())
        wait("tensor", VEs, vtm)
        vcc2 = collapse_and_cc(WK_TMX, WR_TMX, cc2i, cc2o)
        vsgd = emit("scalar", lambda e: e.activation(
            wrow[0:1, WR_SIGD:WR_SIGD + 1], clip_v[0:1, 1:2], AF.Sigmoid), inc=True)
        wait("vector", GD, vcc2)
        wait("vector", SEs, vsgd)
        emit("vector", lambda e: e.tensor_tensor(
            wrow[0:1, WR_SD:WR_SD + 1], wrow[0:1, WR_SIGD:WR_SIGD + 1],
            ccr_sb[0:1, 0:1], ALU.mult), inc=True)
        emit("vector", lambda e: e.drain())
        emit("vector", lambda e: e.tensor_scalar(
            wrow[0:1, WR_SD:WR_SD + 1], wrow[0:1, WR_SD:WR_SD + 1],
            1.0 / 127.0, None, ALU.mult), inc=True)
        emit("vector", lambda e: e.drain())
        vrsd = emit("vector", lambda e: e.reciprocal(
            wrow[0:1, WR_RSD:WR_RSD + 1], wrow[0:1, WR_SD:WR_SD + 1]), inc=True)
        emit("vector", lambda e: e.drain())
        guard("tensor", "ps4")
        wait("tensor", VEs, vrsd)
        emit("tensor", lambda e: e.matmul(
            pb[4][:, 0:2], onesrow_f[0:1, :], wrow[0:1, WR_SD:WR_SD + 2],
            start=True, stop=True), inc=True)
        vbd = PEs.n
        wait("vector", PEs, vbd)
        vsd2 = emit("vector", lambda e: e.tensor_copy(
            wrk[:, WK_SDB:WK_SDB + 2], pb[4][:, 0:2]), inc=True)
        release("ps4", VEs, vsd2)
        wait("vector", VEs, vsd2)
        emit("vector", lambda e: e.drain())
        emit("vector", lambda e: e.tensor_scalar(
            qsc[:], susx[:], wrk[:, WK_RSD:WK_RSD + 1], None, ALU.mult), inc=True)
        emit("vector", lambda e: e.drain())
        bar()

        emit("vector", lambda e: e.drain())
        emit("vector", lambda e: e.reciprocal(cq8[:], rmxd[:]), inc=True)
        emit("vector", lambda e: e.drain())
        vcq8 = emit("vector", lambda e: e.tensor_scalar(
            cq8[:], cq8[:], 127.0, None, ALU.mult), inc=True)
        emit("vector", lambda e: e.drain())
        guard("tensor", "ps4")
        wait("tensor", VEs, vcq8)
        emit("tensor", lambda e: e.transpose(pb[4][0:KC, 0:128], cq8[:], id_f[:]), inc=True)
        vtp = PEs.n
        wait("vector", PEs, vtp)
        vct = emit("vector", lambda e: e.tensor_copy(cqT8[:, :], pb[4][0:KC, 0:128]), inc=True)
        release("ps4", VEs, vct)
        wait("sync", VEs, vct)
        vcr = dma("sync", crowd[0:1, :], cqT8[:, :])
        wait("tensor", LD, vcr)
        for j in range(2):
            bk = 4 + j
            guard("tensor", f"ps{bk}")
            vbc = emit("tensor", lambda e, j=j, bk=bk: e.matmul(
                pb[bk][:, 0:512], onesrow_f[0:1, :], crowd[0:1, j * 512:(j + 1) * 512],
                start=True, stop=True), inc=True)
            wait("vector", PEs, vbc)
            vcb2 = emit("vector", lambda e, j=j, bk=bk: e.tensor_copy(
                cbcd_f[:, j * 512:(j + 1) * 512], pb[bk][:, 0:512]), inc=True)
            release(f"ps{bk}", VEs, vcb2)
        vcbc = VEs.n
        wait("vector", VEs, vcbc)
        for kk in range(IC):
            guard("sync", "wtd")
            vld = dma("sync", wtd_f[:, 0:D], wdTd[kk * 128:(kk + 1) * 128, :])
            wait("vector", LD, vld)
            guard("vector", "wd8")
            v8 = emit("vector", lambda e: e.scalar_tensor_tensor(
                wd8b[:], wtd_f[:, 0:D], 1.0, cbcd_f[:], ALU.mult, ALU.mult), inc=True)
            release("wtd", VEs, v8)
            wait("scalar", VEs, v8)
            vq = emit("scalar", lambda e, kk=kk: e.activation(
                wdq_bf[:, kk * D:(kk + 1) * D], wd8b[:], AF.Identity), inc=True)
            release("wd8", SEs, vq)
        v_dq = SEs.n
        emit("vector", lambda e: e.drain())
        emit("vector", lambda e: e.tensor_scalar(
            sc8[:], rmxd[:], wrk[:, WK_SDB:WK_SDB + 1], 1.0 / 127.0,
            ALU.mult, ALU.mult), inc=True)
        emit("vector", lambda e: e.drain())
        for b_ in range(2):
            emit("vector", lambda e, b_=b_: e.tensor_tensor(
                comb[:, b_ * KC:(b_ + 1) * KC], gbp_v[:, b_ * KC:(b_ + 1) * KC],
                sc8[:], ALU.mult), inc=True)
        emit("vector", lambda e: e.drain())
        bar()

        for nm, tv in (("rmxg", rmxg), ("rmxu", rmxu), ("rmxd", rmxd), ("sgc", sgc_g),
                       ("susx", susx), ("qsc", qsc), ("comb", comb), ("tmaxr", tmaxr),
                       ("wrow2", wrow)):
            dbg(nm, tv)
        bar()
        wait("sync", STm, v_spill_all)
        wait("tensor", SEs, v_dq)
        for ct in range(CT):
            s, t = ct // NT, ct % NT
            tqb = tqb_a if ct % 2 == 0 else tqb_b
            guard("scalar", f"tqb{ct % 2}")
            for ic in range(IC):
                trd = trd_a if ic % 2 == 0 else trd_b
                tq8 = tq8_a if ic % 2 == 0 else tq8_b
                guard("sync", f"trd{ic % 2}")
                vld = dma("sync", trd[:], tsp[ic * 128:(ic + 1) * 128,
                                             ct * FT:(ct + 1) * FT])
                guard("scalar", f"tq8{ic % 2}")
                wait("scalar", LD, vld)
                v8 = emit("scalar", lambda e, ic=ic, tr=trd, t8=tq8: e.activation(
                    t8[:], tr[:], AF.Identity, scale=qsc[:, ic:ic + 1]), inc=True)
                release(f"trd{ic % 2}", SEs, v8)
                vq = emit("scalar", lambda e, ic=ic, t8=tq8, tq=tqb: e.activation(
                    tq[:, ic * FT:(ic + 1) * FT], t8[:], AF.Identity), inc=True)
                release(f"tq8{ic % 2}", SEs, vq)
            vtq = SEs.n
            wait("tensor", SEs, vtq)
            for oc in range(KC):
                bk = oc % 4
                xb = xfd_a if oc % 2 == 0 else xfd_b
                guard("sync", f"xfd{oc % 2}")
                vxf = dma("sync", xb[:], x2T[s * D + oc * 128:s * D + oc * 128 + 128,
                                             t * FT:(t + 1) * FT])
                guard("tensor", f"ps{bk}")
                for ic in range(IC):
                    emit("tensor", lambda e, oc=oc, ic=ic, bk=bk, tq=tqb: e.matmul(
                        pb[bk][:, 0:512],
                        wdq_bf[:, ic * D + oc * 128:ic * D + oc * 128 + 128],
                        tq[:, ic * FT:(ic + 1) * FT],
                        start=(ic == 0), stop=(ic == IC - 1)), inc=(ic == IC - 1))
                vdm = PEs.n
                of = of_a if oc % 2 == 0 else of_b
                guard("vector", f"of{oc % 2}")
                wait("vector", PEs, vdm)
                wait("vector", LD, vxf)
                vof = emit("vector", lambda e, s=s, oc=oc, bk=bk, xb=xb, of=of: e.scalar_tensor_tensor(
                    of[:], pb[bk][:, 0:512], comb[:, s * KC + oc:s * KC + oc + 1],
                    xb[:], ALU.mult, ALU.add), inc=True)
                release(f"ps{bk}", VEs, vof)
                release(f"xfd{oc % 2}", VEs, vof)
                wait("gpsimd", VEs, vof)
                vst = dma("gpsimd", out_d[s * D + oc * 128:s * D + oc * 128 + 128,
                                          t * FT:(t + 1) * FT], of[:], sem=STm)
                release(f"of{oc % 2}", STm, vst)
            release(f"tqb{ct % 2}", PEs, PEs.n)
        for nm, tv in (("wdq", wdq_bf), ("tqa", tqb_a), ("tqb2", tqb_b), ("cbcd", cbcd_f), ("cq8", cq8)):
            dbg(nm, tv)
        wait("sync", STm, STm.n)
        bar()


    try:
        _sched()
    except _StopSched:
        pass

    with nc.Block() as block:
        @block.sync
        def _(e):
            for f in prog["sync"]:
                f(e)

        @block.gpsimd
        def _(e):
            for f in prog["gpsimd"]:
                f(e)

        @block.vector
        def _(e):
            for f in prog["vector"]:
                f(e)

        @block.scalar
        def _(e):
            for f in prog["scalar"]:
                f(e)

        @block.tensor
        def _(e):
            for f in prog["tensor"]:
                f(e)

    es.close()
    return nc


_NC_CACHE = []


def kernel(L0_attn_x, L0_attn_w_qkv, L0_attn_b_qkv, L0_attn_w_g1, L0_attn_b_g1,
           L0_attn_w_g2, L0_attn_b_g2, L0_attn_w_out, L0_attn_b_out,
           L0_attn_ln_g, L0_attn_ln_b, L0_ffn_clip_x, L0_ffn_clip_down,
           L0_ffn_w_gate, L0_ffn_w_up, L0_ffn_w_down, L0_c, L0_wmod):
    f = np.float32
    cc = np.ascontiguousarray

    def pp(v):
        return cc(np.asarray(v, f).reshape(-1, 128).T)

    shared = {
        "wqkvT": cc(np.asarray(L0_attn_w_qkv, f).T),
        "woutT": cc(np.asarray(L0_attn_w_out, f).T),
        "wg1T": cc(np.asarray(L0_attn_w_g1, f).T),
        "wg2T": cc(np.asarray(L0_attn_w_g2, f).T),
        "wgT": cc(np.asarray(L0_ffn_w_gate, f).T),
        "wuT": cc(np.asarray(L0_ffn_w_up, f).T),
        "wdT": cc(np.asarray(L0_ffn_w_down, f).T),
        "wg_nat": cc(np.asarray(L0_ffn_w_gate, f)),
        "wu_nat": cc(np.asarray(L0_ffn_w_up, f)),
        "wd_nat": cc(np.asarray(L0_ffn_w_down, f)),
        "wmod": cc(np.asarray(L0_wmod, f)),
        "bq": pp(np.asarray(L0_attn_b_qkv, f)[0:D]),
        "bv_row": cc(np.asarray(L0_attn_b_qkv, f)[2 * D:3 * D].reshape(1, D)),
        "bg1": cc(np.asarray(L0_attn_b_g1, f).reshape(HD, 1)),
        "bg2": pp(L0_attn_b_g2),
        "bout": pp(L0_attn_b_out),
        "lng": pp(L0_attn_ln_g),
        "lnb": pp(L0_attn_ln_b),
        "clip2": cc(np.stack([np.asarray(L0_ffn_clip_x, f).reshape(()),
                              np.asarray(L0_ffn_clip_down, f).reshape(())]).reshape(1, 2)),
    }
    x = np.asarray(L0_attn_x, f)
    c_all = np.asarray(L0_c, f)
    in_maps = []
    for core in range(8):
        xs = cc(x[2 * core:2 * core + 2].transpose(0, 2, 1).reshape(B * D, S))
        cpair = c_all[2 * core:2 * core + 2]
        cT2 = cc(cpair.reshape(2, KC, 128).transpose(2, 1, 0).reshape(128, 2 * KC))
        m = dict(shared)
        m["xT"] = xs
        m["cT"] = cT2
        in_maps.append(m)

    if not _NC_CACHE:
        _NC_CACHE.append(build_nc())
    nc = _NC_CACHE[0]
    res = run_bass_kernel_spmd(nc, in_maps, core_ids=list(range(8)))
    outs = []
    for core in range(8):
        o = np.asarray(res.results[core]["out"])
        outs.append(o.reshape(B, D, S).transpose(0, 2, 1))
    return np.concatenate(outs, axis=0).astype(np.float32)


# version 18 (final)
# speedup vs baseline: 1.0070x; 1.0070x over previous
import numpy as np
import concourse.bass as bass
import concourse.mybir as mybir
from concourse.bass_utils import run_bass_kernel_spmd

F32 = mybir.dt.float32
BF16 = mybir.dt.bfloat16
I8 = mybir.dt.int8
AF = mybir.ActivationFunctionType
ALU = mybir.AluOpType
AX = mybir.AxisListType

B = 2
S = 2048
D = 1024
KC = 8
NH = 16
HD = 64
DI = 2816
IC = 22
FT = 512
NT = S // FT
KT = S // 128
CT = B * NT

ENGINES = ("sync", "gpsimd", "vector", "scalar", "tensor")


class Sem:
    def __init__(self, h):
        self.h = h
        self.n = 0

    def inc(self, k=1):
        self.n += k
        return self.n


def build_nc():
    nc = bass.Bass("TRN2", target_bir_lowering=False)

    DT = lambda n, sh, dt=F32, kind="ExternalInput": nc.dram_tensor(n, sh, dt, kind=kind)
    xT = DT("xT", [B * D, S])
    wqkvT = DT("wqkvT", [D, 3 * D])
    woutT = DT("woutT", [D, D])
    wg1T = DT("wg1T", [D, HD])
    wg2T = DT("wg2T", [HD, D])
    wgTd = DT("wgT", [D, DI])
    wuTd = DT("wuT", [D, DI])
    wdTd = DT("wdT", [DI, D])
    wg_nat = DT("wg_nat", [DI, D])
    wu_nat = DT("wu_nat", [DI, D])
    wd_nat = DT("wd_nat", [D, DI])
    wmodd = DT("wmod", [D, 6 * D])
    cTd = DT("cT", [128, 2 * KC])
    bq_d = DT("bq", [128, KC])
    bv_row_d = DT("bv_row", [1, D])
    bg1_d = DT("bg1", [HD, 1])
    bg2_d = DT("bg2", [128, KC])
    bout_d = DT("bout", [128, KC])
    lng_d = DT("lng", [128, KC])
    lnb_d = DT("lnb", [128, KC])
    clip_d = DT("clip2", [1, 2])
    import os as _os0
    KDBG = bool(int(_os0.environ.get("KDBG", "0")))
    _K1 = "ExternalOutput" if KDBG else "Internal"
    out_d = DT("out", [B * D, S], F32, "ExternalOutput")
    x2T = DT("x2T", [B * D, S], F32, _K1)
    tsp = DT("tsp", [DI, B * S], BF16, _K1)
    cc1i = DT("cc1i", [1, 16], F32, "Internal")
    cc1o = DT("cc1o", [1, 16], F32, "Internal")
    cc2i = DT("cc2i", [1, 16], F32, "Internal")
    cc2o = DT("cc2o", [1, 16], F32, "Internal")

    id_dram = nc.inline_tensor(np.eye(128, dtype=np.float32), "ident")
    onescol_dram = nc.inline_tensor(np.ones((128, 1), np.float32), "onescol")
    onesrow_dram = nc.inline_tensor(np.ones((1, 128), np.float32), "onesrow")

    A = lambda name, shape, dt=F32: nc.alloc_sbuf_tensor(name, shape, dt)
    id_f = A("id_f", [128, 128])
    onescol_bf = A("onescol_bf", [128, 1], BF16)
    onesrow_f = A("onesrow_f", [1, 128])
    onesrow_bf = A("onesrow_bf", [1, 128], BF16)
    mod6 = A("mod6", [128, 6 * 2 * KC])
    sa_v = A("sa_v", [128, 2 * KC])
    ca1_v = A("ca1_v", [128, 2 * KC])
    gap_v = A("gap_v", [128, 2 * KC])
    sb_v = A("sb_v", [128, 2 * KC])
    cb1_v = A("cb1_v", [128, 2 * KC])
    gbp_v = A("gbp_v", [128, 2 * KC])
    gabo_v = A("gabo_v", [128, 2 * KC])
    sc16a = A("sc16a", [128, 2 * KC])
    sc16b = A("sc16b", [128, 2 * KC])
    bq_v = A("bq_v", [128, KC])
    bg2_v = A("bg2_v", [128, KC])
    bout_v = A("bout_v", [128, KC])
    lng_v = A("lng_v", [128, KC])
    lnb_v = A("lnb_v", [128, KC])
    bg1_v = A("bg1_v", [HD, 1])
    c_sb = A("c_sb", [128, 2 * KC])
    c_bf = A("c_bf", [128, 2 * KC], BF16)
    clip_v = A("clip_v", [1, 2])
    bvbc = A("bvbc", [128, D])
    wrk = A("wrk", [128, 16])
    wrow = A("wrow", [1, 32])
    den_v = A("den_v", [128, 2])
    rmxg = A("rmxg", [128, IC])
    rmxu = A("rmxu", [128, IC])
    rmxd = A("rmxd", [128, KC])
    tmall = A("tmall", [128, IC * CT])
    sgc_g = A("sgc_g", [128, IC])
    susx = A("susx", [128, IC])
    qsc = A("qsc", [128, IC])
    comb = A("comb", [128, 2 * KC])
    tmaxr = A("tmaxr", [128, IC])
    cq22 = A("cq22", [128, IC])
    cq8 = A("cq8", [128, KC])
    sc8 = A("sc8", [128, KC])
    ccs_sb = A("ccs_sb", [1, 16])
    cv = A("cv", [1, 4])
    ccr_sb = A("ccr_sb", [1, 16])

    WK_MRUN, WK_NEGM, WK_HMAX, WK_TMP, WK_SXB, WK_RSX, WK_SDB, WK_RSD, WK_TMX = range(9)
    WR_M, WR_NEGM, WR_HMAX, WR_SIG, WR_SX, WR_RSX, WR_TMX, WR_SIGD, WR_SD, WR_RSD = range(10)

    X0 = (nc.sbuf_base + 31) & ~31
    TOP = nc.sbuf_top

    def AT(name, shape, dt, off):
        assert off % 32 == 0, name
        nbytes = int(np.prod(shape[1:])) * mybir.dt.size(dt)
        assert X0 <= off and off + nbytes <= TOP, (name, off, nbytes, X0, TOP)
        return nc.alloc_sbuf_tensor_at(name, shape, dt, offset=off)

    K = 1024
    Z = X0
    t1a = AT("t1a", [128, FT], BF16, Z + 0 * K)
    t1b = AT("t1b", [128, FT], BF16, Z + 1 * K)
    t2a = AT("t2a", [128, FT], BF16, Z + 2 * K)
    t2b = AT("t2b", [128, FT], BF16, Z + 3 * K)
    sqa = AT("sqa", [128, FT], BF16, Z + 4 * K)
    sqb = AT("sqb", [128, FT], BF16, Z + 5 * K)
    rb_a = AT("rb_a", [128, FT], BF16, Z + 6 * K)
    mb_a = AT("mb_a", [128, FT], BF16, Z + 7 * K)
    eka = AT("eka", [128, D], BF16, Z + 8 * K)
    ekb = AT("ekb", [128, D], BF16, Z + 10 * K)
    muT = AT("muT", [1, FT], F32, Z + 12 * K)
    e2T = AT("e2T", [1, FT], F32, Z + 14 * K)
    tmpT = AT("tmpT", [1, FT], F32, Z + 16 * K)
    varT = AT("varT", [1, FT], F32, Z + 18 * K)
    rT = AT("rT", [1, FT], F32, Z + 20 * K)
    mrT = AT("mrT", [1, FT], F32, Z + 22 * K)
    rT_bf = AT("rT_bf", [1, FT], BF16, Z + 24 * K)
    mrT_bf = AT("mrT_bf", [1, FT], BF16, Z + 25 * K)
    bvrow_sb = AT("bvrow_sb", [1, D], F32, Z + 26 * K)

    Z1 = Z + 31 * K

    wkv_bf = AT("wkv_bf", [128, KC * 2 * D], BF16, Z1)
    wq_bf = AT("wq_bf", [128, KC * D], BF16, Z1)
    wout_bf = AT("wout_bf", [128, KC * D], BF16, Z1 + 32 * K)
    wg1_bf = AT("wg1_bf", [128, KC * HD], BF16, Z1 + 48 * K)
    wg2_bf = AT("wg2_bf", [HD, D], BF16, Z1 + 49 * K)
    xh_bf = AT("xh_bf", [128, KC * S], BF16, Z1 + 52 * K)
    R1 = Z1 + 84 * K
    k_bf = AT("k_bf", [128, KT * D], BF16, R1)
    q_bf = AT("q_bf", [128, KC * FT], BF16, R1)
    y_bf = AT("y_bf", [128, KC * FT], BF16, R1 + 8 * K)
    z_bf = AT("z_bf", [128, KC * FT], BF16, R1 + 16 * K)
    gate_bf = AT("gate_bf", [128, KC * FT], BF16, R1 + 24 * K)
    R2 = Z1 + 116 * K
    v_sb = AT("v_sb", [128, KT * NH * 65], BF16, R2)
    e_f = AT("e_f", [128, FT], F32, R2)
    xf_a = AT("xf_a", [128, FT], F32, R2 + 2 * K)
    xf_b = AT("xf_b", [128, FT], F32, R2 + 4 * K)
    x2f_a = AT("x2f_a", [128, FT], F32, R2 + 6 * K)
    x2f_b = AT("x2f_b", [128, FT], F32, R2 + 8 * K)
    gh_bf = AT("gh_bf", [HD, FT], BF16, R2 + 10 * K)
    kvtmp = AT("kvtmp", [128, HD], F32, R2 + 12 * K)
    kv_bfs = AT("kv_bfs", [128, 8 * HD], BF16, R2 + 13 * K)
    assert R2 + 34 * K <= TOP

    wmod_bf = AT("wmod_bf", [128, KC * 6 * D], BF16, Z1)
    wm32 = AT("wm32", [128, KC * 128], F32, Z1 + 97 * K)
    wm32b = AT("wm32b", [128, KC * 1024], F32, Z1)

    x2t_bf = AT("x2t_bf", [128, KC * FT], BF16, Z1)
    x2t_bf2 = AT("x2t_bf2", [128, KC * FT], BF16, Z1 + 8 * K)
    xq_i8 = AT("xq_i8", [128, KC * B * S], I8, Z1)
    h2_bf = AT("h2_bf", [128, KC * B * S], BF16, Z1 + 33 * K)

    wgq_bf = AT("wgq_bf", [128, KC * DI], BF16, Z1 + 33 * K)
    wuq_bf = AT("wuq_bf", [128, KC * DI], BF16, Z1 + 77 * K)
    cbc_f = AT("cbc_f", [128, DI], F32, Z1 + 121 * K)
    wtf_f = AT("wtf_f", [128, DI], F32, Z1 + 132 * K)
    crow = AT("crow", [1, DI], F32, Z1 + 132 * K)
    cqT22 = AT("cqT22", [IC, 128], F32, Z1 + 143 * K)
    wq8b = AT("wq8b", [128, DI], I8, Z1 + 144 * K)
    natq = AT("natq", [128, D], F32, Z1 + 147 * K)
    xqb = AT("xqb", [128, KC * FT], BF16, Z1 + 151 * K)
    gs_a = AT("gs_a", [128, FT], BF16, Z1 + 159 * K)
    gs_b = AT("gs_b", [128, FT], BF16, Z1 + 160 * K)
    up_a = AT("up_a", [128, FT], BF16, Z1 + 161 * K)
    up_b = AT("up_b", [128, FT], BF16, Z1 + 162 * K)
    tb_a = AT("tb_a", [128, FT], BF16, Z1 + 163 * K)
    tb_b = AT("tb_b", [128, FT], BF16, Z1 + 164 * K)
    assert Z1 + 165 * K <= TOP, (Z1 + 165 * K, TOP)

    wdq_bf = AT("wdq_bf", [128, IC * D], BF16, Z1)
    tqb_a = AT("tqb_a", [128, IC * FT], BF16, Z1 + 44 * K)
    tqb_b = AT("tqb_b", [128, IC * FT], BF16, Z1 + 66 * K)
    cbcd_f = AT("cbcd_f", [128, D], F32, Z1 + 88 * K)
    wtd_f = AT("wtd_f", [128, DI], F32, Z1 + 92 * K)
    natd = AT("natd", [128, DI], F32, Z1 + 103 * K)
    crowd = AT("crowd", [1, D], F32, Z1 + 114 * K)
    cqT8 = AT("cqT8", [KC, 128], F32, Z1 + 118 * K)
    wd8b = AT("wd8b", [128, D], I8, Z1 + 119 * K)
    tq8_a = AT("tq8_a", [128, FT], I8, Z1 + 120 * K)
    tq8_b = AT("tq8_b", [128, FT], I8, Z1 + 121 * K)
    of_a = AT("of_a", [128, FT], F32, Z1 + 122 * K)
    of_b = AT("of_b", [128, FT], F32, Z1 + 124 * K)
    trd_a = AT("trd_a", [128, FT], BF16, Z1 + 126 * K)
    trd_b = AT("trd_b", [128, FT], BF16, Z1 + 127 * K)
    dtmp_a = AT("dtmp_a", [128, FT], F32, Z1 + 128 * K)
    dtmp_b = AT("dtmp_b", [128, FT], F32, Z1 + 130 * K)
    xfd_a = AT("xfd_a", [128, FT], F32, Z1 + 132 * K)
    xfd_b = AT("xfd_b", [128, FT], F32, Z1 + 134 * K)

    pb = [nc.alloc_psum_tensor(f"pb{i}", [128, 512], F32) for i in range(8)]

    import contextlib
    es = contextlib.ExitStack()
    SEM = lambda n: Sem(es.enter_context(nc.semaphore(n)))
    LD, GD, STm = SEM("LD"), SEM("GD"), SEM("STm")
    PEs, VEs, SEs, GEs = SEM("PEs"), SEM("VEs"), SEM("SEs"), SEM("GEs")
    CCs, PHB = SEM("CCs"), SEM("PHB")
    ESEM = {"vector": VEs, "scalar": SEs, "tensor": PEs, "gpsimd": GEs,
            "sync": None}
    DSEM = {"sync": LD, "gpsimd": GD}

    prog = {en: [] for en in ENGINES}

    def E(en, f):
        prog[en].append(f)

    def emit(en, f, inc=False):
        if not inc:
            E(en, f)
            return None
        s = ESEM[en]
        v = s.inc(1)
        E(en, lambda e, ff=f, hh=s.h: ff(e).then_inc(hh, 1))
        return v

    def wait(en, sem, val):
        if val is None:
            return
        E(en, lambda e, hh=sem.h, v=val: e.wait_ge(hh, v))

    def dma(queue, out, in_, sem=None):
        s = sem if sem is not None else DSEM[queue]
        v = s.inc(16)
        E(queue, lambda e, o=out, i=in_, hh=s.h: e.dma_start(out=o, in_=i).then_inc(hh, 16))
        return v

    def bar():
        tgt = PHB.inc(5)
        for en in ENGINES:
            E(en, lambda e, hh=PHB.h: e.nop().then_inc(hh, 1))
            E(en, lambda e, hh=PHB.h, t=tgt: e.wait_ge(hh, t))

    freemap = {}

    def guard(en, name):
        if name in freemap:
            s, v = freemap.pop(name)
            wait(en, s, v)

    def release(name, sem, val):
        freemap[name] = (sem, val)

    def dbg(name, tens):
        if not KDBG:
            return
        sh = list(tens.shape)
        d = nc.dram_tensor("dbg_" + name, sh, tens.dtype, kind="ExternalOutput")
        dma("gpsimd", d[:, :] if len(sh) == 2 else d[:], tens[:])

    import os as _os
    _klim = int(_os.environ.get("KPHASE", "999"))
    class _StopSched(Exception):
        pass
    _phctr = [0]
    _orig_bar = bar
    def bar():
        _orig_bar()
        _phctr[0] += 1
        if _phctr[0] >= _klim:
            raise _StopSched()

    def _sched():
        def qw_rowmax(nat_d, rmx, oc_n, w_in, bufs):
            for oc in range(oc_n):
                nb, key = bufs[oc % len(bufs)]
                guard("sync", key)
                vld = dma("sync", nb[:, 0:w_in], nat_d[oc * 128:(oc + 1) * 128, :])
                wait("vector", LD, vld)
                vr = emit("vector", lambda e, oc=oc, nb=nb, rm=rmx, w_in=w_in: e.reduce_max(
                    rm[:, oc:oc + 1], nb[:, 0:w_in], AX.X, apply_absolute_value=True), inc=True)
                release(key, VEs, vr)

        dma("sync", id_f[:], id_dram[:, :])
        dma("sync", c_sb[:], cTd[:, :])
        dma("sync", bq_v[:], bq_d[:, :])
        dma("sync", bg2_v[:], bg2_d[:, :])
        dma("sync", bout_v[:], bout_d[:, :])
        dma("sync", lng_v[:], lng_d[:, :])
        dma("sync", lnb_v[:], lnb_d[:, :])
        dma("sync", bg1_v[:], bg1_d[:, :])
        dma("sync", clip_v[:], clip_d[:, :])
        dma("sync", onesrow_f[:], onesrow_dram[:, :])
        dma("sync", bvrow_sb[:], bv_row_d[:, :])
        dma("gpsimd", onescol_bf[:], onescol_dram[:, :])
        dma("gpsimd", onesrow_bf[:], onesrow_dram[:, :])
        for kk in range(KC):
            dma("gpsimd", wmod_bf[:, kk * 6 * D:(kk + 1) * 6 * D],
                wmodd[kk * 128:(kk + 1) * 128, :])
        v_ld0, v_gd0 = LD.n, GD.n
        bar()
        for en in ("vector", "scalar", "tensor"):
            wait(en, LD, v_ld0)
            wait(en, GD, v_gd0)

        emit("vector", lambda e: e.memset(ccs_sb[:], 0.0), inc=True)
        emit("vector", lambda e: e.memset(cv[0:1, 0:1], 1.0 / D), inc=True)
        emit("vector", lambda e: e.memset(cv[0:1, 1:2], 1e-5), inc=True)
        vcv = emit("vector", lambda e: e.memset(cv[0:1, 2:3], -1.0), inc=True)
        vsc = emit("scalar", lambda e: e.activation(c_sb[:], c_sb[:], AF.Silu), inc=True)
        wait("vector", SEs, vsc)
        vcb = emit("vector", lambda e: e.tensor_copy(c_bf[:], c_sb[:]), inc=True)
        for j in range(2):
            emit("tensor", lambda e, j=j: e.matmul(
                pb[2 + j][:, 0:512], onesrow_f[0:1, :],
                bvrow_sb[0:1, j * 512:(j + 1) * 512], start=True, stop=True), inc=True)
        vpe_bv = PEs.n
        wait("vector", PEs, vpe_bv)
        for j in range(2):
            vv = emit("vector", lambda e, j=j: e.tensor_copy(
                bvbc[:, j * 512:(j + 1) * 512], pb[2 + j][:, 0:512]), inc=True)
            release(f"ps{2 + j}", VEs, vv)
        wait("tensor", VEs, vcb)
        for mc in [m for m in range(48) if not (16 <= m < 24 or 40 <= m < 48)]:
            bk = mc % 2
            guard("tensor", f"ps{bk}")
            for kk in range(KC):
                emit("tensor", lambda e, mc=mc, kk=kk, bk=bk: e.matmul(
                    pb[bk][:, 0:2],
                    wmod_bf[:, kk * 6 * D + mc * 128:kk * 6 * D + mc * 128 + 128],
                    c_bf[:, kk * 2:kk * 2 + 2],
                    start=(kk == 0), stop=(kk == KC - 1)), inc=(kk == KC - 1))
            vmm = PEs.n
            wait("vector", PEs, vmm)
            Vv, ff = mc // 8, mc % 8
            for b_ in range(2):
                vv = emit("vector", lambda e, bk=bk, b_=b_, Vv=Vv, ff=ff: e.tensor_copy(
                    mod6[:, Vv * 16 + b_ * 8 + ff:Vv * 16 + b_ * 8 + ff + 1],
                    pb[bk][:, b_:b_ + 1]), inc=True)
            release(f"ps{bk}", VEs, vv)
        release("wm32b", PEs, PEs.n)
        for blk, mc0 in enumerate((16, 40)):
            vld = 0
            for kk in range(KC):
                if kk == 0:
                    guard("sync", "wm32b")
                vld = dma("sync", wm32b[:, kk * 1024:(kk + 1) * 1024],
                          wmodd[kk * 128:(kk + 1) * 128, mc0 * 128:(mc0 + 8) * 128])
            wait("tensor", LD, vld)
            for j in range(8):
                mc = mc0 + j
                bk = j % 2
                guard("tensor", f"ps{bk}")
                for kk in range(KC):
                    emit("tensor", lambda e, kk=kk, bk=bk, j=j: e.matmul(
                        pb[bk][:, 0:2],
                        wm32b[:, kk * 1024 + j * 128:kk * 1024 + j * 128 + 128],
                        c_sb[:, kk * 2:kk * 2 + 2],
                        start=(kk == 0), stop=(kk == KC - 1)), inc=(kk == KC - 1))
                vmm = PEs.n
                wait("vector", PEs, vmm)
                Vv, ff = mc // 8, mc % 8
                for b_ in range(2):
                    vv = emit("vector", lambda e, bk=bk, b_=b_, Vv=Vv, ff=ff: e.tensor_copy(
                        mod6[:, Vv * 16 + b_ * 8 + ff:Vv * 16 + b_ * 8 + ff + 1],
                        pb[bk][:, b_:b_ + 1]), inc=True)
                release(f"ps{bk}", VEs, vv)
            release("wm32b", PEs, PEs.n)
        emit("vector", lambda e: e.drain())
        emit("vector", lambda e: e.tensor_copy(sa_v[:], mod6[:, 0:16]), inc=True)
        emit("vector", lambda e: e.tensor_scalar(ca1_v[:], mod6[:, 16:32], 1.0, None, ALU.add), inc=True)
        emit("vector", lambda e: e.tensor_copy(sb_v[:], mod6[:, 48:64]), inc=True)
        emit("vector", lambda e: e.tensor_scalar(cb1_v[:], mod6[:, 64:80], 1.0, None, ALU.add), inc=True)
        for (src0, dstv) in ((32, gap_v), (80, gbp_v)):
            emit("vector", lambda e, s0=src0: e.tensor_single_scalar(
                sc16a[:], mod6[:, s0:s0 + 16], 0.0, ALU.is_ge), inc=True)
            emit("vector", lambda e: e.drain())
            emit("vector", lambda e: e.tensor_scalar(
                sc16b[:], sc16a[:], 2.0, -1.0, ALU.mult, ALU.add), inc=True)
            emit("vector", lambda e, s0=src0: e.tensor_scalar(
                sc16a[:], mod6[:, s0:s0 + 16], -1.0, 0.25, ALU.mult, ALU.max), inc=True)
            emit("vector", lambda e: e.drain())
            emit("vector", lambda e, s0=src0: e.tensor_tensor(
                sc16a[:], sc16a[:], mod6[:, s0:s0 + 16], ALU.max), inc=True)
            emit("vector", lambda e: e.drain())
            emit("vector", lambda e, d=dstv: e.tensor_tensor(
                d[:], sc16a[:], sc16b[:], ALU.mult), inc=True)
            emit("vector", lambda e: e.drain())
        for b_ in range(2):
            emit("vector", lambda e, b_=b_: e.tensor_tensor(
                gabo_v[:, b_ * 8:b_ * 8 + 8], gap_v[:, b_ * 8:b_ * 8 + 8], bout_v[:],
                ALU.mult), inc=True)
        emit("vector", lambda e: e.drain())
        bar()
        for nm, tv in (("mod6", mod6), ("sa", sa_v), ("ca1", ca1_v), ("gap", gap_v),
                       ("sb", sb_v), ("cb1", cb1_v), ("gbp", gbp_v), ("gabo", gabo_v),
                       ("csilu", c_sb), ("bvbc", bvbc)):
            dbg(nm, tv)
        bar()

        for kk in range(KC):
            dma("gpsimd", wkv_bf[:, kk * 2 * D:(kk + 1) * 2 * D],
                wqkvT[kk * 128:(kk + 1) * 128, D:3 * D])
        for kk in range(KC):
            dma("gpsimd", wout_bf[:, kk * D:(kk + 1) * D], woutT[kk * 128:(kk + 1) * 128, :])
        for kk in range(KC):
            dma("gpsimd", wg1_bf[:, kk * HD:(kk + 1) * HD], wg1T[kk * 128:(kk + 1) * 128, :])
        v_wload = dma("gpsimd", wg2_bf[:], wg2T[:, :])

        def ln_tile(srcs, out_writer, chunk_wait=None):
            guard("tensor", "ps0")
            guard("tensor", "ps1")
            for c in range(KC):
                sq = sqa if c % 2 == 0 else sqb
                guard("scalar", f"sq{c % 2}")
                if chunk_wait is not None:
                    w = chunk_wait(c)
                    if w is not None:
                        wait("scalar", *w)
                        wait("vector", *w)
                vsq = emit("scalar", lambda e, s=srcs[c], q=sq: e.activation(
                    q[:], s, AF.Square), inc=True)
                wait("tensor", SEs, vsq)
                emit("tensor", lambda e, s=srcs[c], c=c: e.matmul(
                    pb[0][0:1, 0:512], onescol_bf[:, 0:1], s,
                    start=(c == 0), stop=(c == KC - 1)), inc=(c == KC - 1))
                vq = emit("tensor", lambda e, q=sq, c=c: e.matmul(
                    pb[1][0:1, 0:512], onescol_bf[:, 0:1], q[:],
                    start=(c == 0), stop=(c == KC - 1)), inc=True)
                release(f"sq{c % 2}", PEs, vq)
            vstat = PEs.n
            wait("scalar", PEs, vstat)
            emit("scalar", lambda e: e.activation(muT[:], pb[0][0:1, 0:512], AF.Identity,
                                                  scale=cv[0:1, 0:1]), inc=True)
            v1 = emit("scalar", lambda e: e.activation(e2T[:], pb[1][0:1, 0:512], AF.Identity,
                                                       scale=cv[0:1, 0:1]), inc=True)
            release("ps0", SEs, v1)
            release("ps1", SEs, v1)
            wait("vector", SEs, v1)
            emit("vector", lambda e: e.tensor_tensor(tmpT[:], muT[:], muT[:], ALU.mult), inc=True)
            v2 = emit("vector", lambda e: e.tensor_tensor(varT[:], e2T[:], tmpT[:], ALU.subtract), inc=True)
            wait("scalar", VEs, v2)
            v3 = emit("scalar", lambda e: e.activation(tmpT[:], varT[:], AF.Sqrt,
                                                       bias=cv[0:1, 1:2]), inc=True)
            wait("vector", SEs, v3)
            emit("vector", lambda e: e.reciprocal(rT[:], tmpT[:]), inc=True)
            emit("vector", lambda e: e.tensor_tensor(mrT[:], muT[:], rT[:], ALU.mult), inc=True)
            guard("vector", "rtbf")
            emit("vector", lambda e: e.tensor_copy(rT_bf[:], rT[:]), inc=True)
            v4 = emit("vector", lambda e: e.tensor_copy(mrT_bf[:], mrT[:]), inc=True)
            guard("tensor", "ps2")
            guard("tensor", "ps3")
            wait("tensor", VEs, v4)
            emit("tensor", lambda e: e.matmul(pb[2][:, 0:512], onesrow_bf[0:1, :], rT_bf[0:1, :],
                                              start=True, stop=True), inc=True)
            v5 = emit("tensor", lambda e: e.matmul(pb[3][:, 0:512], onesrow_bf[0:1, :], mrT_bf[0:1, :],
                                                   start=True, stop=True), inc=True)
            release("rtbf", PEs, v5)
            wait("vector", PEs, v5)
            guard("vector", "rbmb")
            emit("vector", lambda e: e.tensor_copy(rb_a[:], pb[2][:, 0:512]), inc=True)
            v6 = emit("vector", lambda e: e.tensor_copy(mb_a[:], pb[3][:, 0:512]), inc=True)
            release("ps2", VEs, v6)
            release("ps3", VEs, v6)
            lastv = None
            for c in range(KC):
                t1 = t1a if c % 2 == 0 else t1b
                t2 = t2a if c % 2 == 0 else t2b
                guard("vector", f"t12{c % 2}")
                emit("vector", lambda e, s=srcs[c], t=t1: e.tensor_tensor(
                    t[:], s, rb_a[:], ALU.mult), inc=True)
                v7 = emit("vector", lambda e, t=t1, u=t2: e.tensor_tensor(
                    u[:], t[:], mb_a[:], ALU.subtract), inc=True)
                en, fn = out_writer(c, t2)
                wait(en, VEs, v7)
                v8 = emit(en, fn, inc=True)
                release(f"t12{c % 2}", ESEM[en], v8)
                lastv = (ESEM[en], v8)
            release("rbmb", *lastv)
            bar()
            return lastv

        def xh_sl(c, t0, n):
            return xh_bf[:, c * S + t0:c * S + t0 + n]

        v_x2store = None
        for s in range(B):
            guard("gpsimd", "xh")
            xdmav = []
            for c in range(KC):
                xdmav.append(dma("gpsimd", xh_bf[:, c * S:(c + 1) * S],
                                 xT[s * D + c * 128:s * D + (c + 1) * 128, :]))
            if s == 1:
                guard("gpsimd", "Wreg")
                for kk in range(KC):
                    dma("gpsimd", wkv_bf[:, kk * 2 * D:(kk + 1) * 2 * D],
                        wqkvT[kk * 128:(kk + 1) * 128, D:3 * D])
            v_wkv = GD.n if s == 1 else v_wload
            emit("vector", lambda e: e.memset(wrk[:, WK_MRUN:WK_MRUN + 1], -1e30), inc=True)
            for _k in ("x2f0", "x2f1", "xf0", "xf1", "e_f"):
                guard("vector", _k)
            emit("vector", lambda e: e.memset(
                v_sb[:, :].rearrange("p (x e) -> p x e", e=65)[:, :, 64:65], 1.0), inc=True)
            wait("tensor", GD, v_wkv)

            def emit_kv(kt):
                for half, bks in ((0, (4, 5)), (1, (6, 7))):
                    for ncnk in range(2):
                        bk = bks[ncnk]
                        guard("tensor", f"ps{bk}")
                        for c in range(KC):
                            emit("tensor", lambda e, kt=kt, c=c, h=half, ncnk=ncnk, bk=bk: e.matmul(
                                pb[bk][:, 0:512],
                                xh_bf[:, c * S + kt * 128:c * S + kt * 128 + 128],
                                wkv_bf[:, c * 2 * D + h * D + ncnk * 512:
                                       c * 2 * D + h * D + ncnk * 512 + 512],
                                start=(c == 0), stop=(c == KC - 1)),
                                inc=(c == KC - 1))
                vkv = PEs.n
                wait("vector", PEs, vkv)
                for j, bk in enumerate((4, 5)):
                    vv = emit("vector", lambda e, kt=kt, j=j, bk=bk: e.tensor_copy(
                        k_bf[:, kt * D + j * 512:kt * D + j * 512 + 512],
                        pb[bk][:, 0:512]), inc=True)
                    release(f"ps{bk}", VEs, vv)
                for j, bk in enumerate((6, 7)):
                    vv = emit("vector", lambda e, kt=kt, j=j, bk=bk: e.tensor_copy(
                        v_sb[:, kt * 1040 + j * 520:kt * 1040 + (j + 1) * 520]
                            .rearrange("p (h e) -> p h e", e=65)[:, :, 0:64],
                        pb[bk][:, 0:512].rearrange("p (h e) -> p h e", e=64)), inc=True)
                    release(f"ps{bk}", VEs, vv)
                emit("vector", lambda e, kt=kt: e.reduce_max(
                    wrk[:, WK_TMP:WK_TMP + 1], k_bf[:, kt * D:(kt + 1) * D],
                    AX.X), inc=True)
                emit("vector", lambda e: e.drain())
                emit("vector", lambda e: e.tensor_tensor(
                    wrk[:, WK_MRUN:WK_MRUN + 1], wrk[:, WK_MRUN:WK_MRUN + 1],
                    wrk[:, WK_TMP:WK_TMP + 1], ALU.max), inc=True)
                emit("vector", lambda e: e.drain())

            for t in range(NT):
                def w_h(s=s, t=t):
                    def w(c, t2buf):
                        return ("scalar", lambda e, c=c, u=t2buf: e.activation(
                            xh_sl(c, t * FT, FT), u[:], AF.Identity,
                            scale=ca1_v[:, s * KC + c:s * KC + c + 1],
                            bias=sa_v[:, s * KC + c:s * KC + c + 1]))
                    return w
                ln_tile([xh_sl(c, t * FT, FT) for c in range(KC)], w_h(),
                        chunk_wait=lambda c, vs=tuple(xdmav): (GD, vs[c]))
                for kt in range(4 * t, 4 * t + 4):
                    emit_kv(kt)
            bar()
            if s == 0:
                dbg("h_s0", xh_bf)
                bar()

            bar()
            if s == 0:
                dbg("k_s0", k_bf)
                dbg("v_s0", v_sb)
                bar()

            emit("tensor", lambda e: e.transpose(
                pb[4][0:1, 0:128], wrk[:, WK_MRUN:WK_MRUN + 1], id_f[:]), inc=True)
            vt = PEs.n
            wait("vector", PEs, vt)
            vv = emit("vector", lambda e: e.reduce_max(
                wrow[0:1, WR_M:WR_M + 1], pb[4][0:1, 0:128], AX.X), inc=True)
            wait("scalar", VEs, vv)
            vsc = emit("scalar", lambda e: e.activation(
                wrow[0:1, WR_NEGM:WR_NEGM + 1], wrow[0:1, WR_M:WR_M + 1],
                AF.Identity, scale=cv[0:1, 2:3]), inc=True)
            wait("tensor", SEs, vsc)
            emit("tensor", lambda e: e.matmul(
                pb[4][:, 0:1], onesrow_f[0:1, :], wrow[0:1, WR_NEGM:WR_NEGM + 1],
                start=True, stop=True), inc=True)
            vt2 = PEs.n
            wait("vector", PEs, vt2)
            vnm = emit("vector", lambda e: e.tensor_copy(
                wrk[:, WK_NEGM:WK_NEGM + 1], pb[4][:, 0:1]), inc=True)
            bar()

            for kt in range(KT):
                ek = eka if kt % 2 == 0 else ekb
                guard("scalar", f"ek{kt % 2}")
                vek = emit("scalar", lambda e, kt=kt, k=ek: e.activation(
                    k[:], k_bf[:, kt * D:(kt + 1) * D], AF.Exp,
                    bias=wrk[:, WK_NEGM:WK_NEGM + 1]), inc=True)
                wait("tensor", SEs, vek)
                for p in range(8):
                    if kt == 0:
                        guard("tensor", f"ps{p}")
                    emit("tensor", lambda e, kt=kt, p=p, k=ek: e.matmul(
                        pb[p][:, 0:130],
                        k[:, p * 128:(p + 1) * 128],
                        v_sb[:, kt * 1040 + p * 130:kt * 1040 + (p + 1) * 130],
                        start=(kt == 0), stop=(kt == KT - 1)),
                        inc=(p == 7))
                release(f"ek{kt % 2}", PEs, PEs.n)
            vkvacc = PEs.n
            wait("vector", PEs, vkvacc)
            guard("vector", "kv_bfs")
            for p in range(8):
                emit("vector", lambda e, p=p: e.tensor_copy(
                    den_v[0:64, 0:1], pb[p][0:64, 64:65]), inc=True)
                emit("vector", lambda e, p=p: e.tensor_copy(
                    den_v[64:128, 0:1], pb[p][64:128, 129:130]), inc=True)
                emit("vector", lambda e: e.drain())
                emit("vector", lambda e, p=p: e.scalar_tensor_tensor(
                    kvtmp[0:64, :], bvbc[0:64, p * 128:p * 128 + 64],
                    den_v[0:64, 0:1], pb[p][0:64, 0:64], ALU.mult, ALU.add), inc=True)
                emit("vector", lambda e, p=p: e.scalar_tensor_tensor(
                    kvtmp[64:128, :], bvbc[64:128, p * 128 + 64:p * 128 + 128],
                    den_v[64:128, 0:1], pb[p][64:128, 65:129], ALU.mult, ALU.add), inc=True)
                emit("vector", lambda e: e.drain())
                vv = emit("vector", lambda e: e.reciprocal(den_v[:, 1:2], den_v[:, 0:1]), inc=True)
                emit("vector", lambda e: e.drain())
                release(f"ps{p}", VEs, vv)
                wait("scalar", VEs, vv)
                vkvp = emit("scalar", lambda e, p=p: e.activation(
                    kv_bfs[:, p * HD:(p + 1) * HD], kvtmp[:, :], AF.Identity,
                    scale=den_v[:, 1:2]), inc=True)
            bar()
            if s == 0:
                dbg("kv_s0", kv_bfs)
                dbg("wrk_s0", wrk)
                bar()

            guard("gpsimd", "Wreg")
            for kk in range(KC):
                dma("gpsimd", wq_bf[:, kk * D:(kk + 1) * D],
                    wqkvT[kk * 128:(kk + 1) * 128, 0:D])
            v_wq = GD.n
            wait("tensor", GD, v_wq)
            for t in range(NT):
                for oc in range(KC):
                    bk = oc % 2
                    guard("tensor", f"ps{bk}")
                    for c in range(KC):
                        emit("tensor", lambda e, oc=oc, c=c, t=t, bk=bk: e.matmul(
                            pb[bk][:, 0:512],
                            wq_bf[:, c * D + oc * 128:c * D + oc * 128 + 128],
                            xh_sl(c, t * FT, FT),
                            start=(c == 0), stop=(c == KC - 1)), inc=(c == KC - 1))
                    vq = PEs.n
                    wait("scalar", PEs, vq)
                    vqe = emit("scalar", lambda e, oc=oc, bk=bk: e.activation(
                        q_bf[:, oc * FT:(oc + 1) * FT], pb[bk][:, 0:512], AF.Silu,
                        bias=bq_v[:, oc:oc + 1]), inc=True)
                    release(f"ps{bk}", SEs, vqe)
                vqall = SEs.n
                wait("tensor", SEs, vqall)
                for oc in range(KC):
                    bk = 2 + oc % 2
                    guard("tensor", f"ps{bk}")
                    emit("tensor", lambda e, oc=oc, bk=bk: e.matmul(
                        pb[bk][0:64, 0:512], kv_bfs[0:64, oc * HD:(oc + 1) * HD],
                        q_bf[0:64, oc * FT:(oc + 1) * FT],
                        start=True, stop=True), inc=True)
                    vy = emit("tensor", lambda e, oc=oc, bk=bk: e.matmul(
                        pb[bk][64:128, 0:512], kv_bfs[64:128, oc * HD:(oc + 1) * HD],
                        q_bf[64:128, oc * FT:(oc + 1) * FT],
                        start=True, stop=True, tile_position=(64, 64)), inc=True)
                    wait("scalar", PEs, vy)
                    vye = emit("scalar", lambda e, oc=oc, bk=bk: e.activation(
                        y_bf[:, oc * FT:(oc + 1) * FT], pb[bk][:, 0:512], AF.Identity), inc=True)
                    release(f"ps{bk}", SEs, vye)
                vyall = SEs.n

                guard("tensor", "ps4")
                for c in range(KC):
                    emit("tensor", lambda e, c=c, t=t: e.matmul(
                        pb[4][0:64, 0:512], wg1_bf[:, c * HD:(c + 1) * HD],
                        xh_sl(c, t * FT, FT),
                        start=(c == 0), stop=(c == KC - 1)), inc=(c == KC - 1))
                vgh = PEs.n
                wait("scalar", PEs, vgh)
                vghe = emit("scalar", lambda e: e.activation(
                    gh_bf[:, :], pb[4][0:64, 0:512], AF.Identity,
                    bias=bg1_v[:, 0:1]), inc=True)
                release("ps4", SEs, vghe)
                def w_z(c, t2buf):
                    return ("scalar", lambda e, c=c, u=t2buf: e.activation(
                        z_bf[:, c * FT:(c + 1) * FT], u[:], AF.Identity,
                        scale=lng_v[:, c:c + 1], bias=lnb_v[:, c:c + 1]))
                ln_tile([y_bf[:, c * FT:(c + 1) * FT] for c in range(KC)], w_z,
                        chunk_wait=lambda c, v=vyall: (SEs, v))

                wait("tensor", SEs, vghe)
                for oc in range(KC):
                    bk = 5 + oc % 2
                    guard("tensor", f"ps{bk}")
                    vg = emit("tensor", lambda e, oc=oc, bk=bk: e.matmul(
                        pb[bk][:, 0:512], wg2_bf[0:64, oc * 128:(oc + 1) * 128],
                        gh_bf[0:64, :], start=True, stop=True), inc=True)
                    wait("scalar", PEs, vg)
                    vge = emit("scalar", lambda e, oc=oc, bk=bk: e.activation(
                        gate_bf[:, oc * FT:(oc + 1) * FT], pb[bk][:, 0:512], AF.Sigmoid,
                        bias=bg2_v[:, oc:oc + 1]), inc=True)
                    release(f"ps{bk}", SEs, vge)
                    wait("vector", SEs, vge)
                    emit("vector", lambda e, oc=oc: e.tensor_tensor(
                        y_bf[:, oc * FT:(oc + 1) * FT], z_bf[:, oc * FT:(oc + 1) * FT],
                        gate_bf[:, oc * FT:(oc + 1) * FT], ALU.mult), inc=True)
                vgated = VEs.n
                wait("tensor", VEs, vgated)
                for oc in range(KC):
                    bk = oc % 2
                    xb = xf_a if oc % 2 == 0 else xf_b
                    guard("sync", f"xf{oc % 2}")
                    vxf = dma("sync", xb[:], xT[s * D + oc * 128:s * D + oc * 128 + 128,
                                                t * FT:(t + 1) * FT])
                    guard("tensor", f"ps{bk}")
                    for c in range(KC):
                        emit("tensor", lambda e, oc=oc, c=c, bk=bk: e.matmul(
                            pb[bk][:, 0:512],
                            wout_bf[:, c * D + oc * 128:c * D + oc * 128 + 128],
                            y_bf[:, c * FT:(c + 1) * FT],
                            start=(c == 0), stop=(c == KC - 1)), inc=(c == KC - 1))
                    vyo = PEs.n
                    wait("scalar", PEs, vyo)
                    guard("scalar", "e_f")
                    vef = emit("scalar", lambda e, s=s, oc=oc, bk=bk: e.activation(
                        e_f[:], pb[bk][:, 0:512], AF.Identity,
                        scale=gap_v[:, s * KC + oc:s * KC + oc + 1],
                        bias=gabo_v[:, s * KC + oc:s * KC + oc + 1]), inc=True)
                    release(f"ps{bk}", SEs, vef)
                    x2b = x2f_a if oc % 2 == 0 else x2f_b
                    guard("vector", f"x2f{oc % 2}")
                    wait("vector", SEs, vef)
                    wait("vector", LD, vxf)
                    vx2 = emit("vector", lambda e, xb=xb, x2b=x2b: e.tensor_tensor(
                        x2b[:], e_f[:], xb[:], ALU.add), inc=True)
                    release("e_f", VEs, vx2)
                    release(f"xf{oc % 2}", VEs, vx2)
                    wait("gpsimd", VEs, vx2)
                    vst = dma("gpsimd", x2T[s * D + oc * 128:s * D + oc * 128 + 128,
                                            t * FT:(t + 1) * FT], x2b[:])
                    release(f"x2f{oc % 2}", GD, vst)
            v_x2store = GD.n
            vqlast = PEs.n
            release("Wreg", PEs, vqlast)
            release("xh", PEs, vqlast)
            release("kv_bfs", PEs, vqlast)
            bar()

        qw_rowmax(wg_nat, rmxg, IC, D, [(natq, "natb"), (wtf_f, "natb2")])
        qw_rowmax(wu_nat, rmxu, IC, D, [(natq, "natb"), (wtf_f, "natb2")])
        qw_rowmax(wd_nat, rmxd, KC, DI, [(wtf_f, "natb2"), (cbc_f, "natbc")])
        emit("vector", lambda e: e.memset(wrk[:, WK_HMAX:WK_HMAX + 1], 0.0), inc=True)
        wait("gpsimd", GD, v_x2store)
        for ct in range(CT):
            s, t = ct // NT, ct % NT
            x2t = x2t_bf if ct % 2 == 0 else x2t_bf2
            x2v = []
            for c in range(KC):
                guard("gpsimd", f"x2t{ct % 2}")
                x2v.append(dma("gpsimd", x2t[:, c * FT:(c + 1) * FT],
                               x2T[s * D + c * 128:s * D + c * 128 + 128,
                                   t * FT:(t + 1) * FT]))

            def w_h2(c, t2buf, s=s, ct=ct):
                return ("scalar", lambda e, c=c, u=t2buf: e.activation(
                    h2_bf[:, c * B * S + ct * FT:c * B * S + (ct + 1) * FT], u[:], AF.Identity,
                    scale=cb1_v[:, s * KC + c:s * KC + c + 1],
                    bias=sb_v[:, s * KC + c:s * KC + c + 1]))
            lv = ln_tile([x2t[:, c * FT:(c + 1) * FT] for c in range(KC)], w_h2,
                         chunk_wait=lambda c, vs=tuple(x2v): (GD, vs[c]))
            release(f"x2t{ct % 2}", *lv)
        vh2 = SEs.n
        wait("vector", SEs, vh2)
        for c in range(KC):
            emit("vector", lambda e, c=c: e.reduce_max(
                wrk[:, WK_TMP:WK_TMP + 1], h2_bf[:, c * B * S:(c + 1) * B * S],
                AX.X, apply_absolute_value=True), inc=True)
            emit("vector", lambda e: e.drain())
            emit("vector", lambda e: e.tensor_tensor(
                wrk[:, WK_HMAX:WK_HMAX + 1], wrk[:, WK_HMAX:WK_HMAX + 1],
                wrk[:, WK_TMP:WK_TMP + 1], ALU.max), inc=True)
            emit("vector", lambda e: e.drain())
        bar()

        def collapse_and_cc(src_col, wr_dst, cci, cco):
            guard("tensor", "ps4")
            emit("tensor", lambda e, sc=src_col: e.transpose(
                pb[4][0:1, 0:128], wrk[:, sc:sc + 1], id_f[:]), inc=True)
            vt = PEs.n
            wait("vector", PEs, vt)
            vv = emit("vector", lambda e, d=wr_dst: e.reduce_max(
                wrow[0:1, d:d + 1], pb[4][0:1, 0:128], AX.X), inc=True)
            release("ps4", VEs, vv)
            emit("vector", lambda e: e.drain())
            vcp = emit("vector", lambda e, d=wr_dst: e.tensor_copy(
                ccs_sb[0:1, 0:1], wrow[0:1, d:d + 1]), inc=True)
            wait("gpsimd", VEs, vcp)
            vdm = dma("gpsimd", cci[0:1, :], ccs_sb[0:1, :])
            wait("gpsimd", GD, vdm)
            ccv = CCs.inc(1)
            E("gpsimd", lambda e, i=cci, o=cco: e.collective_compute(
                "AllReduce", ALU.max, replica_groups=[list(range(8))],
                ins=[i.ap()], outs=[o.ap()]).then_inc(CCs.h, 1))
            wait("gpsimd", CCs, ccv)
            return dma("gpsimd", ccr_sb[0:1, :], cco[0:1, :])

        vcc1 = collapse_and_cc(WK_HMAX, WR_HMAX, cc1i, cc1o)
        vsg = emit("scalar", lambda e: e.activation(
            wrow[0:1, WR_SIG:WR_SIG + 1], clip_v[0:1, 0:1], AF.Sigmoid), inc=True)
        wait("vector", GD, vcc1)
        wait("vector", SEs, vsg)
        emit("vector", lambda e: e.tensor_tensor(
            wrow[0:1, WR_SX:WR_SX + 1], wrow[0:1, WR_SIG:WR_SIG + 1],
            ccr_sb[0:1, 0:1], ALU.mult), inc=True)
        emit("vector", lambda e: e.drain())
        emit("vector", lambda e: e.tensor_scalar(
            wrow[0:1, WR_SX:WR_SX + 1], wrow[0:1, WR_SX:WR_SX + 1],
            1.0 / 127.0, None, ALU.mult), inc=True)
        emit("vector", lambda e: e.drain())
        vrsx = emit("vector", lambda e: e.reciprocal(
            wrow[0:1, WR_RSX:WR_RSX + 1], wrow[0:1, WR_SX:WR_SX + 1]), inc=True)
        emit("vector", lambda e: e.drain())
        guard("tensor", "ps4")
        wait("tensor", VEs, vrsx)
        emit("tensor", lambda e: e.matmul(
            pb[4][:, 0:2], onesrow_f[0:1, :], wrow[0:1, WR_SX:WR_SX + 2],
            start=True, stop=True), inc=True)
        vbx = PEs.n
        wait("vector", PEs, vbx)
        vsx2 = emit("vector", lambda e: e.tensor_copy(
            wrk[:, WK_SXB:WK_SXB + 2], pb[4][:, 0:2]), inc=True)
        release("ps4", VEs, vsx2)
        wait("scalar", VEs, vsx2)
        for c in range(KC):
            emit("scalar", lambda e, c=c: e.activation(
                xq_i8[:, c * B * S:(c + 1) * B * S], h2_bf[:, c * B * S:(c + 1) * B * S],
                AF.Identity, scale=wrk[:, WK_RSX:WK_RSX + 1]), inc=True)
        bar()

        for nm, tv in (("h2", h2_bf), ("xq", xq_i8), ("wrow1", wrow), ("wrk1", wrk)):
            dbg(nm, tv)
        bar()
        def quant_weight(wT_d, rmx, wq_sb, n_out, n_in_chunks, cq, cqT_sb,
                         crow_sb, cbc_sb, wtf, w8):
            oc_n = n_out // 128
            emit("vector", lambda e: e.drain())
            emit("vector", lambda e, rm=rmx, c=cq: e.reciprocal(c[:, 0:oc_n], rm[:, 0:oc_n]), inc=True)
            emit("vector", lambda e: e.drain())
            vcq = emit("vector", lambda e, c=cq: e.tensor_scalar(
                c[:, 0:oc_n], c[:, 0:oc_n], 127.0, None, ALU.mult), inc=True)
            emit("vector", lambda e: e.drain())
            guard("tensor", "ps4")
            wait("tensor", VEs, vcq)
            emit("tensor", lambda e, c=cq, o=oc_n: e.transpose(
                pb[4][0:o, 0:128], c[:, 0:o], id_f[:]), inc=True)
            vtp = PEs.n
            wait("vector", PEs, vtp)
            vct = emit("vector", lambda e, ct_=cqT_sb, o=oc_n: e.tensor_copy(
                ct_[0:o, :], pb[4][0:o, 0:128]), inc=True)
            release("ps4", VEs, vct)
            wait("sync", VEs, vct)
            guard("sync", "wtf")
            vcr = dma("sync", crow_sb[0:1, 0:n_out], cqT_sb[0:oc_n, :])
            wait("tensor", LD, vcr)
            nbk = (n_out + 511) // 512
            for j in range(nbk):
                w_ = min(512, n_out - j * 512)
                bk = 4 + (j % 2)
                guard("tensor", f"ps{bk}")
                vbc = emit("tensor", lambda e, j=j, bk=bk, cr=crow_sb, w_=w_: e.matmul(
                    pb[bk][:, 0:w_], onesrow_f[0:1, :], cr[0:1, j * 512:j * 512 + w_],
                    start=True, stop=True), inc=True)
                wait("vector", PEs, vbc)
                vcb2 = emit("vector", lambda e, j=j, bk=bk, cb=cbc_sb, w_=w_: e.tensor_copy(
                    cb[:, j * 512:j * 512 + w_], pb[bk][:, 0:w_]), inc=True)
                release(f"ps{bk}", VEs, vcb2)
            vcbc = VEs.n
            release("wtf", PEs, PEs.n)
            wait("vector", VEs, vcbc)
            for kk in range(n_in_chunks):
                guard("sync", "wtf")
                vld = dma("sync", wtf[:, 0:n_out], wT_d[kk * 128:(kk + 1) * 128, :])
                wait("vector", LD, vld)
                guard("vector", "w8")
                v8 = emit("vector", lambda e, w=wtf, cb=cbc_sb, o=n_out, x8=w8: e.scalar_tensor_tensor(
                    x8[:, 0:o], w[:, 0:o], 1.0, cb[:, 0:o], ALU.mult, ALU.mult), inc=True)
                release("wtf", VEs, v8)
                if kk % 2 == 0:
                    wait("scalar", VEs, v8)
                    vq = emit("scalar", lambda e, kk=kk, x8=w8, o=n_out, wq=wq_sb: e.activation(
                        wq[:, kk * o:(kk + 1) * o], x8[:, 0:o], AF.Identity), inc=True)
                    release("w8", SEs, vq)
                else:
                    vq = emit("vector", lambda e, kk=kk, x8=w8, o=n_out, wq=wq_sb: e.tensor_copy(
                        wq[:, kk * o:(kk + 1) * o], x8[:, 0:o]), inc=True)
                    release("w8", VEs, vq)
            wait("scalar", VEs, VEs.n)
            return SEs.n

        v_gq = quant_weight(wgTd, rmxg, wgq_bf, DI, KC, cq22, cqT22,
                            crow, cbc_f, wtf_f, wq8b)
        v_uq = quant_weight(wuTd, rmxu, wuq_bf, DI, KC, cq22, cqT22,
                            crow, cbc_f, wtf_f, wq8b)
        wait("vector", VEs, vsx2)
        emit("vector", lambda e: e.drain())
        emit("vector", lambda e: e.tensor_scalar(
            sgc_g[:], rmxg[:], wrk[:, WK_SXB:WK_SXB + 1], 1.0 / 127.0,
            ALU.mult, ALU.mult), inc=True)
        emit("vector", lambda e: e.tensor_scalar(
            susx[:], rmxu[:], wrk[:, WK_SXB:WK_SXB + 1], 1.0 / 127.0,
            ALU.mult, ALU.mult), inc=True)
        emit("vector", lambda e: e.memset(tmaxr[:], 0.0), inc=True)
        bar()

        wait("tensor", SEs, v_uq)
        v_spill_all = None
        for ct in range(CT):
            guard("vector", "xqb")
            for c in range(KC):
                emit("vector", lambda e, c=c, ct=ct: e.tensor_copy(
                    xqb[:, c * FT:(c + 1) * FT],
                    xq_i8[:, c * B * S + ct * FT:c * B * S + (ct + 1) * FT]),
                    inc=(c == KC - 1))
            vxq = VEs.n
            wait("tensor", VEs, vxq)
            for ic in range(IC):
                gbk = ic % 2
                ubk = 2 + ic % 2
                guard("tensor", f"ps{gbk}")
                for c in range(KC):
                    emit("tensor", lambda e, ic=ic, c=c, bk=gbk: e.matmul(
                        pb[bk][:, 0:512],
                        wgq_bf[:, c * DI + ic * 128:c * DI + ic * 128 + 128],
                        xqb[:, c * FT:(c + 1) * FT],
                        start=(c == 0), stop=(c == KC - 1)), inc=(c == KC - 1))
                vgmm = PEs.n
                guard("tensor", f"ps{ubk}")
                for c in range(KC):
                    emit("tensor", lambda e, ic=ic, c=c, bk=ubk: e.matmul(
                        pb[bk][:, 0:512],
                        wuq_bf[:, c * DI + ic * 128:c * DI + ic * 128 + 128],
                        xqb[:, c * FT:(c + 1) * FT],
                        start=(c == 0), stop=(c == KC - 1)), inc=(c == KC - 1))
                vumm = PEs.n
                gs = gs_a if ic % 2 == 0 else gs_b
                up = up_a if ic % 2 == 0 else up_b
                tb = tb_a if ic % 2 == 0 else tb_b
                guard("scalar", f"gs{ic % 2}")
                wait("scalar", PEs, vgmm)
                vgse = emit("scalar", lambda e, ic=ic, bk=gbk, g=gs: e.activation(
                    g[:], pb[bk][:, 0:512], AF.Silu,
                    scale=sgc_g[:, ic:ic + 1]), inc=True)
                release(f"ps{gbk}", SEs, vgse)
                guard("scalar", f"up{ic % 2}")
                wait("scalar", PEs, vumm)
                vupe = emit("scalar", lambda e, bk=ubk, u=up: e.activation(
                    u[:], pb[bk][:, 0:512], AF.Identity), inc=True)
                release(f"ps{ubk}", SEs, vupe)
                guard("vector", f"tb{ic % 2}")
                guard("vector", f"tbS{ic % 2}")
                wait("vector", SEs, vupe)
                vtb = emit("vector", lambda e, g=gs, u=up, t=tb: e.tensor_tensor(
                    t[:], g[:], u[:], ALU.mult), inc=True)
                release(f"gs{ic % 2}", VEs, vtb)
                release(f"up{ic % 2}", VEs, vtb)
                wait("sync", VEs, vtb)
                vsp = dma("sync", tsp[ic * 128:(ic + 1) * 128, ct * FT:(ct + 1) * FT],
                          tb[:], sem=STm)
                release(f"tbS{ic % 2}", STm, vsp)
                vgm = emit("vector", lambda e, t=tb, ic=ic, ct=ct: e.reduce_max(
                    tmall[:, ic * CT + ct:ic * CT + ct + 1], t[:], AX.X,
                    apply_absolute_value=True), inc=True)
                release(f"tb{ic % 2}", VEs, vgm)
            release("xqb", PEs, PEs.n)
        v_spill_all = STm.n
        bar()
        for nm, tv in (("wgq", wgq_bf), ("wuq", wuq_bf), ("xqb", xqb), ("gsb", gs_b),
                       ("upb", up_b), ("tbb", tb_b), ("cbc", cbc_f), ("tmall", tmall),
                       ("cq22", cq22), ("cqT22", cqT22)):
            dbg(nm, tv)
        bar()

        emit("vector", lambda e: e.drain())
        for ic in range(IC):
            emit("vector", lambda e, ic=ic: e.reduce_max(
                tmaxr[:, ic:ic + 1], tmall[:, ic * CT:(ic + 1) * CT], AX.X), inc=True)
        emit("vector", lambda e: e.drain())
        emit("vector", lambda e: e.tensor_tensor(tmaxr[:], tmaxr[:], susx[:], ALU.mult), inc=True)
        emit("vector", lambda e: e.drain())
        vtm = emit("vector", lambda e: e.reduce_max(
            wrk[:, WK_TMX:WK_TMX + 1], tmaxr[:], AX.X), inc=True)
        emit("vector", lambda e: e.drain())
        wait("tensor", VEs, vtm)
        vcc2 = collapse_and_cc(WK_TMX, WR_TMX, cc2i, cc2o)
        vsgd = emit("scalar", lambda e: e.activation(
            wrow[0:1, WR_SIGD:WR_SIGD + 1], clip_v[0:1, 1:2], AF.Sigmoid), inc=True)
        wait("vector", GD, vcc2)
        wait("vector", SEs, vsgd)
        emit("vector", lambda e: e.tensor_tensor(
            wrow[0:1, WR_SD:WR_SD + 1], wrow[0:1, WR_SIGD:WR_SIGD + 1],
            ccr_sb[0:1, 0:1], ALU.mult), inc=True)
        emit("vector", lambda e: e.drain())
        emit("vector", lambda e: e.tensor_scalar(
            wrow[0:1, WR_SD:WR_SD + 1], wrow[0:1, WR_SD:WR_SD + 1],
            1.0 / 127.0, None, ALU.mult), inc=True)
        emit("vector", lambda e: e.drain())
        vrsd = emit("vector", lambda e: e.reciprocal(
            wrow[0:1, WR_RSD:WR_RSD + 1], wrow[0:1, WR_SD:WR_SD + 1]), inc=True)
        emit("vector", lambda e: e.drain())
        guard("tensor", "ps4")
        wait("tensor", VEs, vrsd)
        emit("tensor", lambda e: e.matmul(
            pb[4][:, 0:2], onesrow_f[0:1, :], wrow[0:1, WR_SD:WR_SD + 2],
            start=True, stop=True), inc=True)
        vbd = PEs.n
        wait("vector", PEs, vbd)
        vsd2 = emit("vector", lambda e: e.tensor_copy(
            wrk[:, WK_SDB:WK_SDB + 2], pb[4][:, 0:2]), inc=True)
        release("ps4", VEs, vsd2)
        wait("vector", VEs, vsd2)
        emit("vector", lambda e: e.drain())
        emit("vector", lambda e: e.tensor_scalar(
            qsc[:], susx[:], wrk[:, WK_RSD:WK_RSD + 1], None, ALU.mult), inc=True)
        emit("vector", lambda e: e.drain())
        bar()

        emit("vector", lambda e: e.drain())
        emit("vector", lambda e: e.reciprocal(cq8[:], rmxd[:]), inc=True)
        emit("vector", lambda e: e.drain())
        vcq8 = emit("vector", lambda e: e.tensor_scalar(
            cq8[:], cq8[:], 127.0, None, ALU.mult), inc=True)
        emit("vector", lambda e: e.drain())
        guard("tensor", "ps4")
        wait("tensor", VEs, vcq8)
        emit("tensor", lambda e: e.transpose(pb[4][0:KC, 0:128], cq8[:], id_f[:]), inc=True)
        vtp = PEs.n
        wait("vector", PEs, vtp)
        vct = emit("vector", lambda e: e.tensor_copy(cqT8[:, :], pb[4][0:KC, 0:128]), inc=True)
        release("ps4", VEs, vct)
        wait("sync", VEs, vct)
        vcr = dma("sync", crowd[0:1, :], cqT8[:, :])
        wait("tensor", LD, vcr)
        for j in range(2):
            bk = 4 + j
            guard("tensor", f"ps{bk}")
            vbc = emit("tensor", lambda e, j=j, bk=bk: e.matmul(
                pb[bk][:, 0:512], onesrow_f[0:1, :], crowd[0:1, j * 512:(j + 1) * 512],
                start=True, stop=True), inc=True)
            wait("vector", PEs, vbc)
            vcb2 = emit("vector", lambda e, j=j, bk=bk: e.tensor_copy(
                cbcd_f[:, j * 512:(j + 1) * 512], pb[bk][:, 0:512]), inc=True)
            release(f"ps{bk}", VEs, vcb2)
        vcbc = VEs.n
        wait("vector", VEs, vcbc)
        for kk in range(IC):
            guard("sync", "wtd")
            vld = dma("sync", wtd_f[:, 0:D], wdTd[kk * 128:(kk + 1) * 128, :])
            wait("vector", LD, vld)
            guard("vector", "wd8")
            v8 = emit("vector", lambda e: e.scalar_tensor_tensor(
                wd8b[:], wtd_f[:, 0:D], 1.0, cbcd_f[:], ALU.mult, ALU.mult), inc=True)
            release("wtd", VEs, v8)
            wait("scalar", VEs, v8)
            vq = emit("scalar", lambda e, kk=kk: e.activation(
                wdq_bf[:, kk * D:(kk + 1) * D], wd8b[:], AF.Identity), inc=True)
            release("wd8", SEs, vq)
        v_dq = SEs.n
        emit("vector", lambda e: e.drain())
        emit("vector", lambda e: e.tensor_scalar(
            sc8[:], rmxd[:], wrk[:, WK_SDB:WK_SDB + 1], 1.0 / 127.0,
            ALU.mult, ALU.mult), inc=True)
        emit("vector", lambda e: e.drain())
        for b_ in range(2):
            emit("vector", lambda e, b_=b_: e.tensor_tensor(
                comb[:, b_ * KC:(b_ + 1) * KC], gbp_v[:, b_ * KC:(b_ + 1) * KC],
                sc8[:], ALU.mult), inc=True)
        emit("vector", lambda e: e.drain())
        bar()

        for nm, tv in (("rmxg", rmxg), ("rmxu", rmxu), ("rmxd", rmxd), ("sgc", sgc_g),
                       ("susx", susx), ("qsc", qsc), ("comb", comb), ("tmaxr", tmaxr),
                       ("wrow2", wrow)):
            dbg(nm, tv)
        bar()
        wait("sync", STm, v_spill_all)
        wait("tensor", SEs, v_dq)
        for ct in range(CT):
            s, t = ct // NT, ct % NT
            tqb = tqb_a if ct % 2 == 0 else tqb_b
            guard("scalar", f"tqb{ct % 2}")
            for ic in range(IC):
                trd = trd_a if ic % 2 == 0 else trd_b
                tq8 = tq8_a if ic % 2 == 0 else tq8_b
                guard("sync", f"trd{ic % 2}")
                vld = dma("sync", trd[:], tsp[ic * 128:(ic + 1) * 128,
                                             ct * FT:(ct + 1) * FT])
                guard("scalar", f"tq8{ic % 2}")
                wait("scalar", LD, vld)
                v8 = emit("scalar", lambda e, ic=ic, tr=trd, t8=tq8: e.activation(
                    t8[:], tr[:], AF.Identity, scale=qsc[:, ic:ic + 1]), inc=True)
                release(f"trd{ic % 2}", SEs, v8)
                vq = emit("scalar", lambda e, ic=ic, t8=tq8, tq=tqb: e.activation(
                    tq[:, ic * FT:(ic + 1) * FT], t8[:], AF.Identity), inc=True)
                release(f"tq8{ic % 2}", SEs, vq)
            vtq = SEs.n
            wait("tensor", SEs, vtq)
            for oc in range(KC):
                bk = oc % 4
                xb = xfd_a if oc % 2 == 0 else xfd_b
                guard("sync", f"xfd{oc % 2}")
                vxf = dma("sync", xb[:], x2T[s * D + oc * 128:s * D + oc * 128 + 128,
                                             t * FT:(t + 1) * FT])
                guard("tensor", f"ps{bk}")
                for ic in range(IC):
                    emit("tensor", lambda e, oc=oc, ic=ic, bk=bk, tq=tqb: e.matmul(
                        pb[bk][:, 0:512],
                        wdq_bf[:, ic * D + oc * 128:ic * D + oc * 128 + 128],
                        tq[:, ic * FT:(ic + 1) * FT],
                        start=(ic == 0), stop=(ic == IC - 1)), inc=(ic == IC - 1))
                vdm = PEs.n
                of = of_a if oc % 2 == 0 else of_b
                guard("vector", f"of{oc % 2}")
                wait("vector", PEs, vdm)
                wait("vector", LD, vxf)
                vof = emit("vector", lambda e, s=s, oc=oc, bk=bk, xb=xb, of=of: e.scalar_tensor_tensor(
                    of[:], pb[bk][:, 0:512], comb[:, s * KC + oc:s * KC + oc + 1],
                    xb[:], ALU.mult, ALU.add), inc=True)
                release(f"ps{bk}", VEs, vof)
                release(f"xfd{oc % 2}", VEs, vof)
                wait("gpsimd", VEs, vof)
                vst = dma("gpsimd", out_d[s * D + oc * 128:s * D + oc * 128 + 128,
                                          t * FT:(t + 1) * FT], of[:], sem=STm)
                release(f"of{oc % 2}", STm, vst)
            release(f"tqb{ct % 2}", PEs, PEs.n)
        for nm, tv in (("wdq", wdq_bf), ("tqa", tqb_a), ("tqb2", tqb_b), ("cbcd", cbcd_f), ("cq8", cq8)):
            dbg(nm, tv)
        wait("sync", STm, STm.n)
        bar()


    try:
        _sched()
    except _StopSched:
        pass

    with nc.Block() as block:
        @block.sync
        def _(e):
            for f in prog["sync"]:
                f(e)

        @block.gpsimd
        def _(e):
            for f in prog["gpsimd"]:
                f(e)

        @block.vector
        def _(e):
            for f in prog["vector"]:
                f(e)

        @block.scalar
        def _(e):
            for f in prog["scalar"]:
                f(e)

        @block.tensor
        def _(e):
            for f in prog["tensor"]:
                f(e)

    es.close()
    return nc


_NC_CACHE = []


def kernel(L0_attn_x, L0_attn_w_qkv, L0_attn_b_qkv, L0_attn_w_g1, L0_attn_b_g1,
           L0_attn_w_g2, L0_attn_b_g2, L0_attn_w_out, L0_attn_b_out,
           L0_attn_ln_g, L0_attn_ln_b, L0_ffn_clip_x, L0_ffn_clip_down,
           L0_ffn_w_gate, L0_ffn_w_up, L0_ffn_w_down, L0_c, L0_wmod):
    f = np.float32
    cc = np.ascontiguousarray

    def pp(v):
        return cc(np.asarray(v, f).reshape(-1, 128).T)

    shared = {
        "wqkvT": cc(np.asarray(L0_attn_w_qkv, f).T),
        "woutT": cc(np.asarray(L0_attn_w_out, f).T),
        "wg1T": cc(np.asarray(L0_attn_w_g1, f).T),
        "wg2T": cc(np.asarray(L0_attn_w_g2, f).T),
        "wgT": cc(np.asarray(L0_ffn_w_gate, f).T),
        "wuT": cc(np.asarray(L0_ffn_w_up, f).T),
        "wdT": cc(np.asarray(L0_ffn_w_down, f).T),
        "wg_nat": cc(np.asarray(L0_ffn_w_gate, f)),
        "wu_nat": cc(np.asarray(L0_ffn_w_up, f)),
        "wd_nat": cc(np.asarray(L0_ffn_w_down, f)),
        "wmod": cc(np.asarray(L0_wmod, f)),
        "bq": pp(np.asarray(L0_attn_b_qkv, f)[0:D]),
        "bv_row": cc(np.asarray(L0_attn_b_qkv, f)[2 * D:3 * D].reshape(1, D)),
        "bg1": cc(np.asarray(L0_attn_b_g1, f).reshape(HD, 1)),
        "bg2": pp(L0_attn_b_g2),
        "bout": pp(L0_attn_b_out),
        "lng": pp(L0_attn_ln_g),
        "lnb": pp(L0_attn_ln_b),
        "clip2": cc(np.stack([np.asarray(L0_ffn_clip_x, f).reshape(()),
                              np.asarray(L0_ffn_clip_down, f).reshape(())]).reshape(1, 2)),
    }
    x = np.asarray(L0_attn_x, f)
    c_all = np.asarray(L0_c, f)
    in_maps = []
    for core in range(8):
        xs = cc(x[2 * core:2 * core + 2].transpose(0, 2, 1).reshape(B * D, S))
        cpair = c_all[2 * core:2 * core + 2]
        cT2 = cc(cpair.reshape(2, KC, 128).transpose(2, 1, 0).reshape(128, 2 * KC))
        m = dict(shared)
        m["xT"] = xs
        m["cT"] = cT2
        in_maps.append(m)

    if not _NC_CACHE:
        _NC_CACHE.append(build_nc())
    nc = _NC_CACHE[0]
    res = run_bass_kernel_spmd(nc, in_maps, core_ids=list(range(8)))
    outs = []
    for core in range(8):
        o = np.asarray(res.results[core]["out"])
        outs.append(o.reshape(B, D, S).transpose(0, 2, 1))
    return np.concatenate(outs, axis=0).astype(np.float32)


# version 19
# speedup vs baseline: 1.0150x; 1.0079x over previous
import numpy as np
import concourse.bass as bass
import concourse.mybir as mybir
from concourse.bass_utils import run_bass_kernel_spmd

F32 = mybir.dt.float32
BF16 = mybir.dt.bfloat16
I8 = mybir.dt.int8
AF = mybir.ActivationFunctionType
ALU = mybir.AluOpType
AX = mybir.AxisListType

B = 2
S = 2048
D = 1024
KC = 8
NH = 16
HD = 64
DI = 2816
IC = 22
FT = 512
NT = S // FT
KT = S // 128
CT = B * NT

ENGINES = ("sync", "gpsimd", "vector", "scalar", "tensor")


class Sem:
    def __init__(self, h):
        self.h = h
        self.n = 0

    def inc(self, k=1):
        self.n += k
        return self.n


def build_nc():
    nc = bass.Bass("TRN2", target_bir_lowering=False)

    DT = lambda n, sh, dt=F32, kind="ExternalInput": nc.dram_tensor(n, sh, dt, kind=kind)
    xT = DT("xT", [B * D, S])
    wqkvT = DT("wqkvT", [D, 3 * D])
    woutT = DT("woutT", [D, D])
    wg1T = DT("wg1T", [D, HD])
    wg2T = DT("wg2T", [HD, D])
    wgTd = DT("wgT", [D, DI])
    wuTd = DT("wuT", [D, DI])
    wdTd = DT("wdT", [DI, D])
    wg_nat = DT("wg_nat", [DI, D])
    wu_nat = DT("wu_nat", [DI, D])
    wd_nat = DT("wd_nat", [D, DI])
    wmodd = DT("wmod", [D, 6 * D])
    cTd = DT("cT", [128, 2 * KC])
    bq_d = DT("bq", [128, KC])
    bv_row_d = DT("bv_row", [1, D])
    bg1_d = DT("bg1", [HD, 1])
    bg2_d = DT("bg2", [128, KC])
    bout_d = DT("bout", [128, KC])
    lng_d = DT("lng", [128, KC])
    lnb_d = DT("lnb", [128, KC])
    clip_d = DT("clip2", [1, 2])
    import os as _os0
    KDBG = bool(int(_os0.environ.get("KDBG", "0")))
    _K1 = "ExternalOutput" if KDBG else "Internal"
    out_d = DT("out", [B * D, S], F32, "ExternalOutput")
    x2T = DT("x2T", [B * D, S], F32, _K1)
    tsp = DT("tsp", [DI, B * S], BF16, _K1)
    cc1i = DT("cc1i", [1, 16], F32, "Internal")
    cc1o = DT("cc1o", [1, 16], F32, "Internal")
    cc2i = DT("cc2i", [1, 16], F32, "Internal")
    cc2o = DT("cc2o", [1, 16], F32, "Internal")

    id_dram = nc.inline_tensor(np.eye(128, dtype=np.float32), "ident")
    onescol_dram = nc.inline_tensor(np.ones((128, 1), np.float32), "onescol")
    onesrow_dram = nc.inline_tensor(np.ones((1, 128), np.float32), "onesrow")

    A = lambda name, shape, dt=F32: nc.alloc_sbuf_tensor(name, shape, dt)
    id_f = A("id_f", [128, 128])
    onescol_bf = A("onescol_bf", [128, 1], BF16)
    onesrow_f = A("onesrow_f", [1, 128])
    onesrow_bf = A("onesrow_bf", [1, 128], BF16)
    mod6 = A("mod6", [128, 6 * 2 * KC])
    sa_v = A("sa_v", [128, 2 * KC])
    ca1_v = A("ca1_v", [128, 2 * KC])
    gap_v = A("gap_v", [128, 2 * KC])
    sb_v = A("sb_v", [128, 2 * KC])
    cb1_v = A("cb1_v", [128, 2 * KC])
    gbp_v = A("gbp_v", [128, 2 * KC])
    gabo_v = A("gabo_v", [128, 2 * KC])
    sc16a = A("sc16a", [128, 2 * KC])
    sc16b = A("sc16b", [128, 2 * KC])
    bq_v = A("bq_v", [128, KC])
    bg2_v = A("bg2_v", [128, KC])
    bout_v = A("bout_v", [128, KC])
    lng_v = A("lng_v", [128, KC])
    lnb_v = A("lnb_v", [128, KC])
    bg1_v = A("bg1_v", [HD, 1])
    c_sb = A("c_sb", [128, 2 * KC])
    c_bf = A("c_bf", [128, 2 * KC], BF16)
    clip_v = A("clip_v", [1, 2])
    bvbc = A("bvbc", [128, D])
    wrk = A("wrk", [128, 16])
    wrow = A("wrow", [1, 32])
    den_v = A("den_v", [128, 2])
    rmxg = A("rmxg", [128, IC])
    rmxu = A("rmxu", [128, IC])
    rmxd = A("rmxd", [128, KC])
    tmall = A("tmall", [128, IC * CT])
    sgc_g = A("sgc_g", [128, IC])
    susx = A("susx", [128, IC])
    qsc = A("qsc", [128, IC])
    comb = A("comb", [128, 2 * KC])
    tmaxr = A("tmaxr", [128, IC])
    cq22 = A("cq22", [128, IC])
    cq8 = A("cq8", [128, KC])
    sc8 = A("sc8", [128, KC])
    ccs_sb = A("ccs_sb", [1, 16])
    cv = A("cv", [1, 4])
    ccr_sb = A("ccr_sb", [1, 16])

    WK_MRUN, WK_NEGM, WK_HMAX, WK_TMP, WK_SXB, WK_RSX, WK_SDB, WK_RSD, WK_TMX = range(9)
    WR_M, WR_NEGM, WR_HMAX, WR_SIG, WR_SX, WR_RSX, WR_TMX, WR_SIGD, WR_SD, WR_RSD = range(10)

    X0 = (nc.sbuf_base + 31) & ~31
    TOP = nc.sbuf_top

    def AT(name, shape, dt, off):
        assert off % 32 == 0, name
        nbytes = int(np.prod(shape[1:])) * mybir.dt.size(dt)
        assert X0 <= off and off + nbytes <= TOP, (name, off, nbytes, X0, TOP)
        return nc.alloc_sbuf_tensor_at(name, shape, dt, offset=off)

    K = 1024
    Z = X0
    t1a = AT("t1a", [128, FT], BF16, Z + 0 * K)
    t1b = AT("t1b", [128, FT], BF16, Z + 1 * K)
    t2a = AT("t2a", [128, FT], BF16, Z + 2 * K)
    t2b = AT("t2b", [128, FT], BF16, Z + 3 * K)
    sqa = AT("sqa", [128, FT], BF16, Z + 4 * K)
    sqb = AT("sqb", [128, FT], BF16, Z + 5 * K)
    rb_a = AT("rb_a", [128, FT], BF16, Z + 6 * K)
    mb_a = AT("mb_a", [128, FT], BF16, Z + 7 * K)
    eka = AT("eka", [128, D], BF16, Z + 8 * K)
    ekb = AT("ekb", [128, D], BF16, Z + 10 * K)
    muT = AT("muT", [1, FT], F32, Z + 12 * K)
    e2T = AT("e2T", [1, FT], F32, Z + 14 * K)
    tmpT = AT("tmpT", [1, FT], F32, Z + 16 * K)
    varT = AT("varT", [1, FT], F32, Z + 18 * K)
    rT = AT("rT", [1, FT], F32, Z + 20 * K)
    mrT = AT("mrT", [1, FT], F32, Z + 22 * K)
    rT_bf = AT("rT_bf", [1, FT], BF16, Z + 24 * K)
    mrT_bf = AT("mrT_bf", [1, FT], BF16, Z + 25 * K)
    bvrow_sb = AT("bvrow_sb", [1, D], F32, Z + 26 * K)

    Z1 = Z + 31 * K

    wkv_bf = AT("wkv_bf", [128, KC * 2 * D], BF16, Z1)
    wq_bf = AT("wq_bf", [128, KC * D], BF16, Z1)
    wout_bf = AT("wout_bf", [128, KC * D], BF16, Z1 + 32 * K)
    wg1_bf = AT("wg1_bf", [128, KC * HD], BF16, Z1 + 48 * K)
    wg2_bf = AT("wg2_bf", [HD, D], BF16, Z1 + 49 * K)
    xh_bf = AT("xh_bf", [128, KC * S], BF16, Z1 + 52 * K)
    R1 = Z1 + 84 * K
    k_bf = AT("k_bf", [128, KT * D], BF16, R1)
    q_bf = AT("q_bf", [128, KC * FT], BF16, R1)
    y_bf = AT("y_bf", [128, KC * FT], BF16, R1 + 8 * K)
    z_bf = AT("z_bf", [128, KC * FT], BF16, R1 + 16 * K)
    gate_bf = AT("gate_bf", [128, KC * FT], BF16, R1 + 24 * K)
    R2 = Z1 + 116 * K
    v_sb = AT("v_sb", [128, KT * NH * 65], BF16, R2)
    e_f = AT("e_f", [128, FT], F32, R2)
    xf_a = AT("xf_a", [128, FT], F32, R2 + 2 * K)
    xf_b = AT("xf_b", [128, FT], F32, R2 + 4 * K)
    x2f_a = AT("x2f_a", [128, FT], F32, R2 + 6 * K)
    x2f_b = AT("x2f_b", [128, FT], F32, R2 + 8 * K)
    gh_bf = AT("gh_bf", [HD, FT], BF16, R2 + 10 * K)
    kvtmp = AT("kvtmp", [128, HD], F32, R2 + 12 * K)
    kv_bfs = AT("kv_bfs", [128, 8 * HD], BF16, R2 + 13 * K)
    assert R2 + 34 * K <= TOP

    wmod_bf = AT("wmod_bf", [128, KC * 6 * D], BF16, Z1)
    wm32 = AT("wm32", [128, KC * 128], F32, Z1 + 97 * K)
    wm32b = AT("wm32b", [128, KC * 1024], F32, Z1)

    x2t_bf = AT("x2t_bf", [128, KC * FT], BF16, Z1)
    x2t_bf2 = AT("x2t_bf2", [128, KC * FT], BF16, Z1 + 8 * K)
    xq_i8 = AT("xq_i8", [128, KC * B * S], I8, Z1)
    h2_bf = AT("h2_bf", [128, KC * B * S], BF16, Z1 + 33 * K)

    wgq_bf = AT("wgq_bf", [128, KC * DI], BF16, Z1 + 33 * K)
    wuq_bf = AT("wuq_bf", [128, KC * DI], BF16, Z1 + 77 * K)
    cbc_f = AT("cbc_f", [128, DI], F32, Z1 + 121 * K)
    wtf_f = AT("wtf_f", [128, DI], F32, Z1 + 132 * K)
    crow = AT("crow", [1, DI], F32, Z1 + 132 * K)
    cqT22 = AT("cqT22", [IC, 128], F32, Z1 + 143 * K)
    wq8b = AT("wq8b", [128, DI], I8, Z1 + 144 * K)
    natq = AT("natq", [128, D], F32, Z1 + 147 * K)
    xqb = AT("xqb", [128, KC * FT], BF16, Z1 + 151 * K)
    gs_a = AT("gs_a", [128, FT], BF16, Z1 + 159 * K)
    gs_b = AT("gs_b", [128, FT], BF16, Z1 + 160 * K)
    up_a = AT("up_a", [128, FT], BF16, Z1 + 161 * K)
    up_b = AT("up_b", [128, FT], BF16, Z1 + 162 * K)
    tb_a = AT("tb_a", [128, FT], BF16, Z1 + 163 * K)
    tb_b = AT("tb_b", [128, FT], BF16, Z1 + 164 * K)
    assert Z1 + 165 * K <= TOP, (Z1 + 165 * K, TOP)

    wdq_bf = AT("wdq_bf", [128, IC * D], BF16, Z1)
    tqb_a = AT("tqb_a", [128, IC * FT], BF16, Z1 + 44 * K)
    tqb_b = AT("tqb_b", [128, IC * FT], BF16, Z1 + 66 * K)
    cbcd_f = AT("cbcd_f", [128, D], F32, Z1 + 88 * K)
    wtd_f = AT("wtd_f", [128, DI], F32, Z1 + 92 * K)
    natd = AT("natd", [128, DI], F32, Z1 + 103 * K)
    crowd = AT("crowd", [1, D], F32, Z1 + 114 * K)
    cqT8 = AT("cqT8", [KC, 128], F32, Z1 + 118 * K)
    wd8b = AT("wd8b", [128, D], I8, Z1 + 119 * K)
    tq8_a = AT("tq8_a", [128, FT], I8, Z1 + 120 * K)
    tq8_b = AT("tq8_b", [128, FT], I8, Z1 + 121 * K)
    of_a = AT("of_a", [128, FT], F32, Z1 + 122 * K)
    of_b = AT("of_b", [128, FT], F32, Z1 + 124 * K)
    trd_a = AT("trd_a", [128, FT], BF16, Z1 + 126 * K)
    trd_b = AT("trd_b", [128, FT], BF16, Z1 + 127 * K)
    dtmp_a = AT("dtmp_a", [128, FT], F32, Z1 + 128 * K)
    dtmp_b = AT("dtmp_b", [128, FT], F32, Z1 + 130 * K)
    xfd_a = AT("xfd_a", [128, FT], F32, Z1 + 132 * K)
    xfd_b = AT("xfd_b", [128, FT], F32, Z1 + 134 * K)

    pb = [nc.alloc_psum_tensor(f"pb{i}", [128, 512], F32) for i in range(8)]

    import contextlib
    es = contextlib.ExitStack()
    SEM = lambda n: Sem(es.enter_context(nc.semaphore(n)))
    LD, GD, STm = SEM("LD"), SEM("GD"), SEM("STm")
    PEs, VEs, SEs, GEs = SEM("PEs"), SEM("VEs"), SEM("SEs"), SEM("GEs")
    CCs, PHB, PH3 = SEM("CCs"), SEM("PHB"), SEM("PH3")
    ESEM = {"vector": VEs, "scalar": SEs, "tensor": PEs, "gpsimd": GEs,
            "sync": None}
    DSEM = {"sync": LD, "gpsimd": GD}

    prog = {en: [] for en in ENGINES}

    def E(en, f):
        prog[en].append(f)

    def emit(en, f, inc=False):
        if not inc:
            E(en, f)
            return None
        s = ESEM[en]
        v = s.inc(1)
        E(en, lambda e, ff=f, hh=s.h: ff(e).then_inc(hh, 1))
        return v

    def wait(en, sem, val):
        if val is None:
            return
        E(en, lambda e, hh=sem.h, v=val: e.wait_ge(hh, v))

    def dma(queue, out, in_, sem=None):
        s = sem if sem is not None else DSEM[queue]
        v = s.inc(16)
        E(queue, lambda e, o=out, i=in_, hh=s.h: e.dma_start(out=o, in_=i).then_inc(hh, 16))
        return v

    def bar():
        tgt = PHB.inc(5)
        for en in ENGINES:
            E(en, lambda e, hh=PHB.h: e.nop().then_inc(hh, 1))
            E(en, lambda e, hh=PHB.h, t=tgt: e.wait_ge(hh, t))

    def bar3():
        tgt = PH3.inc(3)
        for en in ("vector", "scalar", "tensor"):
            E(en, lambda e, hh=PH3.h: e.nop().then_inc(hh, 1))
            E(en, lambda e, hh=PH3.h, t=tgt: e.wait_ge(hh, t))

    freemap = {}

    def guard(en, name):
        if name in freemap:
            s, v = freemap.pop(name)
            wait(en, s, v)

    def release(name, sem, val):
        freemap[name] = (sem, val)

    def dbg(name, tens):
        if not KDBG:
            return
        sh = list(tens.shape)
        d = nc.dram_tensor("dbg_" + name, sh, tens.dtype, kind="ExternalOutput")
        dma("gpsimd", d[:, :] if len(sh) == 2 else d[:], tens[:])

    import os as _os
    _klim = int(_os.environ.get("KPHASE", "999"))
    class _StopSched(Exception):
        pass
    _phctr = [0]
    _orig_bar = bar
    def bar():
        _orig_bar()
        _phctr[0] += 1
        if _phctr[0] >= _klim:
            raise _StopSched()

    def _sched():
        def qw_rowmax(nat_d, rmx, oc_n, w_in, bufs):
            for oc in range(oc_n):
                nb, key = bufs[oc % len(bufs)]
                guard("sync", key)
                vld = dma("sync", nb[:, 0:w_in], nat_d[oc * 128:(oc + 1) * 128, :])
                wait("vector", LD, vld)
                vr = emit("vector", lambda e, oc=oc, nb=nb, rm=rmx, w_in=w_in: e.reduce_max(
                    rm[:, oc:oc + 1], nb[:, 0:w_in], AX.X, apply_absolute_value=True), inc=True)
                release(key, VEs, vr)

        dma("sync", id_f[:], id_dram[:, :])
        dma("sync", c_sb[:], cTd[:, :])
        dma("sync", bq_v[:], bq_d[:, :])
        dma("sync", bg2_v[:], bg2_d[:, :])
        dma("sync", bout_v[:], bout_d[:, :])
        dma("sync", lng_v[:], lng_d[:, :])
        dma("sync", lnb_v[:], lnb_d[:, :])
        dma("sync", bg1_v[:], bg1_d[:, :])
        dma("sync", clip_v[:], clip_d[:, :])
        dma("sync", onesrow_f[:], onesrow_dram[:, :])
        dma("sync", bvrow_sb[:], bv_row_d[:, :])
        dma("gpsimd", onescol_bf[:], onescol_dram[:, :])
        dma("gpsimd", onesrow_bf[:], onesrow_dram[:, :])
        for kk in range(KC):
            dma("gpsimd", wmod_bf[:, kk * 6 * D:(kk + 1) * 6 * D],
                wmodd[kk * 128:(kk + 1) * 128, :])
        v_ld0, v_gd0 = LD.n, GD.n
        bar()
        for en in ("vector", "scalar", "tensor"):
            wait(en, LD, v_ld0)
            wait(en, GD, v_gd0)

        emit("vector", lambda e: e.memset(ccs_sb[:], 0.0), inc=True)
        emit("vector", lambda e: e.memset(cv[0:1, 0:1], 1.0 / D), inc=True)
        emit("vector", lambda e: e.memset(cv[0:1, 1:2], 1e-5), inc=True)
        vcv = emit("vector", lambda e: e.memset(cv[0:1, 2:3], -1.0), inc=True)
        vsc = emit("scalar", lambda e: e.activation(c_sb[:], c_sb[:], AF.Silu), inc=True)
        wait("vector", SEs, vsc)
        vcb = emit("vector", lambda e: e.tensor_copy(c_bf[:], c_sb[:]), inc=True)
        for j in range(2):
            emit("tensor", lambda e, j=j: e.matmul(
                pb[2 + j][:, 0:512], onesrow_f[0:1, :],
                bvrow_sb[0:1, j * 512:(j + 1) * 512], start=True, stop=True), inc=True)
        vpe_bv = PEs.n
        wait("vector", PEs, vpe_bv)
        for j in range(2):
            vv = emit("vector", lambda e, j=j: e.tensor_copy(
                bvbc[:, j * 512:(j + 1) * 512], pb[2 + j][:, 0:512]), inc=True)
            release(f"ps{2 + j}", VEs, vv)
        wait("tensor", VEs, vcb)
        for mc in [m for m in range(48) if not (16 <= m < 24 or 40 <= m < 48)]:
            bk = mc % 2
            guard("tensor", f"ps{bk}")
            for kk in range(KC):
                emit("tensor", lambda e, mc=mc, kk=kk, bk=bk: e.matmul(
                    pb[bk][:, 0:2],
                    wmod_bf[:, kk * 6 * D + mc * 128:kk * 6 * D + mc * 128 + 128],
                    c_bf[:, kk * 2:kk * 2 + 2],
                    start=(kk == 0), stop=(kk == KC - 1)), inc=(kk == KC - 1))
            vmm = PEs.n
            wait("vector", PEs, vmm)
            Vv, ff = mc // 8, mc % 8
            for b_ in range(2):
                vv = emit("vector", lambda e, bk=bk, b_=b_, Vv=Vv, ff=ff: e.tensor_copy(
                    mod6[:, Vv * 16 + b_ * 8 + ff:Vv * 16 + b_ * 8 + ff + 1],
                    pb[bk][:, b_:b_ + 1]), inc=True)
            release(f"ps{bk}", VEs, vv)
        release("wm32b", PEs, PEs.n)
        for blk, mc0 in enumerate((16, 40)):
            vld = 0
            for kk in range(KC):
                if kk == 0:
                    guard("sync", "wm32b")
                vld = dma("sync", wm32b[:, kk * 1024:(kk + 1) * 1024],
                          wmodd[kk * 128:(kk + 1) * 128, mc0 * 128:(mc0 + 8) * 128])
            wait("tensor", LD, vld)
            for j in range(8):
                mc = mc0 + j
                bk = j % 2
                guard("tensor", f"ps{bk}")
                for kk in range(KC):
                    emit("tensor", lambda e, kk=kk, bk=bk, j=j: e.matmul(
                        pb[bk][:, 0:2],
                        wm32b[:, kk * 1024 + j * 128:kk * 1024 + j * 128 + 128],
                        c_sb[:, kk * 2:kk * 2 + 2],
                        start=(kk == 0), stop=(kk == KC - 1)), inc=(kk == KC - 1))
                vmm = PEs.n
                wait("vector", PEs, vmm)
                Vv, ff = mc // 8, mc % 8
                for b_ in range(2):
                    vv = emit("vector", lambda e, bk=bk, b_=b_, Vv=Vv, ff=ff: e.tensor_copy(
                        mod6[:, Vv * 16 + b_ * 8 + ff:Vv * 16 + b_ * 8 + ff + 1],
                        pb[bk][:, b_:b_ + 1]), inc=True)
                release(f"ps{bk}", VEs, vv)
            release("wm32b", PEs, PEs.n)
        emit("vector", lambda e: e.drain())
        emit("vector", lambda e: e.tensor_copy(sa_v[:], mod6[:, 0:16]), inc=True)
        emit("vector", lambda e: e.tensor_scalar(ca1_v[:], mod6[:, 16:32], 1.0, None, ALU.add), inc=True)
        emit("vector", lambda e: e.tensor_copy(sb_v[:], mod6[:, 48:64]), inc=True)
        emit("vector", lambda e: e.tensor_scalar(cb1_v[:], mod6[:, 64:80], 1.0, None, ALU.add), inc=True)
        for (src0, dstv) in ((32, gap_v), (80, gbp_v)):
            emit("vector", lambda e, s0=src0: e.tensor_single_scalar(
                sc16a[:], mod6[:, s0:s0 + 16], 0.0, ALU.is_ge), inc=True)
            emit("vector", lambda e: e.drain())
            emit("vector", lambda e: e.tensor_scalar(
                sc16b[:], sc16a[:], 2.0, -1.0, ALU.mult, ALU.add), inc=True)
            emit("vector", lambda e, s0=src0: e.tensor_scalar(
                sc16a[:], mod6[:, s0:s0 + 16], -1.0, 0.25, ALU.mult, ALU.max), inc=True)
            emit("vector", lambda e: e.drain())
            emit("vector", lambda e, s0=src0: e.tensor_tensor(
                sc16a[:], sc16a[:], mod6[:, s0:s0 + 16], ALU.max), inc=True)
            emit("vector", lambda e: e.drain())
            emit("vector", lambda e, d=dstv: e.tensor_tensor(
                d[:], sc16a[:], sc16b[:], ALU.mult), inc=True)
            emit("vector", lambda e: e.drain())
        for b_ in range(2):
            emit("vector", lambda e, b_=b_: e.tensor_tensor(
                gabo_v[:, b_ * 8:b_ * 8 + 8], gap_v[:, b_ * 8:b_ * 8 + 8], bout_v[:],
                ALU.mult), inc=True)
        emit("vector", lambda e: e.drain())
        bar()
        for nm, tv in (("mod6", mod6), ("sa", sa_v), ("ca1", ca1_v), ("gap", gap_v),
                       ("sb", sb_v), ("cb1", cb1_v), ("gbp", gbp_v), ("gabo", gabo_v),
                       ("csilu", c_sb), ("bvbc", bvbc)):
            dbg(nm, tv)
        bar()

        for kk in range(KC):
            dma("gpsimd", wkv_bf[:, kk * 2 * D:(kk + 1) * 2 * D],
                wqkvT[kk * 128:(kk + 1) * 128, D:3 * D])
        for kk in range(KC):
            dma("gpsimd", wout_bf[:, kk * D:(kk + 1) * D], woutT[kk * 128:(kk + 1) * 128, :])
        for kk in range(KC):
            dma("gpsimd", wg1_bf[:, kk * HD:(kk + 1) * HD], wg1T[kk * 128:(kk + 1) * 128, :])
        v_wload = dma("gpsimd", wg2_bf[:], wg2T[:, :])

        def ln_tile(srcs, out_writer, chunk_wait=None):
            guard("tensor", "ps0")
            guard("tensor", "ps1")
            for c in range(KC):
                sq = sqa if c % 2 == 0 else sqb
                guard("scalar", f"sq{c % 2}")
                if chunk_wait is not None:
                    w = chunk_wait(c)
                    if w is not None:
                        wait("scalar", *w)
                        wait("vector", *w)
                vsq = emit("scalar", lambda e, s=srcs[c], q=sq: e.activation(
                    q[:], s, AF.Square), inc=True)
                wait("tensor", SEs, vsq)
                emit("tensor", lambda e, s=srcs[c], c=c: e.matmul(
                    pb[0][0:1, 0:512], onescol_bf[:, 0:1], s,
                    start=(c == 0), stop=(c == KC - 1)), inc=(c == KC - 1))
                vq = emit("tensor", lambda e, q=sq, c=c: e.matmul(
                    pb[1][0:1, 0:512], onescol_bf[:, 0:1], q[:],
                    start=(c == 0), stop=(c == KC - 1)), inc=True)
                release(f"sq{c % 2}", PEs, vq)
            vstat = PEs.n
            wait("scalar", PEs, vstat)
            emit("scalar", lambda e: e.activation(muT[:], pb[0][0:1, 0:512], AF.Identity,
                                                  scale=cv[0:1, 0:1]), inc=True)
            v1 = emit("scalar", lambda e: e.activation(e2T[:], pb[1][0:1, 0:512], AF.Identity,
                                                       scale=cv[0:1, 0:1]), inc=True)
            release("ps0", SEs, v1)
            release("ps1", SEs, v1)
            wait("vector", SEs, v1)
            emit("vector", lambda e: e.tensor_tensor(tmpT[:], muT[:], muT[:], ALU.mult), inc=True)
            v2 = emit("vector", lambda e: e.tensor_tensor(varT[:], e2T[:], tmpT[:], ALU.subtract), inc=True)
            wait("scalar", VEs, v2)
            v3 = emit("scalar", lambda e: e.activation(tmpT[:], varT[:], AF.Sqrt,
                                                       bias=cv[0:1, 1:2]), inc=True)
            wait("vector", SEs, v3)
            emit("vector", lambda e: e.reciprocal(rT[:], tmpT[:]), inc=True)
            emit("vector", lambda e: e.tensor_tensor(mrT[:], muT[:], rT[:], ALU.mult), inc=True)
            guard("vector", "rtbf")
            emit("vector", lambda e: e.tensor_copy(rT_bf[:], rT[:]), inc=True)
            v4 = emit("vector", lambda e: e.tensor_copy(mrT_bf[:], mrT[:]), inc=True)
            guard("tensor", "ps2")
            guard("tensor", "ps3")
            wait("tensor", VEs, v4)
            emit("tensor", lambda e: e.matmul(pb[2][:, 0:512], onesrow_bf[0:1, :], rT_bf[0:1, :],
                                              start=True, stop=True), inc=True)
            v5 = emit("tensor", lambda e: e.matmul(pb[3][:, 0:512], onesrow_bf[0:1, :], mrT_bf[0:1, :],
                                                   start=True, stop=True), inc=True)
            release("rtbf", PEs, v5)
            wait("vector", PEs, v5)
            guard("vector", "rbmb")
            emit("vector", lambda e: e.tensor_copy(rb_a[:], pb[2][:, 0:512]), inc=True)
            v6 = emit("vector", lambda e: e.tensor_copy(mb_a[:], pb[3][:, 0:512]), inc=True)
            release("ps2", VEs, v6)
            release("ps3", VEs, v6)
            lastv = None
            for c in range(KC):
                t1 = t1a if c % 2 == 0 else t1b
                t2 = t2a if c % 2 == 0 else t2b
                guard("vector", f"t12{c % 2}")
                emit("vector", lambda e, s=srcs[c], t=t1: e.tensor_tensor(
                    t[:], s, rb_a[:], ALU.mult), inc=True)
                v7 = emit("vector", lambda e, t=t1, u=t2: e.tensor_tensor(
                    u[:], t[:], mb_a[:], ALU.subtract), inc=True)
                en, fn = out_writer(c, t2)
                wait(en, VEs, v7)
                v8 = emit(en, fn, inc=True)
                release(f"t12{c % 2}", ESEM[en], v8)
                lastv = (ESEM[en], v8)
            release("rbmb", *lastv)
            bar3()
            return lastv

        def xh_sl(c, t0, n):
            return xh_bf[:, c * S + t0:c * S + t0 + n]

        v_x2store = None
        for s in range(B):
            guard("gpsimd", "xh")
            xdmav = []
            for c in range(KC):
                xdmav.append(dma("gpsimd", xh_bf[:, c * S:(c + 1) * S],
                                 xT[s * D + c * 128:s * D + (c + 1) * 128, :]))
            if s == 1:
                guard("gpsimd", "Wreg")
                for kk in range(KC):
                    dma("gpsimd", wkv_bf[:, kk * 2 * D:(kk + 1) * 2 * D],
                        wqkvT[kk * 128:(kk + 1) * 128, D:3 * D])
            v_wkv = GD.n if s == 1 else v_wload
            emit("vector", lambda e: e.memset(wrk[:, WK_MRUN:WK_MRUN + 1], -1e30), inc=True)
            for _k in ("x2f0", "x2f1", "xf0", "xf1", "e_f"):
                guard("vector", _k)
            emit("vector", lambda e: e.memset(
                v_sb[:, :].rearrange("p (x e) -> p x e", e=65)[:, :, 64:65], 1.0), inc=True)
            wait("tensor", GD, v_wkv)

            def emit_kv(kt):
                for half, bks in ((0, (4, 5)), (1, (6, 7))):
                    for ncnk in range(2):
                        bk = bks[ncnk]
                        guard("tensor", f"ps{bk}")
                        for c in range(KC):
                            emit("tensor", lambda e, kt=kt, c=c, h=half, ncnk=ncnk, bk=bk: e.matmul(
                                pb[bk][:, 0:512],
                                xh_bf[:, c * S + kt * 128:c * S + kt * 128 + 128],
                                wkv_bf[:, c * 2 * D + h * D + ncnk * 512:
                                       c * 2 * D + h * D + ncnk * 512 + 512],
                                start=(c == 0), stop=(c == KC - 1)),
                                inc=(c == KC - 1))
                vkv = PEs.n
                wait("vector", PEs, vkv)
                for j, bk in enumerate((4, 5)):
                    vv = emit("vector", lambda e, kt=kt, j=j, bk=bk: e.tensor_copy(
                        k_bf[:, kt * D + j * 512:kt * D + j * 512 + 512],
                        pb[bk][:, 0:512]), inc=True)
                    release(f"ps{bk}", VEs, vv)
                for j, bk in enumerate((6, 7)):
                    vv = emit("vector", lambda e, kt=kt, j=j, bk=bk: e.tensor_copy(
                        v_sb[:, kt * 1040 + j * 520:kt * 1040 + (j + 1) * 520]
                            .rearrange("p (h e) -> p h e", e=65)[:, :, 0:64],
                        pb[bk][:, 0:512].rearrange("p (h e) -> p h e", e=64)), inc=True)
                    release(f"ps{bk}", VEs, vv)
                emit("vector", lambda e, kt=kt: e.reduce_max(
                    wrk[:, WK_TMP:WK_TMP + 1], k_bf[:, kt * D:(kt + 1) * D],
                    AX.X), inc=True)
                emit("vector", lambda e: e.drain())
                emit("vector", lambda e: e.tensor_tensor(
                    wrk[:, WK_MRUN:WK_MRUN + 1], wrk[:, WK_MRUN:WK_MRUN + 1],
                    wrk[:, WK_TMP:WK_TMP + 1], ALU.max), inc=True)
                emit("vector", lambda e: e.drain())

            for t in range(NT):
                def w_h(s=s, t=t):
                    def w(c, t2buf):
                        return ("scalar", lambda e, c=c, u=t2buf: e.activation(
                            xh_sl(c, t * FT, FT), u[:], AF.Identity,
                            scale=ca1_v[:, s * KC + c:s * KC + c + 1],
                            bias=sa_v[:, s * KC + c:s * KC + c + 1]))
                    return w
                ln_tile([xh_sl(c, t * FT, FT) for c in range(KC)], w_h(),
                        chunk_wait=lambda c, vs=tuple(xdmav): (GD, vs[c]))
                for kt in range(4 * t, 4 * t + 4):
                    emit_kv(kt)
            bar()
            if s == 0:
                dbg("h_s0", xh_bf)
                bar()

            bar()
            if s == 0:
                dbg("k_s0", k_bf)
                dbg("v_s0", v_sb)
                bar()

            emit("tensor", lambda e: e.transpose(
                pb[4][0:1, 0:128], wrk[:, WK_MRUN:WK_MRUN + 1], id_f[:]), inc=True)
            vt = PEs.n
            wait("vector", PEs, vt)
            vv = emit("vector", lambda e: e.reduce_max(
                wrow[0:1, WR_M:WR_M + 1], pb[4][0:1, 0:128], AX.X), inc=True)
            wait("scalar", VEs, vv)
            vsc = emit("scalar", lambda e: e.activation(
                wrow[0:1, WR_NEGM:WR_NEGM + 1], wrow[0:1, WR_M:WR_M + 1],
                AF.Identity, scale=cv[0:1, 2:3]), inc=True)
            wait("tensor", SEs, vsc)
            emit("tensor", lambda e: e.matmul(
                pb[4][:, 0:1], onesrow_f[0:1, :], wrow[0:1, WR_NEGM:WR_NEGM + 1],
                start=True, stop=True), inc=True)
            vt2 = PEs.n
            wait("vector", PEs, vt2)
            vnm = emit("vector", lambda e: e.tensor_copy(
                wrk[:, WK_NEGM:WK_NEGM + 1], pb[4][:, 0:1]), inc=True)
            bar()

            for kt in range(KT):
                ek = eka if kt % 2 == 0 else ekb
                guard("scalar", f"ek{kt % 2}")
                vek = emit("scalar", lambda e, kt=kt, k=ek: e.activation(
                    k[:], k_bf[:, kt * D:(kt + 1) * D], AF.Exp,
                    bias=wrk[:, WK_NEGM:WK_NEGM + 1]), inc=True)
                wait("tensor", SEs, vek)
                for p in range(8):
                    if kt == 0:
                        guard("tensor", f"ps{p}")
                    emit("tensor", lambda e, kt=kt, p=p, k=ek: e.matmul(
                        pb[p][:, 0:130],
                        k[:, p * 128:(p + 1) * 128],
                        v_sb[:, kt * 1040 + p * 130:kt * 1040 + (p + 1) * 130],
                        start=(kt == 0), stop=(kt == KT - 1)),
                        inc=(p == 7))
                release(f"ek{kt % 2}", PEs, PEs.n)
            vkvacc = PEs.n
            wait("vector", PEs, vkvacc)
            guard("vector", "kv_bfs")
            for p in range(8):
                emit("vector", lambda e, p=p: e.tensor_copy(
                    den_v[0:64, 0:1], pb[p][0:64, 64:65]), inc=True)
                emit("vector", lambda e, p=p: e.tensor_copy(
                    den_v[64:128, 0:1], pb[p][64:128, 129:130]), inc=True)
                emit("vector", lambda e: e.drain())
                emit("vector", lambda e, p=p: e.scalar_tensor_tensor(
                    kvtmp[0:64, :], bvbc[0:64, p * 128:p * 128 + 64],
                    den_v[0:64, 0:1], pb[p][0:64, 0:64], ALU.mult, ALU.add), inc=True)
                emit("vector", lambda e, p=p: e.scalar_tensor_tensor(
                    kvtmp[64:128, :], bvbc[64:128, p * 128 + 64:p * 128 + 128],
                    den_v[64:128, 0:1], pb[p][64:128, 65:129], ALU.mult, ALU.add), inc=True)
                emit("vector", lambda e: e.drain())
                vv = emit("vector", lambda e: e.reciprocal(den_v[:, 1:2], den_v[:, 0:1]), inc=True)
                emit("vector", lambda e: e.drain())
                release(f"ps{p}", VEs, vv)
                wait("scalar", VEs, vv)
                vkvp = emit("scalar", lambda e, p=p: e.activation(
                    kv_bfs[:, p * HD:(p + 1) * HD], kvtmp[:, :], AF.Identity,
                    scale=den_v[:, 1:2]), inc=True)
            bar()
            if s == 0:
                dbg("kv_s0", kv_bfs)
                dbg("wrk_s0", wrk)
                bar()

            guard("gpsimd", "Wreg")
            for kk in range(KC):
                dma("gpsimd", wq_bf[:, kk * D:(kk + 1) * D],
                    wqkvT[kk * 128:(kk + 1) * 128, 0:D])
            v_wq = GD.n
            wait("tensor", GD, v_wq)
            for t in range(NT):
                for oc in range(KC):
                    bk = oc % 2
                    guard("tensor", f"ps{bk}")
                    for c in range(KC):
                        emit("tensor", lambda e, oc=oc, c=c, t=t, bk=bk: e.matmul(
                            pb[bk][:, 0:512],
                            wq_bf[:, c * D + oc * 128:c * D + oc * 128 + 128],
                            xh_sl(c, t * FT, FT),
                            start=(c == 0), stop=(c == KC - 1)), inc=(c == KC - 1))
                    vq = PEs.n
                    wait("scalar", PEs, vq)
                    vqe = emit("scalar", lambda e, oc=oc, bk=bk: e.activation(
                        q_bf[:, oc * FT:(oc + 1) * FT], pb[bk][:, 0:512], AF.Silu,
                        bias=bq_v[:, oc:oc + 1]), inc=True)
                    release(f"ps{bk}", SEs, vqe)
                vqall = SEs.n
                wait("tensor", SEs, vqall)
                for oc in range(KC):
                    bk = 2 + oc % 2
                    guard("tensor", f"ps{bk}")
                    emit("tensor", lambda e, oc=oc, bk=bk: e.matmul(
                        pb[bk][0:64, 0:512], kv_bfs[0:64, oc * HD:(oc + 1) * HD],
                        q_bf[0:64, oc * FT:(oc + 1) * FT],
                        start=True, stop=True), inc=True)
                    vy = emit("tensor", lambda e, oc=oc, bk=bk: e.matmul(
                        pb[bk][64:128, 0:512], kv_bfs[64:128, oc * HD:(oc + 1) * HD],
                        q_bf[64:128, oc * FT:(oc + 1) * FT],
                        start=True, stop=True, tile_position=(64, 64)), inc=True)
                    wait("scalar", PEs, vy)
                    vye = emit("scalar", lambda e, oc=oc, bk=bk: e.activation(
                        y_bf[:, oc * FT:(oc + 1) * FT], pb[bk][:, 0:512], AF.Identity), inc=True)
                    release(f"ps{bk}", SEs, vye)
                vyall = SEs.n

                guard("tensor", "ps4")
                for c in range(KC):
                    emit("tensor", lambda e, c=c, t=t: e.matmul(
                        pb[4][0:64, 0:512], wg1_bf[:, c * HD:(c + 1) * HD],
                        xh_sl(c, t * FT, FT),
                        start=(c == 0), stop=(c == KC - 1)), inc=(c == KC - 1))
                vgh = PEs.n
                wait("scalar", PEs, vgh)
                vghe = emit("scalar", lambda e: e.activation(
                    gh_bf[:, :], pb[4][0:64, 0:512], AF.Identity,
                    bias=bg1_v[:, 0:1]), inc=True)
                release("ps4", SEs, vghe)
                def w_z(c, t2buf):
                    return ("scalar", lambda e, c=c, u=t2buf: e.activation(
                        z_bf[:, c * FT:(c + 1) * FT], u[:], AF.Identity,
                        scale=lng_v[:, c:c + 1], bias=lnb_v[:, c:c + 1]))
                ln_tile([y_bf[:, c * FT:(c + 1) * FT] for c in range(KC)], w_z,
                        chunk_wait=lambda c, v=vyall: (SEs, v))

                wait("tensor", SEs, vghe)
                for oc in range(KC):
                    bk = 5 + oc % 2
                    guard("tensor", f"ps{bk}")
                    vg = emit("tensor", lambda e, oc=oc, bk=bk: e.matmul(
                        pb[bk][:, 0:512], wg2_bf[0:64, oc * 128:(oc + 1) * 128],
                        gh_bf[0:64, :], start=True, stop=True), inc=True)
                    wait("scalar", PEs, vg)
                    vge = emit("scalar", lambda e, oc=oc, bk=bk: e.activation(
                        gate_bf[:, oc * FT:(oc + 1) * FT], pb[bk][:, 0:512], AF.Sigmoid,
                        bias=bg2_v[:, oc:oc + 1]), inc=True)
                    release(f"ps{bk}", SEs, vge)
                    wait("vector", SEs, vge)
                    emit("vector", lambda e, oc=oc: e.tensor_tensor(
                        y_bf[:, oc * FT:(oc + 1) * FT], z_bf[:, oc * FT:(oc + 1) * FT],
                        gate_bf[:, oc * FT:(oc + 1) * FT], ALU.mult), inc=True)
                vgated = VEs.n
                wait("tensor", VEs, vgated)
                for oc in range(KC):
                    bk = 7 if oc % 2 == 0 else 4
                    xb = xf_a if oc % 2 == 0 else xf_b
                    guard("sync", f"xf{oc % 2}")
                    vxf = dma("sync", xb[:], xT[s * D + oc * 128:s * D + oc * 128 + 128,
                                                t * FT:(t + 1) * FT])
                    guard("tensor", f"ps{bk}")
                    for c in range(KC):
                        emit("tensor", lambda e, oc=oc, c=c, bk=bk: e.matmul(
                            pb[bk][:, 0:512],
                            wout_bf[:, c * D + oc * 128:c * D + oc * 128 + 128],
                            y_bf[:, c * FT:(c + 1) * FT],
                            start=(c == 0), stop=(c == KC - 1)), inc=(c == KC - 1))
                    vyo = PEs.n
                    wait("scalar", PEs, vyo)
                    guard("scalar", "e_f")
                    vef = emit("scalar", lambda e, s=s, oc=oc, bk=bk: e.activation(
                        e_f[:], pb[bk][:, 0:512], AF.Identity,
                        scale=gap_v[:, s * KC + oc:s * KC + oc + 1],
                        bias=gabo_v[:, s * KC + oc:s * KC + oc + 1]), inc=True)
                    release(f"ps{bk}", SEs, vef)
                    x2b = x2f_a if oc % 2 == 0 else x2f_b
                    guard("vector", f"x2f{oc % 2}")
                    wait("vector", SEs, vef)
                    wait("vector", LD, vxf)
                    vx2 = emit("vector", lambda e, xb=xb, x2b=x2b: e.tensor_tensor(
                        x2b[:], e_f[:], xb[:], ALU.add), inc=True)
                    release("e_f", VEs, vx2)
                    release(f"xf{oc % 2}", VEs, vx2)
                    wait("gpsimd", VEs, vx2)
                    vst = dma("gpsimd", x2T[s * D + oc * 128:s * D + oc * 128 + 128,
                                            t * FT:(t + 1) * FT], x2b[:])
                    release(f"x2f{oc % 2}", GD, vst)
            v_x2store = GD.n
            vqlast = PEs.n
            release("Wreg", PEs, vqlast)
            release("xh", PEs, vqlast)
            release("kv_bfs", PEs, vqlast)
            bar()

        qw_rowmax(wg_nat, rmxg, IC, D, [(natq, "natb"), (wtf_f, "natb2")])
        qw_rowmax(wu_nat, rmxu, IC, D, [(natq, "natb"), (wtf_f, "natb2")])
        qw_rowmax(wd_nat, rmxd, KC, DI, [(wtf_f, "natb2"), (cbc_f, "natbc")])
        emit("vector", lambda e: e.memset(wrk[:, WK_HMAX:WK_HMAX + 1], 0.0), inc=True)
        wait("gpsimd", GD, v_x2store)
        for ct in range(CT):
            s, t = ct // NT, ct % NT
            x2t = x2t_bf if ct % 2 == 0 else x2t_bf2
            x2v = []
            for c in range(KC):
                guard("gpsimd", f"x2t{ct % 2}")
                x2v.append(dma("gpsimd", x2t[:, c * FT:(c + 1) * FT],
                               x2T[s * D + c * 128:s * D + c * 128 + 128,
                                   t * FT:(t + 1) * FT]))

            def w_h2(c, t2buf, s=s, ct=ct):
                return ("scalar", lambda e, c=c, u=t2buf: e.activation(
                    h2_bf[:, c * B * S + ct * FT:c * B * S + (ct + 1) * FT], u[:], AF.Identity,
                    scale=cb1_v[:, s * KC + c:s * KC + c + 1],
                    bias=sb_v[:, s * KC + c:s * KC + c + 1]))
            lv = ln_tile([x2t[:, c * FT:(c + 1) * FT] for c in range(KC)], w_h2,
                         chunk_wait=lambda c, vs=tuple(x2v): (GD, vs[c]))
            release(f"x2t{ct % 2}", *lv)
        vh2 = SEs.n
        wait("vector", SEs, vh2)
        for c in range(KC):
            emit("vector", lambda e, c=c: e.reduce_max(
                wrk[:, WK_TMP:WK_TMP + 1], h2_bf[:, c * B * S:(c + 1) * B * S],
                AX.X, apply_absolute_value=True), inc=True)
            emit("vector", lambda e: e.drain())
            emit("vector", lambda e: e.tensor_tensor(
                wrk[:, WK_HMAX:WK_HMAX + 1], wrk[:, WK_HMAX:WK_HMAX + 1],
                wrk[:, WK_TMP:WK_TMP + 1], ALU.max), inc=True)
            emit("vector", lambda e: e.drain())
        bar()

        def collapse_and_cc(src_col, wr_dst, cci, cco):
            guard("tensor", "ps4")
            emit("tensor", lambda e, sc=src_col: e.transpose(
                pb[4][0:1, 0:128], wrk[:, sc:sc + 1], id_f[:]), inc=True)
            vt = PEs.n
            wait("vector", PEs, vt)
            vv = emit("vector", lambda e, d=wr_dst: e.reduce_max(
                wrow[0:1, d:d + 1], pb[4][0:1, 0:128], AX.X), inc=True)
            release("ps4", VEs, vv)
            emit("vector", lambda e: e.drain())
            vcp = emit("vector", lambda e, d=wr_dst: e.tensor_copy(
                ccs_sb[0:1, 0:1], wrow[0:1, d:d + 1]), inc=True)
            wait("gpsimd", VEs, vcp)
            vdm = dma("gpsimd", cci[0:1, :], ccs_sb[0:1, :])
            wait("gpsimd", GD, vdm)
            ccv = CCs.inc(1)
            E("gpsimd", lambda e, i=cci, o=cco: e.collective_compute(
                "AllReduce", ALU.max, replica_groups=[list(range(8))],
                ins=[i.ap()], outs=[o.ap()]).then_inc(CCs.h, 1))
            wait("gpsimd", CCs, ccv)
            return dma("gpsimd", ccr_sb[0:1, :], cco[0:1, :])

        vcc1 = collapse_and_cc(WK_HMAX, WR_HMAX, cc1i, cc1o)
        vsg = emit("scalar", lambda e: e.activation(
            wrow[0:1, WR_SIG:WR_SIG + 1], clip_v[0:1, 0:1], AF.Sigmoid), inc=True)
        wait("vector", GD, vcc1)
        wait("vector", SEs, vsg)
        emit("vector", lambda e: e.tensor_tensor(
            wrow[0:1, WR_SX:WR_SX + 1], wrow[0:1, WR_SIG:WR_SIG + 1],
            ccr_sb[0:1, 0:1], ALU.mult), inc=True)
        emit("vector", lambda e: e.drain())
        emit("vector", lambda e: e.tensor_scalar(
            wrow[0:1, WR_SX:WR_SX + 1], wrow[0:1, WR_SX:WR_SX + 1],
            1.0 / 127.0, None, ALU.mult), inc=True)
        emit("vector", lambda e: e.drain())
        vrsx = emit("vector", lambda e: e.reciprocal(
            wrow[0:1, WR_RSX:WR_RSX + 1], wrow[0:1, WR_SX:WR_SX + 1]), inc=True)
        emit("vector", lambda e: e.drain())
        guard("tensor", "ps4")
        wait("tensor", VEs, vrsx)
        emit("tensor", lambda e: e.matmul(
            pb[4][:, 0:2], onesrow_f[0:1, :], wrow[0:1, WR_SX:WR_SX + 2],
            start=True, stop=True), inc=True)
        vbx = PEs.n
        wait("vector", PEs, vbx)
        vsx2 = emit("vector", lambda e: e.tensor_copy(
            wrk[:, WK_SXB:WK_SXB + 2], pb[4][:, 0:2]), inc=True)
        release("ps4", VEs, vsx2)
        wait("scalar", VEs, vsx2)
        for c in range(KC):
            emit("scalar", lambda e, c=c: e.activation(
                xq_i8[:, c * B * S:(c + 1) * B * S], h2_bf[:, c * B * S:(c + 1) * B * S],
                AF.Identity, scale=wrk[:, WK_RSX:WK_RSX + 1]), inc=True)
        bar()

        for nm, tv in (("h2", h2_bf), ("xq", xq_i8), ("wrow1", wrow), ("wrk1", wrk)):
            dbg(nm, tv)
        bar()
        def quant_weight(wT_d, rmx, wq_sb, n_out, n_in_chunks, cq, cqT_sb,
                         crow_sb, cbc_sb, wtf, w8):
            oc_n = n_out // 128
            emit("vector", lambda e: e.drain())
            emit("vector", lambda e, rm=rmx, c=cq: e.reciprocal(c[:, 0:oc_n], rm[:, 0:oc_n]), inc=True)
            emit("vector", lambda e: e.drain())
            vcq = emit("vector", lambda e, c=cq: e.tensor_scalar(
                c[:, 0:oc_n], c[:, 0:oc_n], 127.0, None, ALU.mult), inc=True)
            emit("vector", lambda e: e.drain())
            guard("tensor", "ps4")
            wait("tensor", VEs, vcq)
            emit("tensor", lambda e, c=cq, o=oc_n: e.transpose(
                pb[4][0:o, 0:128], c[:, 0:o], id_f[:]), inc=True)
            vtp = PEs.n
            wait("vector", PEs, vtp)
            vct = emit("vector", lambda e, ct_=cqT_sb, o=oc_n: e.tensor_copy(
                ct_[0:o, :], pb[4][0:o, 0:128]), inc=True)
            release("ps4", VEs, vct)
            wait("sync", VEs, vct)
            guard("sync", "wtf")
            vcr = dma("sync", crow_sb[0:1, 0:n_out], cqT_sb[0:oc_n, :])
            wait("tensor", LD, vcr)
            nbk = (n_out + 511) // 512
            for j in range(nbk):
                w_ = min(512, n_out - j * 512)
                bk = 4 + (j % 2)
                guard("tensor", f"ps{bk}")
                vbc = emit("tensor", lambda e, j=j, bk=bk, cr=crow_sb, w_=w_: e.matmul(
                    pb[bk][:, 0:w_], onesrow_f[0:1, :], cr[0:1, j * 512:j * 512 + w_],
                    start=True, stop=True), inc=True)
                wait("vector", PEs, vbc)
                vcb2 = emit("vector", lambda e, j=j, bk=bk, cb=cbc_sb, w_=w_: e.tensor_copy(
                    cb[:, j * 512:j * 512 + w_], pb[bk][:, 0:w_]), inc=True)
                release(f"ps{bk}", VEs, vcb2)
            vcbc = VEs.n
            release("wtf", PEs, PEs.n)
            wait("vector", VEs, vcbc)
            for kk in range(n_in_chunks):
                guard("sync", "wtf")
                vld = dma("sync", wtf[:, 0:n_out], wT_d[kk * 128:(kk + 1) * 128, :])
                wait("vector", LD, vld)
                guard("vector", "w8")
                v8 = emit("vector", lambda e, w=wtf, cb=cbc_sb, o=n_out, x8=w8: e.scalar_tensor_tensor(
                    x8[:, 0:o], w[:, 0:o], 1.0, cb[:, 0:o], ALU.mult, ALU.mult), inc=True)
                release("wtf", VEs, v8)
                if kk % 2 == 0:
                    wait("scalar", VEs, v8)
                    vq = emit("scalar", lambda e, kk=kk, x8=w8, o=n_out, wq=wq_sb: e.activation(
                        wq[:, kk * o:(kk + 1) * o], x8[:, 0:o], AF.Identity), inc=True)
                    release("w8", SEs, vq)
                else:
                    vq = emit("vector", lambda e, kk=kk, x8=w8, o=n_out, wq=wq_sb: e.tensor_copy(
                        wq[:, kk * o:(kk + 1) * o], x8[:, 0:o]), inc=True)
                    release("w8", VEs, vq)
            wait("scalar", VEs, VEs.n)
            return SEs.n

        v_gq = quant_weight(wgTd, rmxg, wgq_bf, DI, KC, cq22, cqT22,
                            crow, cbc_f, wtf_f, wq8b)
        v_uq = quant_weight(wuTd, rmxu, wuq_bf, DI, KC, cq22, cqT22,
                            crow, cbc_f, wtf_f, wq8b)
        wait("vector", VEs, vsx2)
        emit("vector", lambda e: e.drain())
        emit("vector", lambda e: e.tensor_scalar(
            sgc_g[:], rmxg[:], wrk[:, WK_SXB:WK_SXB + 1], 1.0 / 127.0,
            ALU.mult, ALU.mult), inc=True)
        emit("vector", lambda e: e.tensor_scalar(
            susx[:], rmxu[:], wrk[:, WK_SXB:WK_SXB + 1], 1.0 / 127.0,
            ALU.mult, ALU.mult), inc=True)
        emit("vector", lambda e: e.memset(tmaxr[:], 0.0), inc=True)
        bar()

        wait("tensor", SEs, v_uq)
        v_spill_all = None
        for ct in range(CT):
            guard("vector", "xqb")
            for c in range(KC):
                emit("vector", lambda e, c=c, ct=ct: e.tensor_copy(
                    xqb[:, c * FT:(c + 1) * FT],
                    xq_i8[:, c * B * S + ct * FT:c * B * S + (ct + 1) * FT]),
                    inc=(c == KC - 1))
            vxq = VEs.n
            wait("tensor", VEs, vxq)
            for ic in range(IC):
                gbk = ic % 2
                ubk = 2 + ic % 2
                guard("tensor", f"ps{gbk}")
                for c in range(KC):
                    emit("tensor", lambda e, ic=ic, c=c, bk=gbk: e.matmul(
                        pb[bk][:, 0:512],
                        wgq_bf[:, c * DI + ic * 128:c * DI + ic * 128 + 128],
                        xqb[:, c * FT:(c + 1) * FT],
                        start=(c == 0), stop=(c == KC - 1)), inc=(c == KC - 1))
                vgmm = PEs.n
                guard("tensor", f"ps{ubk}")
                for c in range(KC):
                    emit("tensor", lambda e, ic=ic, c=c, bk=ubk: e.matmul(
                        pb[bk][:, 0:512],
                        wuq_bf[:, c * DI + ic * 128:c * DI + ic * 128 + 128],
                        xqb[:, c * FT:(c + 1) * FT],
                        start=(c == 0), stop=(c == KC - 1)), inc=(c == KC - 1))
                vumm = PEs.n
                gs = gs_a if ic % 2 == 0 else gs_b
                up = up_a if ic % 2 == 0 else up_b
                tb = tb_a if ic % 2 == 0 else tb_b
                guard("scalar", f"gs{ic % 2}")
                wait("scalar", PEs, vgmm)
                vgse = emit("scalar", lambda e, ic=ic, bk=gbk, g=gs: e.activation(
                    g[:], pb[bk][:, 0:512], AF.Silu,
                    scale=sgc_g[:, ic:ic + 1]), inc=True)
                release(f"ps{gbk}", SEs, vgse)
                guard("scalar", f"up{ic % 2}")
                wait("scalar", PEs, vumm)
                vupe = emit("scalar", lambda e, bk=ubk, u=up: e.activation(
                    u[:], pb[bk][:, 0:512], AF.Identity), inc=True)
                release(f"ps{ubk}", SEs, vupe)
                guard("vector", f"tb{ic % 2}")
                guard("vector", f"tbS{ic % 2}")
                wait("vector", SEs, vupe)
                vtb = emit("vector", lambda e, g=gs, u=up, t=tb: e.tensor_tensor(
                    t[:], g[:], u[:], ALU.mult), inc=True)
                release(f"gs{ic % 2}", VEs, vtb)
                release(f"up{ic % 2}", VEs, vtb)
                wait("sync", VEs, vtb)
                vsp = dma("sync", tsp[ic * 128:(ic + 1) * 128, ct * FT:(ct + 1) * FT],
                          tb[:], sem=STm)
                release(f"tbS{ic % 2}", STm, vsp)
                vgm = emit("vector", lambda e, t=tb, ic=ic, ct=ct: e.reduce_max(
                    tmall[:, ic * CT + ct:ic * CT + ct + 1], t[:], AX.X,
                    apply_absolute_value=True), inc=True)
                release(f"tb{ic % 2}", VEs, vgm)
            release("xqb", PEs, PEs.n)
        v_spill_all = STm.n
        bar()
        for nm, tv in (("wgq", wgq_bf), ("wuq", wuq_bf), ("xqb", xqb), ("gsb", gs_b),
                       ("upb", up_b), ("tbb", tb_b), ("cbc", cbc_f), ("tmall", tmall),
                       ("cq22", cq22), ("cqT22", cqT22)):
            dbg(nm, tv)
        bar()

        emit("vector", lambda e: e.drain())
        for ic in range(IC):
            emit("vector", lambda e, ic=ic: e.reduce_max(
                tmaxr[:, ic:ic + 1], tmall[:, ic * CT:(ic + 1) * CT], AX.X), inc=True)
        emit("vector", lambda e: e.drain())
        emit("vector", lambda e: e.tensor_tensor(tmaxr[:], tmaxr[:], susx[:], ALU.mult), inc=True)
        emit("vector", lambda e: e.drain())
        vtm = emit("vector", lambda e: e.reduce_max(
            wrk[:, WK_TMX:WK_TMX + 1], tmaxr[:], AX.X), inc=True)
        emit("vector", lambda e: e.drain())
        wait("tensor", VEs, vtm)
        vcc2 = collapse_and_cc(WK_TMX, WR_TMX, cc2i, cc2o)
        vsgd = emit("scalar", lambda e: e.activation(
            wrow[0:1, WR_SIGD:WR_SIGD + 1], clip_v[0:1, 1:2], AF.Sigmoid), inc=True)
        wait("vector", GD, vcc2)
        wait("vector", SEs, vsgd)
        emit("vector", lambda e: e.tensor_tensor(
            wrow[0:1, WR_SD:WR_SD + 1], wrow[0:1, WR_SIGD:WR_SIGD + 1],
            ccr_sb[0:1, 0:1], ALU.mult), inc=True)
        emit("vector", lambda e: e.drain())
        emit("vector", lambda e: e.tensor_scalar(
            wrow[0:1, WR_SD:WR_SD + 1], wrow[0:1, WR_SD:WR_SD + 1],
            1.0 / 127.0, None, ALU.mult), inc=True)
        emit("vector", lambda e: e.drain())
        vrsd = emit("vector", lambda e: e.reciprocal(
            wrow[0:1, WR_RSD:WR_RSD + 1], wrow[0:1, WR_SD:WR_SD + 1]), inc=True)
        emit("vector", lambda e: e.drain())
        guard("tensor", "ps4")
        wait("tensor", VEs, vrsd)
        emit("tensor", lambda e: e.matmul(
            pb[4][:, 0:2], onesrow_f[0:1, :], wrow[0:1, WR_SD:WR_SD + 2],
            start=True, stop=True), inc=True)
        vbd = PEs.n
        wait("vector", PEs, vbd)
        vsd2 = emit("vector", lambda e: e.tensor_copy(
            wrk[:, WK_SDB:WK_SDB + 2], pb[4][:, 0:2]), inc=True)
        release("ps4", VEs, vsd2)
        wait("vector", VEs, vsd2)
        emit("vector", lambda e: e.drain())
        emit("vector", lambda e: e.tensor_scalar(
            qsc[:], susx[:], wrk[:, WK_RSD:WK_RSD + 1], None, ALU.mult), inc=True)
        emit("vector", lambda e: e.drain())
        bar()

        emit("vector", lambda e: e.drain())
        emit("vector", lambda e: e.reciprocal(cq8[:], rmxd[:]), inc=True)
        emit("vector", lambda e: e.drain())
        vcq8 = emit("vector", lambda e: e.tensor_scalar(
            cq8[:], cq8[:], 127.0, None, ALU.mult), inc=True)
        emit("vector", lambda e: e.drain())
        guard("tensor", "ps4")
        wait("tensor", VEs, vcq8)
        emit("tensor", lambda e: e.transpose(pb[4][0:KC, 0:128], cq8[:], id_f[:]), inc=True)
        vtp = PEs.n
        wait("vector", PEs, vtp)
        vct = emit("vector", lambda e: e.tensor_copy(cqT8[:, :], pb[4][0:KC, 0:128]), inc=True)
        release("ps4", VEs, vct)
        wait("sync", VEs, vct)
        vcr = dma("sync", crowd[0:1, :], cqT8[:, :])
        wait("tensor", LD, vcr)
        for j in range(2):
            bk = 4 + j
            guard("tensor", f"ps{bk}")
            vbc = emit("tensor", lambda e, j=j, bk=bk: e.matmul(
                pb[bk][:, 0:512], onesrow_f[0:1, :], crowd[0:1, j * 512:(j + 1) * 512],
                start=True, stop=True), inc=True)
            wait("vector", PEs, vbc)
            vcb2 = emit("vector", lambda e, j=j, bk=bk: e.tensor_copy(
                cbcd_f[:, j * 512:(j + 1) * 512], pb[bk][:, 0:512]), inc=True)
            release(f"ps{bk}", VEs, vcb2)
        vcbc = VEs.n
        wait("vector", VEs, vcbc)
        for kk in range(IC):
            guard("sync", "wtd")
            vld = dma("sync", wtd_f[:, 0:D], wdTd[kk * 128:(kk + 1) * 128, :])
            wait("vector", LD, vld)
            guard("vector", "wd8")
            v8 = emit("vector", lambda e: e.scalar_tensor_tensor(
                wd8b[:], wtd_f[:, 0:D], 1.0, cbcd_f[:], ALU.mult, ALU.mult), inc=True)
            release("wtd", VEs, v8)
            wait("scalar", VEs, v8)
            vq = emit("scalar", lambda e, kk=kk: e.activation(
                wdq_bf[:, kk * D:(kk + 1) * D], wd8b[:], AF.Identity), inc=True)
            release("wd8", SEs, vq)
        v_dq = SEs.n
        emit("vector", lambda e: e.drain())
        emit("vector", lambda e: e.tensor_scalar(
            sc8[:], rmxd[:], wrk[:, WK_SDB:WK_SDB + 1], 1.0 / 127.0,
            ALU.mult, ALU.mult), inc=True)
        emit("vector", lambda e: e.drain())
        for b_ in range(2):
            emit("vector", lambda e, b_=b_: e.tensor_tensor(
                comb[:, b_ * KC:(b_ + 1) * KC], gbp_v[:, b_ * KC:(b_ + 1) * KC],
                sc8[:], ALU.mult), inc=True)
        emit("vector", lambda e: e.drain())
        bar()

        for nm, tv in (("rmxg", rmxg), ("rmxu", rmxu), ("rmxd", rmxd), ("sgc", sgc_g),
                       ("susx", susx), ("qsc", qsc), ("comb", comb), ("tmaxr", tmaxr),
                       ("wrow2", wrow)):
            dbg(nm, tv)
        bar()
        wait("sync", STm, v_spill_all)
        wait("tensor", SEs, v_dq)
        for ct in range(CT):
            s, t = ct // NT, ct % NT
            tqb = tqb_a if ct % 2 == 0 else tqb_b
            guard("scalar", f"tqb{ct % 2}")
            for ic in range(IC):
                trd = trd_a if ic % 2 == 0 else trd_b
                tq8 = tq8_a if ic % 2 == 0 else tq8_b
                guard("sync", f"trd{ic % 2}")
                vld = dma("sync", trd[:], tsp[ic * 128:(ic + 1) * 128,
                                             ct * FT:(ct + 1) * FT])
                guard("scalar", f"tq8{ic % 2}")
                wait("scalar", LD, vld)
                v8 = emit("scalar", lambda e, ic=ic, tr=trd, t8=tq8: e.activation(
                    t8[:], tr[:], AF.Identity, scale=qsc[:, ic:ic + 1]), inc=True)
                release(f"trd{ic % 2}", SEs, v8)
                vq = emit("scalar", lambda e, ic=ic, t8=tq8, tq=tqb: e.activation(
                    tq[:, ic * FT:(ic + 1) * FT], t8[:], AF.Identity), inc=True)
                release(f"tq8{ic % 2}", SEs, vq)
            vtq = SEs.n
            wait("tensor", SEs, vtq)
            for oc in range(KC):
                bk = oc % 4
                xb = xfd_a if oc % 2 == 0 else xfd_b
                guard("scalar", f"xfd{oc % 2}")
                vxf = dma("scalar", xb[:], x2T[s * D + oc * 128:s * D + oc * 128 + 128,
                                               t * FT:(t + 1) * FT], sem=LD)
                guard("tensor", f"ps{bk}")
                for ic in range(IC):
                    emit("tensor", lambda e, oc=oc, ic=ic, bk=bk, tq=tqb: e.matmul(
                        pb[bk][:, 0:512],
                        wdq_bf[:, ic * D + oc * 128:ic * D + oc * 128 + 128],
                        tq[:, ic * FT:(ic + 1) * FT],
                        start=(ic == 0), stop=(ic == IC - 1)), inc=(ic == IC - 1))
                vdm = PEs.n
                of = of_a if oc % 2 == 0 else of_b
                guard("vector", f"of{oc % 2}")
                wait("vector", PEs, vdm)
                wait("vector", LD, vxf)
                vof = emit("vector", lambda e, s=s, oc=oc, bk=bk, xb=xb, of=of: e.scalar_tensor_tensor(
                    of[:], pb[bk][:, 0:512], comb[:, s * KC + oc:s * KC + oc + 1],
                    xb[:], ALU.mult, ALU.add), inc=True)
                release(f"ps{bk}", VEs, vof)
                release(f"xfd{oc % 2}", VEs, vof)
                wait("gpsimd", VEs, vof)
                vst = dma("gpsimd", out_d[s * D + oc * 128:s * D + oc * 128 + 128,
                                          t * FT:(t + 1) * FT], of[:], sem=STm)
                release(f"of{oc % 2}", STm, vst)
            release(f"tqb{ct % 2}", PEs, PEs.n)
        for nm, tv in (("wdq", wdq_bf), ("tqa", tqb_a), ("tqb2", tqb_b), ("cbcd", cbcd_f), ("cq8", cq8)):
            dbg(nm, tv)
        wait("sync", STm, STm.n)
        bar()


    try:
        _sched()
    except _StopSched:
        pass

    with nc.Block() as block:
        @block.sync
        def _(e):
            for f in prog["sync"]:
                f(e)

        @block.gpsimd
        def _(e):
            for f in prog["gpsimd"]:
                f(e)

        @block.vector
        def _(e):
            for f in prog["vector"]:
                f(e)

        @block.scalar
        def _(e):
            for f in prog["scalar"]:
                f(e)

        @block.tensor
        def _(e):
            for f in prog["tensor"]:
                f(e)

    es.close()
    return nc


_NC_CACHE = []


def kernel(L0_attn_x, L0_attn_w_qkv, L0_attn_b_qkv, L0_attn_w_g1, L0_attn_b_g1,
           L0_attn_w_g2, L0_attn_b_g2, L0_attn_w_out, L0_attn_b_out,
           L0_attn_ln_g, L0_attn_ln_b, L0_ffn_clip_x, L0_ffn_clip_down,
           L0_ffn_w_gate, L0_ffn_w_up, L0_ffn_w_down, L0_c, L0_wmod):
    f = np.float32
    cc = np.ascontiguousarray

    def pp(v):
        return cc(np.asarray(v, f).reshape(-1, 128).T)

    shared = {
        "wqkvT": cc(np.asarray(L0_attn_w_qkv, f).T),
        "woutT": cc(np.asarray(L0_attn_w_out, f).T),
        "wg1T": cc(np.asarray(L0_attn_w_g1, f).T),
        "wg2T": cc(np.asarray(L0_attn_w_g2, f).T),
        "wgT": cc(np.asarray(L0_ffn_w_gate, f).T),
        "wuT": cc(np.asarray(L0_ffn_w_up, f).T),
        "wdT": cc(np.asarray(L0_ffn_w_down, f).T),
        "wg_nat": cc(np.asarray(L0_ffn_w_gate, f)),
        "wu_nat": cc(np.asarray(L0_ffn_w_up, f)),
        "wd_nat": cc(np.asarray(L0_ffn_w_down, f)),
        "wmod": cc(np.asarray(L0_wmod, f)),
        "bq": pp(np.asarray(L0_attn_b_qkv, f)[0:D]),
        "bv_row": cc(np.asarray(L0_attn_b_qkv, f)[2 * D:3 * D].reshape(1, D)),
        "bg1": cc(np.asarray(L0_attn_b_g1, f).reshape(HD, 1)),
        "bg2": pp(L0_attn_b_g2),
        "bout": pp(L0_attn_b_out),
        "lng": pp(L0_attn_ln_g),
        "lnb": pp(L0_attn_ln_b),
        "clip2": cc(np.stack([np.asarray(L0_ffn_clip_x, f).reshape(()),
                              np.asarray(L0_ffn_clip_down, f).reshape(())]).reshape(1, 2)),
    }
    x = np.asarray(L0_attn_x, f)
    c_all = np.asarray(L0_c, f)
    in_maps = []
    for core in range(8):
        xs = cc(x[2 * core:2 * core + 2].transpose(0, 2, 1).reshape(B * D, S))
        cpair = c_all[2 * core:2 * core + 2]
        cT2 = cc(cpair.reshape(2, KC, 128).transpose(2, 1, 0).reshape(128, 2 * KC))
        m = dict(shared)
        m["xT"] = xs
        m["cT"] = cT2
        in_maps.append(m)

    if not _NC_CACHE:
        _NC_CACHE.append(build_nc())
    nc = _NC_CACHE[0]
    res = run_bass_kernel_spmd(nc, in_maps, core_ids=list(range(8)))
    outs = []
    for core in range(8):
        o = np.asarray(res.results[core]["out"])
        outs.append(o.reshape(B, D, S).transpose(0, 2, 1))
    return np.concatenate(outs, axis=0).astype(np.float32)


# version 20
# speedup vs baseline: 1.0345x; 1.0192x over previous
import numpy as np
import concourse.bass as bass
import concourse.mybir as mybir
from concourse.bass_utils import run_bass_kernel_spmd

F32 = mybir.dt.float32
BF16 = mybir.dt.bfloat16
I8 = mybir.dt.int8
AF = mybir.ActivationFunctionType
ALU = mybir.AluOpType
AX = mybir.AxisListType

B = 2
S = 2048
D = 1024
KC = 8
NH = 16
HD = 64
DI = 2816
IC = 22
FT = 512
NT = S // FT
KT = S // 128
CT = B * NT

ENGINES = ("sync", "gpsimd", "vector", "scalar", "tensor")


class Sem:
    def __init__(self, h):
        self.h = h
        self.n = 0

    def inc(self, k=1):
        self.n += k
        return self.n


def build_nc():
    nc = bass.Bass("TRN2", target_bir_lowering=False)

    DT = lambda n, sh, dt=F32, kind="ExternalInput": nc.dram_tensor(n, sh, dt, kind=kind)
    xT = DT("xT", [B * D, S])
    wqkvT = DT("wqkvT", [D, 3 * D])
    woutT = DT("woutT", [D, D])
    wg1T = DT("wg1T", [D, HD])
    wg2T = DT("wg2T", [HD, D])
    wgTd = DT("wgT", [D, DI])
    wuTd = DT("wuT", [D, DI])
    wdTd = DT("wdT", [DI, D])
    wg_nat = DT("wg_nat", [DI, D])
    wu_nat = DT("wu_nat", [DI, D])
    wd_nat = DT("wd_nat", [D, DI])
    wmodd = DT("wmod", [D, 6 * D])
    cTd = DT("cT", [128, 2 * KC])
    bq_d = DT("bq", [128, KC])
    bv_row_d = DT("bv_row", [1, D])
    bg1_d = DT("bg1", [HD, 1])
    bg2_d = DT("bg2", [128, KC])
    bout_d = DT("bout", [128, KC])
    lng_d = DT("lng", [128, KC])
    lnb_d = DT("lnb", [128, KC])
    clip_d = DT("clip2", [1, 2])
    import os as _os0
    KDBG = bool(int(_os0.environ.get("KDBG", "0")))
    _K1 = "ExternalOutput" if KDBG else "Internal"
    out_d = DT("out", [B * D, S], F32, "ExternalOutput")
    x2T = DT("x2T", [B * D, S], F32, _K1)
    tsp = DT("tsp", [DI, B * S], BF16, _K1)
    cc1i = DT("cc1i", [1, 16], F32, "Internal")
    cc1o = DT("cc1o", [1, 16], F32, "Internal")
    cc2i = DT("cc2i", [1, 16], F32, "Internal")
    cc2o = DT("cc2o", [1, 16], F32, "Internal")

    id_dram = nc.inline_tensor(np.eye(128, dtype=np.float32), "ident")
    onescol_dram = nc.inline_tensor(np.ones((128, 1), np.float32), "onescol")
    onesrow_dram = nc.inline_tensor(np.ones((1, 128), np.float32), "onesrow")

    A = lambda name, shape, dt=F32: nc.alloc_sbuf_tensor(name, shape, dt)
    id_f = A("id_f", [128, 128])
    onescol_bf = A("onescol_bf", [128, 1], BF16)
    onesrow_f = A("onesrow_f", [1, 128])
    onesrow_bf = A("onesrow_bf", [1, 128], BF16)
    mod6 = A("mod6", [128, 6 * 2 * KC])
    sa_v = A("sa_v", [128, 2 * KC])
    ca1_v = A("ca1_v", [128, 2 * KC])
    gap_v = A("gap_v", [128, 2 * KC])
    sb_v = A("sb_v", [128, 2 * KC])
    cb1_v = A("cb1_v", [128, 2 * KC])
    gbp_v = A("gbp_v", [128, 2 * KC])
    gabo_v = A("gabo_v", [128, 2 * KC])
    sc16a = A("sc16a", [128, 2 * KC])
    sc16b = A("sc16b", [128, 2 * KC])
    bq_v = A("bq_v", [128, KC])
    bg2_v = A("bg2_v", [128, KC])
    bout_v = A("bout_v", [128, KC])
    lng_v = A("lng_v", [128, KC])
    lnb_v = A("lnb_v", [128, KC])
    bg1_v = A("bg1_v", [HD, 1])
    c_sb = A("c_sb", [128, 2 * KC])
    c_bf = A("c_bf", [128, 2 * KC], BF16)
    clip_v = A("clip_v", [1, 2])
    bvbc = A("bvbc", [128, D])
    wrk = A("wrk", [128, 16])
    wrow = A("wrow", [1, 32])
    den_v = A("den_v", [128, 2])
    rmxg = A("rmxg", [128, IC])
    rmxu = A("rmxu", [128, IC])
    rmxd = A("rmxd", [128, KC])
    tmall = A("tmall", [128, IC * CT])
    sgc_g = A("sgc_g", [128, IC])
    susx = A("susx", [128, IC])
    qsc = A("qsc", [128, IC])
    comb = A("comb", [128, 2 * KC])
    tmaxr = A("tmaxr", [128, IC])
    cq22 = A("cq22", [128, IC])
    cq8 = A("cq8", [128, KC])
    sc8 = A("sc8", [128, KC])
    ccs_sb = A("ccs_sb", [1, 16])
    cv = A("cv", [1, 4])
    ccr_sb = A("ccr_sb", [1, 16])

    WK_MRUN, WK_NEGM, WK_HMAX, WK_TMP, WK_SXB, WK_RSX, WK_SDB, WK_RSD, WK_TMX = range(9)
    WR_M, WR_NEGM, WR_HMAX, WR_SIG, WR_SX, WR_RSX, WR_TMX, WR_SIGD, WR_SD, WR_RSD = range(10)

    X0 = (nc.sbuf_base + 31) & ~31
    TOP = nc.sbuf_top

    def AT(name, shape, dt, off):
        assert off % 32 == 0, name
        nbytes = int(np.prod(shape[1:])) * mybir.dt.size(dt)
        assert X0 <= off and off + nbytes <= TOP, (name, off, nbytes, X0, TOP)
        return nc.alloc_sbuf_tensor_at(name, shape, dt, offset=off)

    K = 1024
    Z = X0
    t1a = AT("t1a", [128, FT], BF16, Z + 0 * K)
    t1b = AT("t1b", [128, FT], BF16, Z + 1 * K)
    t2a = AT("t2a", [128, FT], BF16, Z + 2 * K)
    t2b = AT("t2b", [128, FT], BF16, Z + 3 * K)
    sqa = AT("sqa", [128, FT], BF16, Z + 4 * K)
    sqb = AT("sqb", [128, FT], BF16, Z + 5 * K)
    rb_a = AT("rb_a", [128, FT], BF16, Z + 6 * K)
    mb_a = AT("mb_a", [128, FT], BF16, Z + 7 * K)
    eka = AT("eka", [128, D], BF16, Z + 8 * K)
    ekb = AT("ekb", [128, D], BF16, Z + 10 * K)
    muT = AT("muT", [1, FT], F32, Z + 12 * K)
    e2T = AT("e2T", [1, FT], F32, Z + 14 * K)
    tmpT = AT("tmpT", [1, FT], F32, Z + 16 * K)
    varT = AT("varT", [1, FT], F32, Z + 18 * K)
    rT = AT("rT", [1, FT], F32, Z + 20 * K)
    mrT = AT("mrT", [1, FT], F32, Z + 22 * K)
    rT_bf = AT("rT_bf", [1, FT], BF16, Z + 24 * K)
    mrT_bf = AT("mrT_bf", [1, FT], BF16, Z + 25 * K)
    bvrow_sb = AT("bvrow_sb", [1, D], F32, Z + 26 * K)

    Z1 = Z + 31 * K

    wkv_bf = AT("wkv_bf", [128, KC * 2 * D], BF16, Z1)
    wq_bf = AT("wq_bf", [128, KC * D], BF16, Z1)
    wout_bf = AT("wout_bf", [128, KC * D], BF16, Z1 + 32 * K)
    wg1_bf = AT("wg1_bf", [128, KC * HD], BF16, Z1 + 48 * K)
    wg2_bf = AT("wg2_bf", [HD, D], BF16, Z1 + 49 * K)
    xh_bf = AT("xh_bf", [128, KC * S], BF16, Z1 + 52 * K)
    R1 = Z1 + 84 * K
    k_bf = AT("k_bf", [128, KT * D], BF16, R1)
    q_bf = AT("q_bf", [128, KC * FT], BF16, R1)
    y_bf = AT("y_bf", [128, KC * FT], BF16, R1 + 8 * K)
    z_bf = AT("z_bf", [128, KC * FT], BF16, R1 + 16 * K)
    gate_bf = AT("gate_bf", [128, KC * FT], BF16, R1 + 24 * K)
    R2 = Z1 + 116 * K
    v_sb = AT("v_sb", [128, KT * NH * 65], BF16, R2)
    e_f = AT("e_f", [128, FT], F32, R2)
    xf_a = AT("xf_a", [128, FT], F32, R2 + 2 * K)
    xf_b = AT("xf_b", [128, FT], F32, R2 + 4 * K)
    x2f_a = AT("x2f_a", [128, FT], F32, R2 + 6 * K)
    x2f_b = AT("x2f_b", [128, FT], F32, R2 + 8 * K)
    gh_bf = AT("gh_bf", [HD, FT], BF16, R2 + 10 * K)
    kvtmp = AT("kvtmp", [128, HD], F32, R2 + 12 * K)
    kv_bfs = AT("kv_bfs", [128, 8 * HD], BF16, R2 + 13 * K)
    assert R2 + 34 * K <= TOP

    wmod_bf = AT("wmod_bf", [128, KC * 6 * D], BF16, Z1)
    wm32 = AT("wm32", [128, KC * 128], F32, Z1 + 97 * K)
    wm32b = AT("wm32b", [128, KC * 1024], F32, Z1)

    x2t_bf = AT("x2t_bf", [128, KC * FT], BF16, Z1)
    x2t_bf2 = AT("x2t_bf2", [128, KC * FT], BF16, Z1 + 8 * K)
    xq_i8 = AT("xq_i8", [128, KC * B * S], I8, Z1)
    h2_bf = AT("h2_bf", [128, KC * B * S], BF16, Z1 + 33 * K)

    wgq_bf = AT("wgq_bf", [128, KC * DI], BF16, Z1 + 33 * K)
    wuq_bf = AT("wuq_bf", [128, KC * DI], BF16, Z1 + 77 * K)
    cbc_f = AT("cbc_f", [128, DI], F32, Z1 + 121 * K)
    wtf_f = AT("wtf_f", [128, DI], F32, Z1 + 132 * K)
    crow = AT("crow", [1, DI], F32, Z1 + 132 * K)
    cqT22 = AT("cqT22", [IC, 128], F32, Z1 + 143 * K)
    wq8b = AT("wq8b", [128, DI], I8, Z1 + 144 * K)
    natq = AT("natq", [128, D], F32, Z1 + 147 * K)
    xqb = AT("xqb", [128, KC * FT], BF16, Z1 + 151 * K)
    gs_a = AT("gs_a", [128, FT], BF16, Z1 + 159 * K)
    gs_b = AT("gs_b", [128, FT], BF16, Z1 + 160 * K)
    up_a = AT("up_a", [128, FT], BF16, Z1 + 161 * K)
    up_b = AT("up_b", [128, FT], BF16, Z1 + 162 * K)
    tb_a = AT("tb_a", [128, FT], BF16, Z1 + 163 * K)
    tb_b = AT("tb_b", [128, FT], BF16, Z1 + 164 * K)
    assert Z1 + 165 * K <= TOP, (Z1 + 165 * K, TOP)

    wdq_bf = AT("wdq_bf", [128, IC * D], BF16, Z1)
    tqb_a = AT("tqb_a", [128, IC * FT], BF16, Z1 + 44 * K)
    tqb_b = AT("tqb_b", [128, IC * FT], BF16, Z1 + 66 * K)
    cbcd_f = AT("cbcd_f", [128, D], F32, Z1 + 88 * K)
    wtd_f = AT("wtd_f", [128, DI], F32, Z1 + 92 * K)
    natd = AT("natd", [128, DI], F32, Z1 + 103 * K)
    crowd = AT("crowd", [1, D], F32, Z1 + 114 * K)
    cqT8 = AT("cqT8", [KC, 128], F32, Z1 + 118 * K)
    wd8b = AT("wd8b", [128, D], I8, Z1 + 119 * K)
    tq8_a = AT("tq8_a", [128, FT], I8, Z1 + 120 * K)
    tq8_b = AT("tq8_b", [128, FT], I8, Z1 + 121 * K)
    of_a = AT("of_a", [128, FT], F32, Z1 + 122 * K)
    of_b = AT("of_b", [128, FT], F32, Z1 + 124 * K)
    trd_a = AT("trd_a", [128, FT], BF16, Z1 + 126 * K)
    trd_b = AT("trd_b", [128, FT], BF16, Z1 + 127 * K)
    trd_c = AT("trd_c", [128, FT], BF16, Z1 + 136 * K)
    trd_d = AT("trd_d", [128, FT], BF16, Z1 + 137 * K)
    dtmp_a = AT("dtmp_a", [128, FT], F32, Z1 + 128 * K)
    dtmp_b = AT("dtmp_b", [128, FT], F32, Z1 + 130 * K)
    xfd_a = AT("xfd_a", [128, FT], F32, Z1 + 132 * K)
    xfd_b = AT("xfd_b", [128, FT], F32, Z1 + 134 * K)

    pb = [nc.alloc_psum_tensor(f"pb{i}", [128, 512], F32) for i in range(8)]

    import contextlib
    es = contextlib.ExitStack()
    SEM = lambda n: Sem(es.enter_context(nc.semaphore(n)))
    LD, GD, STm = SEM("LD"), SEM("GD"), SEM("STm")
    PEs, VEs, SEs, GEs = SEM("PEs"), SEM("VEs"), SEM("SEs"), SEM("GEs")
    CCs, PHB = SEM("CCs"), SEM("PHB")
    ESEM = {"vector": VEs, "scalar": SEs, "tensor": PEs, "gpsimd": GEs,
            "sync": None}
    DSEM = {"sync": LD, "gpsimd": GD}

    prog = {en: [] for en in ENGINES}

    def E(en, f):
        prog[en].append(f)

    def emit(en, f, inc=False):
        if not inc:
            E(en, f)
            return None
        s = ESEM[en]
        v = s.inc(1)
        E(en, lambda e, ff=f, hh=s.h: ff(e).then_inc(hh, 1))
        return v

    def wait(en, sem, val):
        if val is None:
            return
        E(en, lambda e, hh=sem.h, v=val: e.wait_ge(hh, v))

    def dma(queue, out, in_, sem=None):
        s = sem if sem is not None else DSEM[queue]
        v = s.inc(16)
        E(queue, lambda e, o=out, i=in_, hh=s.h: e.dma_start(out=o, in_=i).then_inc(hh, 16))
        return v

    def bar():
        tgt = PHB.inc(5)
        for en in ENGINES:
            E(en, lambda e, hh=PHB.h: e.nop().then_inc(hh, 1))
            E(en, lambda e, hh=PHB.h, t=tgt: e.wait_ge(hh, t))

    freemap = {}

    def guard(en, name):
        if name in freemap:
            s, v = freemap.pop(name)
            wait(en, s, v)

    def release(name, sem, val):
        freemap[name] = (sem, val)

    def dbg(name, tens):
        if not KDBG:
            return
        sh = list(tens.shape)
        d = nc.dram_tensor("dbg_" + name, sh, tens.dtype, kind="ExternalOutput")
        dma("gpsimd", d[:, :] if len(sh) == 2 else d[:], tens[:])

    import os as _os
    _klim = int(_os.environ.get("KPHASE", "999"))
    class _StopSched(Exception):
        pass
    _phctr = [0]
    _orig_bar = bar
    def bar():
        _orig_bar()
        _phctr[0] += 1
        if _phctr[0] >= _klim:
            raise _StopSched()

    def _sched():
        def qw_rowmax(nat_d, rmx, oc_n, w_in, bufs):
            for oc in range(oc_n):
                nb, key = bufs[oc % len(bufs)]
                guard("sync", key)
                vld = dma("sync", nb[:, 0:w_in], nat_d[oc * 128:(oc + 1) * 128, :])
                wait("vector", LD, vld)
                vr = emit("vector", lambda e, oc=oc, nb=nb, rm=rmx, w_in=w_in: e.reduce_max(
                    rm[:, oc:oc + 1], nb[:, 0:w_in], AX.X, apply_absolute_value=True), inc=True)
                release(key, VEs, vr)

        dma("sync", id_f[:], id_dram[:, :])
        dma("sync", c_sb[:], cTd[:, :])
        dma("sync", bq_v[:], bq_d[:, :])
        dma("sync", bg2_v[:], bg2_d[:, :])
        dma("sync", bout_v[:], bout_d[:, :])
        dma("sync", lng_v[:], lng_d[:, :])
        dma("sync", lnb_v[:], lnb_d[:, :])
        dma("sync", bg1_v[:], bg1_d[:, :])
        dma("sync", clip_v[:], clip_d[:, :])
        dma("sync", onesrow_f[:], onesrow_dram[:, :])
        dma("sync", bvrow_sb[:], bv_row_d[:, :])
        dma("gpsimd", onescol_bf[:], onescol_dram[:, :])
        dma("gpsimd", onesrow_bf[:], onesrow_dram[:, :])
        for kk in range(KC):
            dma("gpsimd", wmod_bf[:, kk * 6 * D:(kk + 1) * 6 * D],
                wmodd[kk * 128:(kk + 1) * 128, :])
        v_ld0, v_gd0 = LD.n, GD.n
        bar()
        for en in ("vector", "scalar", "tensor"):
            wait(en, LD, v_ld0)
            wait(en, GD, v_gd0)

        emit("vector", lambda e: e.memset(ccs_sb[:], 0.0), inc=True)
        emit("vector", lambda e: e.memset(cv[0:1, 0:1], 1.0 / D), inc=True)
        emit("vector", lambda e: e.memset(cv[0:1, 1:2], 1e-5), inc=True)
        vcv = emit("vector", lambda e: e.memset(cv[0:1, 2:3], -1.0), inc=True)
        vsc = emit("scalar", lambda e: e.activation(c_sb[:], c_sb[:], AF.Silu), inc=True)
        wait("vector", SEs, vsc)
        vcb = emit("vector", lambda e: e.tensor_copy(c_bf[:], c_sb[:]), inc=True)
        for j in range(2):
            emit("tensor", lambda e, j=j: e.matmul(
                pb[2 + j][:, 0:512], onesrow_f[0:1, :],
                bvrow_sb[0:1, j * 512:(j + 1) * 512], start=True, stop=True), inc=True)
        vpe_bv = PEs.n
        wait("vector", PEs, vpe_bv)
        for j in range(2):
            vv = emit("vector", lambda e, j=j: e.tensor_copy(
                bvbc[:, j * 512:(j + 1) * 512], pb[2 + j][:, 0:512]), inc=True)
            release(f"ps{2 + j}", VEs, vv)
        wait("tensor", VEs, vcb)
        for mc in [m for m in range(48) if not (16 <= m < 24 or 40 <= m < 48)]:
            bk = mc % 2
            guard("tensor", f"ps{bk}")
            for kk in range(KC):
                emit("tensor", lambda e, mc=mc, kk=kk, bk=bk: e.matmul(
                    pb[bk][:, 0:2],
                    wmod_bf[:, kk * 6 * D + mc * 128:kk * 6 * D + mc * 128 + 128],
                    c_bf[:, kk * 2:kk * 2 + 2],
                    start=(kk == 0), stop=(kk == KC - 1)), inc=(kk == KC - 1))
            vmm = PEs.n
            wait("vector", PEs, vmm)
            Vv, ff = mc // 8, mc % 8
            for b_ in range(2):
                vv = emit("vector", lambda e, bk=bk, b_=b_, Vv=Vv, ff=ff: e.tensor_copy(
                    mod6[:, Vv * 16 + b_ * 8 + ff:Vv * 16 + b_ * 8 + ff + 1],
                    pb[bk][:, b_:b_ + 1]), inc=True)
            release(f"ps{bk}", VEs, vv)
        release("wm32b", PEs, PEs.n)
        for blk, mc0 in enumerate((16, 40)):
            vld = 0
            for kk in range(KC):
                if kk == 0:
                    guard("sync", "wm32b")
                vld = dma("sync", wm32b[:, kk * 1024:(kk + 1) * 1024],
                          wmodd[kk * 128:(kk + 1) * 128, mc0 * 128:(mc0 + 8) * 128])
            wait("tensor", LD, vld)
            for j in range(8):
                mc = mc0 + j
                bk = j % 2
                guard("tensor", f"ps{bk}")
                for kk in range(KC):
                    emit("tensor", lambda e, kk=kk, bk=bk, j=j: e.matmul(
                        pb[bk][:, 0:2],
                        wm32b[:, kk * 1024 + j * 128:kk * 1024 + j * 128 + 128],
                        c_sb[:, kk * 2:kk * 2 + 2],
                        start=(kk == 0), stop=(kk == KC - 1)), inc=(kk == KC - 1))
                vmm = PEs.n
                wait("vector", PEs, vmm)
                Vv, ff = mc // 8, mc % 8
                for b_ in range(2):
                    vv = emit("vector", lambda e, bk=bk, b_=b_, Vv=Vv, ff=ff: e.tensor_copy(
                        mod6[:, Vv * 16 + b_ * 8 + ff:Vv * 16 + b_ * 8 + ff + 1],
                        pb[bk][:, b_:b_ + 1]), inc=True)
                release(f"ps{bk}", VEs, vv)
            release("wm32b", PEs, PEs.n)
        emit("vector", lambda e: e.drain())
        emit("vector", lambda e: e.tensor_copy(sa_v[:], mod6[:, 0:16]), inc=True)
        emit("vector", lambda e: e.tensor_scalar(ca1_v[:], mod6[:, 16:32], 1.0, None, ALU.add), inc=True)
        emit("vector", lambda e: e.tensor_copy(sb_v[:], mod6[:, 48:64]), inc=True)
        emit("vector", lambda e: e.tensor_scalar(cb1_v[:], mod6[:, 64:80], 1.0, None, ALU.add), inc=True)
        for (src0, dstv) in ((32, gap_v), (80, gbp_v)):
            emit("vector", lambda e, s0=src0: e.tensor_single_scalar(
                sc16a[:], mod6[:, s0:s0 + 16], 0.0, ALU.is_ge), inc=True)
            emit("vector", lambda e: e.drain())
            emit("vector", lambda e: e.tensor_scalar(
                sc16b[:], sc16a[:], 2.0, -1.0, ALU.mult, ALU.add), inc=True)
            emit("vector", lambda e, s0=src0: e.tensor_scalar(
                sc16a[:], mod6[:, s0:s0 + 16], -1.0, 0.25, ALU.mult, ALU.max), inc=True)
            emit("vector", lambda e: e.drain())
            emit("vector", lambda e, s0=src0: e.tensor_tensor(
                sc16a[:], sc16a[:], mod6[:, s0:s0 + 16], ALU.max), inc=True)
            emit("vector", lambda e: e.drain())
            emit("vector", lambda e, d=dstv: e.tensor_tensor(
                d[:], sc16a[:], sc16b[:], ALU.mult), inc=True)
            emit("vector", lambda e: e.drain())
        for b_ in range(2):
            emit("vector", lambda e, b_=b_: e.tensor_tensor(
                gabo_v[:, b_ * 8:b_ * 8 + 8], gap_v[:, b_ * 8:b_ * 8 + 8], bout_v[:],
                ALU.mult), inc=True)
        emit("vector", lambda e: e.drain())
        bar()
        for nm, tv in (("mod6", mod6), ("sa", sa_v), ("ca1", ca1_v), ("gap", gap_v),
                       ("sb", sb_v), ("cb1", cb1_v), ("gbp", gbp_v), ("gabo", gabo_v),
                       ("csilu", c_sb), ("bvbc", bvbc)):
            dbg(nm, tv)
        bar()

        for kk in range(KC):
            dma("gpsimd", wkv_bf[:, kk * 2 * D:(kk + 1) * 2 * D],
                wqkvT[kk * 128:(kk + 1) * 128, D:3 * D])
        for kk in range(KC):
            dma("gpsimd", wout_bf[:, kk * D:(kk + 1) * D], woutT[kk * 128:(kk + 1) * 128, :])
        for kk in range(KC):
            dma("gpsimd", wg1_bf[:, kk * HD:(kk + 1) * HD], wg1T[kk * 128:(kk + 1) * 128, :])
        v_wload = dma("gpsimd", wg2_bf[:], wg2T[:, :])

        def ln_tile(srcs, out_writer, chunk_wait=None):
            guard("tensor", "ps0")
            guard("tensor", "ps1")
            for c in range(KC):
                sq = sqa if c % 2 == 0 else sqb
                guard("scalar", f"sq{c % 2}")
                if chunk_wait is not None:
                    w = chunk_wait(c)
                    if w is not None:
                        wait("scalar", *w)
                        wait("vector", *w)
                vsq = emit("scalar", lambda e, s=srcs[c], q=sq: e.activation(
                    q[:], s, AF.Square), inc=True)
                wait("tensor", SEs, vsq)
                emit("tensor", lambda e, s=srcs[c], c=c: e.matmul(
                    pb[0][0:1, 0:512], onescol_bf[:, 0:1], s,
                    start=(c == 0), stop=(c == KC - 1)), inc=(c == KC - 1))
                vq = emit("tensor", lambda e, q=sq, c=c: e.matmul(
                    pb[1][0:1, 0:512], onescol_bf[:, 0:1], q[:],
                    start=(c == 0), stop=(c == KC - 1)), inc=True)
                release(f"sq{c % 2}", PEs, vq)
            vstat = PEs.n
            wait("scalar", PEs, vstat)
            emit("scalar", lambda e: e.activation(muT[:], pb[0][0:1, 0:512], AF.Identity,
                                                  scale=cv[0:1, 0:1]), inc=True)
            v1 = emit("scalar", lambda e: e.activation(e2T[:], pb[1][0:1, 0:512], AF.Identity,
                                                       scale=cv[0:1, 0:1]), inc=True)
            release("ps0", SEs, v1)
            release("ps1", SEs, v1)
            wait("vector", SEs, v1)
            emit("vector", lambda e: e.tensor_tensor(tmpT[:], muT[:], muT[:], ALU.mult), inc=True)
            v2 = emit("vector", lambda e: e.tensor_tensor(varT[:], e2T[:], tmpT[:], ALU.subtract), inc=True)
            wait("scalar", VEs, v2)
            v3 = emit("scalar", lambda e: e.activation(tmpT[:], varT[:], AF.Sqrt,
                                                       bias=cv[0:1, 1:2]), inc=True)
            wait("vector", SEs, v3)
            emit("vector", lambda e: e.reciprocal(rT[:], tmpT[:]), inc=True)
            emit("vector", lambda e: e.tensor_tensor(mrT[:], muT[:], rT[:], ALU.mult), inc=True)
            guard("vector", "rtbf")
            emit("vector", lambda e: e.tensor_copy(rT_bf[:], rT[:]), inc=True)
            v4 = emit("vector", lambda e: e.tensor_copy(mrT_bf[:], mrT[:]), inc=True)
            guard("tensor", "ps2")
            guard("tensor", "ps3")
            wait("tensor", VEs, v4)
            emit("tensor", lambda e: e.matmul(pb[2][:, 0:512], onesrow_bf[0:1, :], rT_bf[0:1, :],
                                              start=True, stop=True), inc=True)
            v5 = emit("tensor", lambda e: e.matmul(pb[3][:, 0:512], onesrow_bf[0:1, :], mrT_bf[0:1, :],
                                                   start=True, stop=True), inc=True)
            release("rtbf", PEs, v5)
            wait("vector", PEs, v5)
            guard("vector", "rbmb")
            emit("vector", lambda e: e.tensor_copy(rb_a[:], pb[2][:, 0:512]), inc=True)
            v6 = emit("vector", lambda e: e.tensor_copy(mb_a[:], pb[3][:, 0:512]), inc=True)
            release("ps2", VEs, v6)
            release("ps3", VEs, v6)
            lastv = None
            for c in range(KC):
                t1 = t1a if c % 2 == 0 else t1b
                t2 = t2a if c % 2 == 0 else t2b
                guard("vector", f"t12{c % 2}")
                emit("vector", lambda e, s=srcs[c], t=t1: e.tensor_tensor(
                    t[:], s, rb_a[:], ALU.mult), inc=True)
                v7 = emit("vector", lambda e, t=t1, u=t2: e.tensor_tensor(
                    u[:], t[:], mb_a[:], ALU.subtract), inc=True)
                en, fn = out_writer(c, t2)
                wait(en, VEs, v7)
                v8 = emit(en, fn, inc=True)
                release(f"t12{c % 2}", ESEM[en], v8)
                lastv = (ESEM[en], v8)
            release("rbmb", *lastv)
            bar()
            return lastv

        def xh_sl(c, t0, n):
            return xh_bf[:, c * S + t0:c * S + t0 + n]

        v_x2store = None
        for s in range(B):
            guard("gpsimd", "xh")
            xdmav = []
            for c in range(KC):
                xdmav.append(dma("gpsimd", xh_bf[:, c * S:(c + 1) * S],
                                 xT[s * D + c * 128:s * D + (c + 1) * 128, :]))
            if s == 1:
                guard("gpsimd", "Wreg")
                for kk in range(KC):
                    dma("gpsimd", wkv_bf[:, kk * 2 * D:(kk + 1) * 2 * D],
                        wqkvT[kk * 128:(kk + 1) * 128, D:3 * D])
            v_wkv = GD.n if s == 1 else v_wload
            emit("vector", lambda e: e.memset(wrk[:, WK_MRUN:WK_MRUN + 1], -1e30), inc=True)
            for _k in ("x2f0", "x2f1", "xf0", "xf1", "e_f"):
                guard("vector", _k)
            emit("vector", lambda e: e.memset(
                v_sb[:, :].rearrange("p (x e) -> p x e", e=65)[:, :, 64:65], 1.0), inc=True)
            wait("tensor", GD, v_wkv)

            def emit_kv(kt):
                for half, bks in ((0, (4, 5)), (1, (6, 7))):
                    for ncnk in range(2):
                        bk = bks[ncnk]
                        guard("tensor", f"ps{bk}")
                        for c in range(KC):
                            emit("tensor", lambda e, kt=kt, c=c, h=half, ncnk=ncnk, bk=bk: e.matmul(
                                pb[bk][:, 0:512],
                                xh_bf[:, c * S + kt * 128:c * S + kt * 128 + 128],
                                wkv_bf[:, c * 2 * D + h * D + ncnk * 512:
                                       c * 2 * D + h * D + ncnk * 512 + 512],
                                start=(c == 0), stop=(c == KC - 1)),
                                inc=(c == KC - 1))
                vkv = PEs.n
                wait("vector", PEs, vkv)
                for j, bk in enumerate((4, 5)):
                    vv = emit("vector", lambda e, kt=kt, j=j, bk=bk: e.tensor_copy(
                        k_bf[:, kt * D + j * 512:kt * D + j * 512 + 512],
                        pb[bk][:, 0:512]), inc=True)
                    release(f"ps{bk}", VEs, vv)
                for j, bk in enumerate((6, 7)):
                    vv = emit("vector", lambda e, kt=kt, j=j, bk=bk: e.tensor_copy(
                        v_sb[:, kt * 1040 + j * 520:kt * 1040 + (j + 1) * 520]
                            .rearrange("p (h e) -> p h e", e=65)[:, :, 0:64],
                        pb[bk][:, 0:512].rearrange("p (h e) -> p h e", e=64)), inc=True)
                    release(f"ps{bk}", VEs, vv)
                emit("vector", lambda e, kt=kt: e.reduce_max(
                    wrk[:, WK_TMP:WK_TMP + 1], k_bf[:, kt * D:(kt + 1) * D],
                    AX.X), inc=True)
                emit("vector", lambda e: e.drain())
                emit("vector", lambda e: e.tensor_tensor(
                    wrk[:, WK_MRUN:WK_MRUN + 1], wrk[:, WK_MRUN:WK_MRUN + 1],
                    wrk[:, WK_TMP:WK_TMP + 1], ALU.max), inc=True)
                emit("vector", lambda e: e.drain())

            for t in range(NT):
                def w_h(s=s, t=t):
                    def w(c, t2buf):
                        return ("scalar", lambda e, c=c, u=t2buf: e.activation(
                            xh_sl(c, t * FT, FT), u[:], AF.Identity,
                            scale=ca1_v[:, s * KC + c:s * KC + c + 1],
                            bias=sa_v[:, s * KC + c:s * KC + c + 1]))
                    return w
                ln_tile([xh_sl(c, t * FT, FT) for c in range(KC)], w_h(),
                        chunk_wait=lambda c, vs=tuple(xdmav): (GD, vs[c]))
                for kt in range(4 * t, 4 * t + 4):
                    emit_kv(kt)
            bar()
            if s == 0:
                dbg("h_s0", xh_bf)
                bar()

            bar()
            if s == 0:
                dbg("k_s0", k_bf)
                dbg("v_s0", v_sb)
                bar()

            emit("tensor", lambda e: e.transpose(
                pb[4][0:1, 0:128], wrk[:, WK_MRUN:WK_MRUN + 1], id_f[:]), inc=True)
            vt = PEs.n
            wait("vector", PEs, vt)
            vv = emit("vector", lambda e: e.reduce_max(
                wrow[0:1, WR_M:WR_M + 1], pb[4][0:1, 0:128], AX.X), inc=True)
            wait("scalar", VEs, vv)
            vsc = emit("scalar", lambda e: e.activation(
                wrow[0:1, WR_NEGM:WR_NEGM + 1], wrow[0:1, WR_M:WR_M + 1],
                AF.Identity, scale=cv[0:1, 2:3]), inc=True)
            wait("tensor", SEs, vsc)
            emit("tensor", lambda e: e.matmul(
                pb[4][:, 0:1], onesrow_f[0:1, :], wrow[0:1, WR_NEGM:WR_NEGM + 1],
                start=True, stop=True), inc=True)
            vt2 = PEs.n
            wait("vector", PEs, vt2)
            vnm = emit("vector", lambda e: e.tensor_copy(
                wrk[:, WK_NEGM:WK_NEGM + 1], pb[4][:, 0:1]), inc=True)
            bar()

            for kt in range(KT):
                ek = eka if kt % 2 == 0 else ekb
                guard("scalar", f"ek{kt % 2}")
                vek = emit("scalar", lambda e, kt=kt, k=ek: e.activation(
                    k[:], k_bf[:, kt * D:(kt + 1) * D], AF.Exp,
                    bias=wrk[:, WK_NEGM:WK_NEGM + 1]), inc=True)
                wait("tensor", SEs, vek)
                for p in range(8):
                    if kt == 0:
                        guard("tensor", f"ps{p}")
                    emit("tensor", lambda e, kt=kt, p=p, k=ek: e.matmul(
                        pb[p][:, 0:130],
                        k[:, p * 128:(p + 1) * 128],
                        v_sb[:, kt * 1040 + p * 130:kt * 1040 + (p + 1) * 130],
                        start=(kt == 0), stop=(kt == KT - 1)),
                        inc=(p == 7))
                release(f"ek{kt % 2}", PEs, PEs.n)
            vkvacc = PEs.n
            wait("vector", PEs, vkvacc)
            guard("vector", "kv_bfs")
            for p in range(8):
                emit("vector", lambda e, p=p: e.tensor_copy(
                    den_v[0:64, 0:1], pb[p][0:64, 64:65]), inc=True)
                emit("vector", lambda e, p=p: e.tensor_copy(
                    den_v[64:128, 0:1], pb[p][64:128, 129:130]), inc=True)
                emit("vector", lambda e: e.drain())
                emit("vector", lambda e, p=p: e.scalar_tensor_tensor(
                    kvtmp[0:64, :], bvbc[0:64, p * 128:p * 128 + 64],
                    den_v[0:64, 0:1], pb[p][0:64, 0:64], ALU.mult, ALU.add), inc=True)
                emit("vector", lambda e, p=p: e.scalar_tensor_tensor(
                    kvtmp[64:128, :], bvbc[64:128, p * 128 + 64:p * 128 + 128],
                    den_v[64:128, 0:1], pb[p][64:128, 65:129], ALU.mult, ALU.add), inc=True)
                emit("vector", lambda e: e.drain())
                vv = emit("vector", lambda e: e.reciprocal(den_v[:, 1:2], den_v[:, 0:1]), inc=True)
                emit("vector", lambda e: e.drain())
                release(f"ps{p}", VEs, vv)
                wait("scalar", VEs, vv)
                vkvp = emit("scalar", lambda e, p=p: e.activation(
                    kv_bfs[:, p * HD:(p + 1) * HD], kvtmp[:, :], AF.Identity,
                    scale=den_v[:, 1:2]), inc=True)
            bar()
            if s == 0:
                dbg("kv_s0", kv_bfs)
                dbg("wrk_s0", wrk)
                bar()

            guard("gpsimd", "Wreg")
            for kk in range(KC):
                dma("gpsimd", wq_bf[:, kk * D:(kk + 1) * D],
                    wqkvT[kk * 128:(kk + 1) * 128, 0:D])
            v_wq = GD.n
            wait("tensor", GD, v_wq)
            for t in range(NT):
                for oc in range(KC):
                    bk = oc % 2
                    guard("tensor", f"ps{bk}")
                    for c in range(KC):
                        emit("tensor", lambda e, oc=oc, c=c, t=t, bk=bk: e.matmul(
                            pb[bk][:, 0:512],
                            wq_bf[:, c * D + oc * 128:c * D + oc * 128 + 128],
                            xh_sl(c, t * FT, FT),
                            start=(c == 0), stop=(c == KC - 1)), inc=(c == KC - 1))
                    vq = PEs.n
                    wait("scalar", PEs, vq)
                    vqe = emit("scalar", lambda e, oc=oc, bk=bk: e.activation(
                        q_bf[:, oc * FT:(oc + 1) * FT], pb[bk][:, 0:512], AF.Silu,
                        bias=bq_v[:, oc:oc + 1]), inc=True)
                    release(f"ps{bk}", SEs, vqe)
                vqall = SEs.n
                wait("tensor", SEs, vqall)
                for oc in range(KC):
                    bk = 2 + oc % 2
                    guard("tensor", f"ps{bk}")
                    emit("tensor", lambda e, oc=oc, bk=bk: e.matmul(
                        pb[bk][0:64, 0:512], kv_bfs[0:64, oc * HD:(oc + 1) * HD],
                        q_bf[0:64, oc * FT:(oc + 1) * FT],
                        start=True, stop=True), inc=True)
                    vy = emit("tensor", lambda e, oc=oc, bk=bk: e.matmul(
                        pb[bk][64:128, 0:512], kv_bfs[64:128, oc * HD:(oc + 1) * HD],
                        q_bf[64:128, oc * FT:(oc + 1) * FT],
                        start=True, stop=True, tile_position=(64, 64)), inc=True)
                    wait("scalar", PEs, vy)
                    vye = emit("scalar", lambda e, oc=oc, bk=bk: e.activation(
                        y_bf[:, oc * FT:(oc + 1) * FT], pb[bk][:, 0:512], AF.Identity), inc=True)
                    release(f"ps{bk}", SEs, vye)
                vyall = SEs.n

                guard("tensor", "ps4")
                for c in range(KC):
                    emit("tensor", lambda e, c=c, t=t: e.matmul(
                        pb[4][0:64, 0:512], wg1_bf[:, c * HD:(c + 1) * HD],
                        xh_sl(c, t * FT, FT),
                        start=(c == 0), stop=(c == KC - 1)), inc=(c == KC - 1))
                vgh = PEs.n
                wait("scalar", PEs, vgh)
                vghe = emit("scalar", lambda e: e.activation(
                    gh_bf[:, :], pb[4][0:64, 0:512], AF.Identity,
                    bias=bg1_v[:, 0:1]), inc=True)
                release("ps4", SEs, vghe)
                def w_z(c, t2buf):
                    return ("scalar", lambda e, c=c, u=t2buf: e.activation(
                        z_bf[:, c * FT:(c + 1) * FT], u[:], AF.Identity,
                        scale=lng_v[:, c:c + 1], bias=lnb_v[:, c:c + 1]))
                ln_tile([y_bf[:, c * FT:(c + 1) * FT] for c in range(KC)], w_z,
                        chunk_wait=lambda c, v=vyall: (SEs, v))

                wait("tensor", SEs, vghe)
                for oc in range(KC):
                    bk = 5 + oc % 2
                    guard("tensor", f"ps{bk}")
                    vg = emit("tensor", lambda e, oc=oc, bk=bk: e.matmul(
                        pb[bk][:, 0:512], wg2_bf[0:64, oc * 128:(oc + 1) * 128],
                        gh_bf[0:64, :], start=True, stop=True), inc=True)
                    wait("scalar", PEs, vg)
                    vge = emit("scalar", lambda e, oc=oc, bk=bk: e.activation(
                        gate_bf[:, oc * FT:(oc + 1) * FT], pb[bk][:, 0:512], AF.Sigmoid,
                        bias=bg2_v[:, oc:oc + 1]), inc=True)
                    release(f"ps{bk}", SEs, vge)
                    wait("vector", SEs, vge)
                    emit("vector", lambda e, oc=oc: e.tensor_tensor(
                        y_bf[:, oc * FT:(oc + 1) * FT], z_bf[:, oc * FT:(oc + 1) * FT],
                        gate_bf[:, oc * FT:(oc + 1) * FT], ALU.mult), inc=True)
                vgated = VEs.n
                wait("tensor", VEs, vgated)
                for oc in range(KC):
                    bk = oc % 2
                    xb = xf_a if oc % 2 == 0 else xf_b
                    guard("sync", f"xf{oc % 2}")
                    vxf = dma("sync", xb[:], xT[s * D + oc * 128:s * D + oc * 128 + 128,
                                                t * FT:(t + 1) * FT])
                    guard("tensor", f"ps{bk}")
                    for c in range(KC):
                        emit("tensor", lambda e, oc=oc, c=c, bk=bk: e.matmul(
                            pb[bk][:, 0:512],
                            wout_bf[:, c * D + oc * 128:c * D + oc * 128 + 128],
                            y_bf[:, c * FT:(c + 1) * FT],
                            start=(c == 0), stop=(c == KC - 1)), inc=(c == KC - 1))
                    vyo = PEs.n
                    wait("scalar", PEs, vyo)
                    guard("scalar", "e_f")
                    vef = emit("scalar", lambda e, s=s, oc=oc, bk=bk: e.activation(
                        e_f[:], pb[bk][:, 0:512], AF.Identity,
                        scale=gap_v[:, s * KC + oc:s * KC + oc + 1],
                        bias=gabo_v[:, s * KC + oc:s * KC + oc + 1]), inc=True)
                    release(f"ps{bk}", SEs, vef)
                    x2b = x2f_a if oc % 2 == 0 else x2f_b
                    guard("vector", f"x2f{oc % 2}")
                    wait("vector", SEs, vef)
                    wait("vector", LD, vxf)
                    vx2 = emit("vector", lambda e, xb=xb, x2b=x2b: e.tensor_tensor(
                        x2b[:], e_f[:], xb[:], ALU.add), inc=True)
                    release("e_f", VEs, vx2)
                    release(f"xf{oc % 2}", VEs, vx2)
                    wait("gpsimd", VEs, vx2)
                    vst = dma("gpsimd", x2T[s * D + oc * 128:s * D + oc * 128 + 128,
                                            t * FT:(t + 1) * FT], x2b[:])
                    release(f"x2f{oc % 2}", GD, vst)
            v_x2store = GD.n
            vqlast = PEs.n
            release("Wreg", PEs, vqlast)
            release("xh", PEs, vqlast)
            release("kv_bfs", PEs, vqlast)
            bar()

        qw_rowmax(wg_nat, rmxg, IC, D, [(natq, "natb"), (wtf_f, "natb2")])
        qw_rowmax(wu_nat, rmxu, IC, D, [(natq, "natb"), (wtf_f, "natb2")])
        qw_rowmax(wd_nat, rmxd, KC, DI, [(wtf_f, "natb2"), (cbc_f, "natbc")])
        emit("vector", lambda e: e.memset(wrk[:, WK_HMAX:WK_HMAX + 1], 0.0), inc=True)
        wait("gpsimd", GD, v_x2store)
        for ct in range(CT):
            s, t = ct // NT, ct % NT
            x2t = x2t_bf if ct % 2 == 0 else x2t_bf2
            x2v = []
            for c in range(KC):
                guard("gpsimd", f"x2t{ct % 2}")
                x2v.append(dma("gpsimd", x2t[:, c * FT:(c + 1) * FT],
                               x2T[s * D + c * 128:s * D + c * 128 + 128,
                                   t * FT:(t + 1) * FT]))

            def w_h2(c, t2buf, s=s, ct=ct):
                return ("scalar", lambda e, c=c, u=t2buf: e.activation(
                    h2_bf[:, c * B * S + ct * FT:c * B * S + (ct + 1) * FT], u[:], AF.Identity,
                    scale=cb1_v[:, s * KC + c:s * KC + c + 1],
                    bias=sb_v[:, s * KC + c:s * KC + c + 1]))
            lv = ln_tile([x2t[:, c * FT:(c + 1) * FT] for c in range(KC)], w_h2,
                         chunk_wait=lambda c, vs=tuple(x2v): (GD, vs[c]))
            release(f"x2t{ct % 2}", *lv)
        vh2 = SEs.n
        wait("vector", SEs, vh2)
        for c in range(KC):
            emit("vector", lambda e, c=c: e.reduce_max(
                wrk[:, WK_TMP:WK_TMP + 1], h2_bf[:, c * B * S:(c + 1) * B * S],
                AX.X, apply_absolute_value=True), inc=True)
            emit("vector", lambda e: e.drain())
            emit("vector", lambda e: e.tensor_tensor(
                wrk[:, WK_HMAX:WK_HMAX + 1], wrk[:, WK_HMAX:WK_HMAX + 1],
                wrk[:, WK_TMP:WK_TMP + 1], ALU.max), inc=True)
            emit("vector", lambda e: e.drain())
        bar()

        def collapse_and_cc(src_col, wr_dst, cci, cco):
            guard("tensor", "ps4")
            emit("tensor", lambda e, sc=src_col: e.transpose(
                pb[4][0:1, 0:128], wrk[:, sc:sc + 1], id_f[:]), inc=True)
            vt = PEs.n
            wait("vector", PEs, vt)
            vv = emit("vector", lambda e, d=wr_dst: e.reduce_max(
                wrow[0:1, d:d + 1], pb[4][0:1, 0:128], AX.X), inc=True)
            release("ps4", VEs, vv)
            emit("vector", lambda e: e.drain())
            vcp = emit("vector", lambda e, d=wr_dst: e.tensor_copy(
                ccs_sb[0:1, 0:1], wrow[0:1, d:d + 1]), inc=True)
            wait("gpsimd", VEs, vcp)
            vdm = dma("gpsimd", cci[0:1, :], ccs_sb[0:1, :])
            wait("gpsimd", GD, vdm)
            ccv = CCs.inc(1)
            E("gpsimd", lambda e, i=cci, o=cco: e.collective_compute(
                "AllReduce", ALU.max, replica_groups=[list(range(8))],
                ins=[i.ap()], outs=[o.ap()]).then_inc(CCs.h, 1))
            wait("gpsimd", CCs, ccv)
            return dma("gpsimd", ccr_sb[0:1, :], cco[0:1, :])

        vcc1 = collapse_and_cc(WK_HMAX, WR_HMAX, cc1i, cc1o)
        vsg = emit("scalar", lambda e: e.activation(
            wrow[0:1, WR_SIG:WR_SIG + 1], clip_v[0:1, 0:1], AF.Sigmoid), inc=True)
        wait("vector", GD, vcc1)
        wait("vector", SEs, vsg)
        emit("vector", lambda e: e.tensor_tensor(
            wrow[0:1, WR_SX:WR_SX + 1], wrow[0:1, WR_SIG:WR_SIG + 1],
            ccr_sb[0:1, 0:1], ALU.mult), inc=True)
        emit("vector", lambda e: e.drain())
        emit("vector", lambda e: e.tensor_scalar(
            wrow[0:1, WR_SX:WR_SX + 1], wrow[0:1, WR_SX:WR_SX + 1],
            1.0 / 127.0, None, ALU.mult), inc=True)
        emit("vector", lambda e: e.drain())
        vrsx = emit("vector", lambda e: e.reciprocal(
            wrow[0:1, WR_RSX:WR_RSX + 1], wrow[0:1, WR_SX:WR_SX + 1]), inc=True)
        emit("vector", lambda e: e.drain())
        guard("tensor", "ps4")
        wait("tensor", VEs, vrsx)
        emit("tensor", lambda e: e.matmul(
            pb[4][:, 0:2], onesrow_f[0:1, :], wrow[0:1, WR_SX:WR_SX + 2],
            start=True, stop=True), inc=True)
        vbx = PEs.n
        wait("vector", PEs, vbx)
        vsx2 = emit("vector", lambda e: e.tensor_copy(
            wrk[:, WK_SXB:WK_SXB + 2], pb[4][:, 0:2]), inc=True)
        release("ps4", VEs, vsx2)
        wait("scalar", VEs, vsx2)
        for c in range(KC):
            emit("scalar", lambda e, c=c: e.activation(
                xq_i8[:, c * B * S:(c + 1) * B * S], h2_bf[:, c * B * S:(c + 1) * B * S],
                AF.Identity, scale=wrk[:, WK_RSX:WK_RSX + 1]), inc=True)
        bar()

        for nm, tv in (("h2", h2_bf), ("xq", xq_i8), ("wrow1", wrow), ("wrk1", wrk)):
            dbg(nm, tv)
        bar()
        def quant_weight(wT_d, rmx, wq_sb, n_out, n_in_chunks, cq, cqT_sb,
                         crow_sb, cbc_sb, wtf, w8):
            oc_n = n_out // 128
            emit("vector", lambda e: e.drain())
            emit("vector", lambda e, rm=rmx, c=cq: e.reciprocal(c[:, 0:oc_n], rm[:, 0:oc_n]), inc=True)
            emit("vector", lambda e: e.drain())
            vcq = emit("vector", lambda e, c=cq: e.tensor_scalar(
                c[:, 0:oc_n], c[:, 0:oc_n], 127.0, None, ALU.mult), inc=True)
            emit("vector", lambda e: e.drain())
            guard("tensor", "ps4")
            wait("tensor", VEs, vcq)
            emit("tensor", lambda e, c=cq, o=oc_n: e.transpose(
                pb[4][0:o, 0:128], c[:, 0:o], id_f[:]), inc=True)
            vtp = PEs.n
            wait("vector", PEs, vtp)
            vct = emit("vector", lambda e, ct_=cqT_sb, o=oc_n: e.tensor_copy(
                ct_[0:o, :], pb[4][0:o, 0:128]), inc=True)
            release("ps4", VEs, vct)
            wait("sync", VEs, vct)
            guard("sync", "wtf")
            vcr = dma("sync", crow_sb[0:1, 0:n_out], cqT_sb[0:oc_n, :])
            wait("tensor", LD, vcr)
            nbk = (n_out + 511) // 512
            for j in range(nbk):
                w_ = min(512, n_out - j * 512)
                bk = 4 + (j % 2)
                guard("tensor", f"ps{bk}")
                vbc = emit("tensor", lambda e, j=j, bk=bk, cr=crow_sb, w_=w_: e.matmul(
                    pb[bk][:, 0:w_], onesrow_f[0:1, :], cr[0:1, j * 512:j * 512 + w_],
                    start=True, stop=True), inc=True)
                wait("vector", PEs, vbc)
                vcb2 = emit("vector", lambda e, j=j, bk=bk, cb=cbc_sb, w_=w_: e.tensor_copy(
                    cb[:, j * 512:j * 512 + w_], pb[bk][:, 0:w_]), inc=True)
                release(f"ps{bk}", VEs, vcb2)
            vcbc = VEs.n
            release("wtf", PEs, PEs.n)
            wait("vector", VEs, vcbc)
            for kk in range(n_in_chunks):
                guard("sync", "wtf")
                vld = dma("sync", wtf[:, 0:n_out], wT_d[kk * 128:(kk + 1) * 128, :])
                wait("vector", LD, vld)
                guard("vector", "w8")
                v8 = emit("vector", lambda e, w=wtf, cb=cbc_sb, o=n_out, x8=w8: e.scalar_tensor_tensor(
                    x8[:, 0:o], w[:, 0:o], 1.0, cb[:, 0:o], ALU.mult, ALU.mult), inc=True)
                release("wtf", VEs, v8)
                if kk % 2 == 0:
                    wait("scalar", VEs, v8)
                    vq = emit("scalar", lambda e, kk=kk, x8=w8, o=n_out, wq=wq_sb: e.activation(
                        wq[:, kk * o:(kk + 1) * o], x8[:, 0:o], AF.Identity), inc=True)
                    release("w8", SEs, vq)
                else:
                    vq = emit("vector", lambda e, kk=kk, x8=w8, o=n_out, wq=wq_sb: e.tensor_copy(
                        wq[:, kk * o:(kk + 1) * o], x8[:, 0:o]), inc=True)
                    release("w8", VEs, vq)
            wait("scalar", VEs, VEs.n)
            return SEs.n

        v_gq = quant_weight(wgTd, rmxg, wgq_bf, DI, KC, cq22, cqT22,
                            crow, cbc_f, wtf_f, wq8b)
        v_uq = quant_weight(wuTd, rmxu, wuq_bf, DI, KC, cq22, cqT22,
                            crow, cbc_f, wtf_f, wq8b)
        wait("vector", VEs, vsx2)
        emit("vector", lambda e: e.drain())
        emit("vector", lambda e: e.tensor_scalar(
            sgc_g[:], rmxg[:], wrk[:, WK_SXB:WK_SXB + 1], 1.0 / 127.0,
            ALU.mult, ALU.mult), inc=True)
        emit("vector", lambda e: e.tensor_scalar(
            susx[:], rmxu[:], wrk[:, WK_SXB:WK_SXB + 1], 1.0 / 127.0,
            ALU.mult, ALU.mult), inc=True)
        emit("vector", lambda e: e.memset(tmaxr[:], 0.0), inc=True)
        bar()

        wait("tensor", SEs, v_uq)
        v_spill_all = None
        for ct in range(CT):
            guard("vector", "xqb")
            for c in range(KC):
                emit("vector", lambda e, c=c, ct=ct: e.tensor_copy(
                    xqb[:, c * FT:(c + 1) * FT],
                    xq_i8[:, c * B * S + ct * FT:c * B * S + (ct + 1) * FT]),
                    inc=(c == KC - 1))
            vxq = VEs.n
            wait("tensor", VEs, vxq)
            for ic in range(IC):
                gbk = ic % 2
                ubk = 2 + ic % 2
                guard("tensor", f"ps{gbk}")
                for c in range(KC):
                    emit("tensor", lambda e, ic=ic, c=c, bk=gbk: e.matmul(
                        pb[bk][:, 0:512],
                        wgq_bf[:, c * DI + ic * 128:c * DI + ic * 128 + 128],
                        xqb[:, c * FT:(c + 1) * FT],
                        start=(c == 0), stop=(c == KC - 1)), inc=(c == KC - 1))
                vgmm = PEs.n
                guard("tensor", f"ps{ubk}")
                for c in range(KC):
                    emit("tensor", lambda e, ic=ic, c=c, bk=ubk: e.matmul(
                        pb[bk][:, 0:512],
                        wuq_bf[:, c * DI + ic * 128:c * DI + ic * 128 + 128],
                        xqb[:, c * FT:(c + 1) * FT],
                        start=(c == 0), stop=(c == KC - 1)), inc=(c == KC - 1))
                vumm = PEs.n
                gs = gs_a if ic % 2 == 0 else gs_b
                up = up_a if ic % 2 == 0 else up_b
                tb = tb_a if ic % 2 == 0 else tb_b
                guard("scalar", f"gs{ic % 2}")
                wait("scalar", PEs, vgmm)
                vgse = emit("scalar", lambda e, ic=ic, bk=gbk, g=gs: e.activation(
                    g[:], pb[bk][:, 0:512], AF.Silu,
                    scale=sgc_g[:, ic:ic + 1]), inc=True)
                release(f"ps{gbk}", SEs, vgse)
                guard("scalar", f"up{ic % 2}")
                wait("scalar", PEs, vumm)
                vupe = emit("scalar", lambda e, bk=ubk, u=up: e.activation(
                    u[:], pb[bk][:, 0:512], AF.Identity), inc=True)
                release(f"ps{ubk}", SEs, vupe)
                guard("vector", f"tb{ic % 2}")
                guard("vector", f"tbS{ic % 2}")
                wait("vector", SEs, vupe)
                vtb = emit("vector", lambda e, g=gs, u=up, t=tb: e.tensor_tensor(
                    t[:], g[:], u[:], ALU.mult), inc=True)
                release(f"gs{ic % 2}", VEs, vtb)
                release(f"up{ic % 2}", VEs, vtb)
                wait("sync", VEs, vtb)
                vsp = dma("sync", tsp[ic * 128:(ic + 1) * 128, ct * FT:(ct + 1) * FT],
                          tb[:], sem=STm)
                release(f"tbS{ic % 2}", STm, vsp)
                vgm = emit("vector", lambda e, t=tb, ic=ic, ct=ct: e.reduce_max(
                    tmall[:, ic * CT + ct:ic * CT + ct + 1], t[:], AX.X,
                    apply_absolute_value=True), inc=True)
                release(f"tb{ic % 2}", VEs, vgm)
            release("xqb", PEs, PEs.n)
        v_spill_all = STm.n
        bar()
        for nm, tv in (("wgq", wgq_bf), ("wuq", wuq_bf), ("xqb", xqb), ("gsb", gs_b),
                       ("upb", up_b), ("tbb", tb_b), ("cbc", cbc_f), ("tmall", tmall),
                       ("cq22", cq22), ("cqT22", cqT22)):
            dbg(nm, tv)
        bar()

        emit("vector", lambda e: e.drain())
        for ic in range(IC):
            emit("vector", lambda e, ic=ic: e.reduce_max(
                tmaxr[:, ic:ic + 1], tmall[:, ic * CT:(ic + 1) * CT], AX.X), inc=True)
        emit("vector", lambda e: e.drain())
        emit("vector", lambda e: e.tensor_tensor(tmaxr[:], tmaxr[:], susx[:], ALU.mult), inc=True)
        emit("vector", lambda e: e.drain())
        vtm = emit("vector", lambda e: e.reduce_max(
            wrk[:, WK_TMX:WK_TMX + 1], tmaxr[:], AX.X), inc=True)
        emit("vector", lambda e: e.drain())
        wait("tensor", VEs, vtm)
        vcc2 = collapse_and_cc(WK_TMX, WR_TMX, cc2i, cc2o)
        vsgd = emit("scalar", lambda e: e.activation(
            wrow[0:1, WR_SIGD:WR_SIGD + 1], clip_v[0:1, 1:2], AF.Sigmoid), inc=True)
        wait("vector", GD, vcc2)
        wait("vector", SEs, vsgd)
        emit("vector", lambda e: e.tensor_tensor(
            wrow[0:1, WR_SD:WR_SD + 1], wrow[0:1, WR_SIGD:WR_SIGD + 1],
            ccr_sb[0:1, 0:1], ALU.mult), inc=True)
        emit("vector", lambda e: e.drain())
        emit("vector", lambda e: e.tensor_scalar(
            wrow[0:1, WR_SD:WR_SD + 1], wrow[0:1, WR_SD:WR_SD + 1],
            1.0 / 127.0, None, ALU.mult), inc=True)
        emit("vector", lambda e: e.drain())
        vrsd = emit("vector", lambda e: e.reciprocal(
            wrow[0:1, WR_RSD:WR_RSD + 1], wrow[0:1, WR_SD:WR_SD + 1]), inc=True)
        emit("vector", lambda e: e.drain())
        guard("tensor", "ps4")
        wait("tensor", VEs, vrsd)
        emit("tensor", lambda e: e.matmul(
            pb[4][:, 0:2], onesrow_f[0:1, :], wrow[0:1, WR_SD:WR_SD + 2],
            start=True, stop=True), inc=True)
        vbd = PEs.n
        wait("vector", PEs, vbd)
        vsd2 = emit("vector", lambda e: e.tensor_copy(
            wrk[:, WK_SDB:WK_SDB + 2], pb[4][:, 0:2]), inc=True)
        release("ps4", VEs, vsd2)
        wait("vector", VEs, vsd2)
        emit("vector", lambda e: e.drain())
        emit("vector", lambda e: e.tensor_scalar(
            qsc[:], susx[:], wrk[:, WK_RSD:WK_RSD + 1], None, ALU.mult), inc=True)
        emit("vector", lambda e: e.drain())
        bar()

        emit("vector", lambda e: e.drain())
        emit("vector", lambda e: e.reciprocal(cq8[:], rmxd[:]), inc=True)
        emit("vector", lambda e: e.drain())
        vcq8 = emit("vector", lambda e: e.tensor_scalar(
            cq8[:], cq8[:], 127.0, None, ALU.mult), inc=True)
        emit("vector", lambda e: e.drain())
        guard("tensor", "ps4")
        wait("tensor", VEs, vcq8)
        emit("tensor", lambda e: e.transpose(pb[4][0:KC, 0:128], cq8[:], id_f[:]), inc=True)
        vtp = PEs.n
        wait("vector", PEs, vtp)
        vct = emit("vector", lambda e: e.tensor_copy(cqT8[:, :], pb[4][0:KC, 0:128]), inc=True)
        release("ps4", VEs, vct)
        wait("sync", VEs, vct)
        vcr = dma("sync", crowd[0:1, :], cqT8[:, :])
        wait("tensor", LD, vcr)
        for j in range(2):
            bk = 4 + j
            guard("tensor", f"ps{bk}")
            vbc = emit("tensor", lambda e, j=j, bk=bk: e.matmul(
                pb[bk][:, 0:512], onesrow_f[0:1, :], crowd[0:1, j * 512:(j + 1) * 512],
                start=True, stop=True), inc=True)
            wait("vector", PEs, vbc)
            vcb2 = emit("vector", lambda e, j=j, bk=bk: e.tensor_copy(
                cbcd_f[:, j * 512:(j + 1) * 512], pb[bk][:, 0:512]), inc=True)
            release(f"ps{bk}", VEs, vcb2)
        vcbc = VEs.n
        wait("vector", VEs, vcbc)
        for kk in range(IC):
            guard("sync", "wtd")
            vld = dma("sync", wtd_f[:, 0:D], wdTd[kk * 128:(kk + 1) * 128, :])
            wait("vector", LD, vld)
            guard("vector", "wd8")
            v8 = emit("vector", lambda e: e.scalar_tensor_tensor(
                wd8b[:], wtd_f[:, 0:D], 1.0, cbcd_f[:], ALU.mult, ALU.mult), inc=True)
            release("wtd", VEs, v8)
            wait("scalar", VEs, v8)
            vq = emit("scalar", lambda e, kk=kk: e.activation(
                wdq_bf[:, kk * D:(kk + 1) * D], wd8b[:], AF.Identity), inc=True)
            release("wd8", SEs, vq)
        v_dq = SEs.n
        emit("vector", lambda e: e.drain())
        emit("vector", lambda e: e.tensor_scalar(
            sc8[:], rmxd[:], wrk[:, WK_SDB:WK_SDB + 1], 1.0 / 127.0,
            ALU.mult, ALU.mult), inc=True)
        emit("vector", lambda e: e.drain())
        for b_ in range(2):
            emit("vector", lambda e, b_=b_: e.tensor_tensor(
                comb[:, b_ * KC:(b_ + 1) * KC], gbp_v[:, b_ * KC:(b_ + 1) * KC],
                sc8[:], ALU.mult), inc=True)
        emit("vector", lambda e: e.drain())
        bar()

        for nm, tv in (("rmxg", rmxg), ("rmxu", rmxu), ("rmxd", rmxd), ("sgc", sgc_g),
                       ("susx", susx), ("qsc", qsc), ("comb", comb), ("tmaxr", tmaxr),
                       ("wrow2", wrow)):
            dbg(nm, tv)
        bar()
        wait("sync", STm, v_spill_all)
        wait("tensor", SEs, v_dq)
        for ct in range(CT):
            s, t = ct // NT, ct % NT
            tqb = tqb_a if ct % 2 == 0 else tqb_b
            guard("scalar", f"tqb{ct % 2}")
            for ic in range(IC):
                trd = (trd_a, trd_b, trd_c, trd_d)[ic % 4]
                tq8 = tq8_a if ic % 2 == 0 else tq8_b
                guard("sync", f"trd{ic % 4}")
                vld = dma("sync", trd[:], tsp[ic * 128:(ic + 1) * 128,
                                             ct * FT:(ct + 1) * FT])
                guard("scalar", f"tq8{ic % 2}")
                wait("scalar", LD, vld)
                v8 = emit("scalar", lambda e, ic=ic, tr=trd, t8=tq8: e.activation(
                    t8[:], tr[:], AF.Identity, scale=qsc[:, ic:ic + 1]), inc=True)
                release(f"trd{ic % 4}", SEs, v8)
                vq = emit("scalar", lambda e, ic=ic, t8=tq8, tq=tqb: e.activation(
                    tq[:, ic * FT:(ic + 1) * FT], t8[:], AF.Identity), inc=True)
                release(f"tq8{ic % 2}", SEs, vq)
            vtq = SEs.n
            wait("tensor", SEs, vtq)
            for oc in range(KC):
                bk = oc % 4
                xb = xfd_a if oc % 2 == 0 else xfd_b
                guard("sync", f"xfd{oc % 2}")
                vxf = dma("sync", xb[:], x2T[s * D + oc * 128:s * D + oc * 128 + 128,
                                             t * FT:(t + 1) * FT])
                guard("tensor", f"ps{bk}")
                for ic in range(IC):
                    emit("tensor", lambda e, oc=oc, ic=ic, bk=bk, tq=tqb: e.matmul(
                        pb[bk][:, 0:512],
                        wdq_bf[:, ic * D + oc * 128:ic * D + oc * 128 + 128],
                        tq[:, ic * FT:(ic + 1) * FT],
                        start=(ic == 0), stop=(ic == IC - 1)), inc=(ic == IC - 1))
                vdm = PEs.n
                of = of_a if oc % 2 == 0 else of_b
                guard("vector", f"of{oc % 2}")
                wait("vector", PEs, vdm)
                wait("vector", LD, vxf)
                vof = emit("vector", lambda e, s=s, oc=oc, bk=bk, xb=xb, of=of: e.scalar_tensor_tensor(
                    of[:], pb[bk][:, 0:512], comb[:, s * KC + oc:s * KC + oc + 1],
                    xb[:], ALU.mult, ALU.add), inc=True)
                release(f"ps{bk}", VEs, vof)
                release(f"xfd{oc % 2}", VEs, vof)
                wait("gpsimd", VEs, vof)
                vst = dma("gpsimd", out_d[s * D + oc * 128:s * D + oc * 128 + 128,
                                          t * FT:(t + 1) * FT], of[:], sem=STm)
                release(f"of{oc % 2}", STm, vst)
            release(f"tqb{ct % 2}", PEs, PEs.n)
        for nm, tv in (("wdq", wdq_bf), ("tqa", tqb_a), ("tqb2", tqb_b), ("cbcd", cbcd_f), ("cq8", cq8)):
            dbg(nm, tv)
        wait("sync", STm, STm.n)
        bar()


    try:
        _sched()
    except _StopSched:
        pass

    with nc.Block() as block:
        @block.sync
        def _(e):
            for f in prog["sync"]:
                f(e)

        @block.gpsimd
        def _(e):
            for f in prog["gpsimd"]:
                f(e)

        @block.vector
        def _(e):
            for f in prog["vector"]:
                f(e)

        @block.scalar
        def _(e):
            for f in prog["scalar"]:
                f(e)

        @block.tensor
        def _(e):
            for f in prog["tensor"]:
                f(e)

    es.close()
    return nc


_NC_CACHE = []


def kernel(L0_attn_x, L0_attn_w_qkv, L0_attn_b_qkv, L0_attn_w_g1, L0_attn_b_g1,
           L0_attn_w_g2, L0_attn_b_g2, L0_attn_w_out, L0_attn_b_out,
           L0_attn_ln_g, L0_attn_ln_b, L0_ffn_clip_x, L0_ffn_clip_down,
           L0_ffn_w_gate, L0_ffn_w_up, L0_ffn_w_down, L0_c, L0_wmod):
    f = np.float32
    cc = np.ascontiguousarray

    def pp(v):
        return cc(np.asarray(v, f).reshape(-1, 128).T)

    shared = {
        "wqkvT": cc(np.asarray(L0_attn_w_qkv, f).T),
        "woutT": cc(np.asarray(L0_attn_w_out, f).T),
        "wg1T": cc(np.asarray(L0_attn_w_g1, f).T),
        "wg2T": cc(np.asarray(L0_attn_w_g2, f).T),
        "wgT": cc(np.asarray(L0_ffn_w_gate, f).T),
        "wuT": cc(np.asarray(L0_ffn_w_up, f).T),
        "wdT": cc(np.asarray(L0_ffn_w_down, f).T),
        "wg_nat": cc(np.asarray(L0_ffn_w_gate, f)),
        "wu_nat": cc(np.asarray(L0_ffn_w_up, f)),
        "wd_nat": cc(np.asarray(L0_ffn_w_down, f)),
        "wmod": cc(np.asarray(L0_wmod, f)),
        "bq": pp(np.asarray(L0_attn_b_qkv, f)[0:D]),
        "bv_row": cc(np.asarray(L0_attn_b_qkv, f)[2 * D:3 * D].reshape(1, D)),
        "bg1": cc(np.asarray(L0_attn_b_g1, f).reshape(HD, 1)),
        "bg2": pp(L0_attn_b_g2),
        "bout": pp(L0_attn_b_out),
        "lng": pp(L0_attn_ln_g),
        "lnb": pp(L0_attn_ln_b),
        "clip2": cc(np.stack([np.asarray(L0_ffn_clip_x, f).reshape(()),
                              np.asarray(L0_ffn_clip_down, f).reshape(())]).reshape(1, 2)),
    }
    x = np.asarray(L0_attn_x, f)
    c_all = np.asarray(L0_c, f)
    in_maps = []
    for core in range(8):
        xs = cc(x[2 * core:2 * core + 2].transpose(0, 2, 1).reshape(B * D, S))
        cpair = c_all[2 * core:2 * core + 2]
        cT2 = cc(cpair.reshape(2, KC, 128).transpose(2, 1, 0).reshape(128, 2 * KC))
        m = dict(shared)
        m["xT"] = xs
        m["cT"] = cT2
        in_maps.append(m)

    if not _NC_CACHE:
        _NC_CACHE.append(build_nc())
    nc = _NC_CACHE[0]
    res = run_bass_kernel_spmd(nc, in_maps, core_ids=list(range(8)))
    outs = []
    for core in range(8):
        o = np.asarray(res.results[core]["out"])
        outs.append(o.reshape(B, D, S).transpose(0, 2, 1))
    return np.concatenate(outs, axis=0).astype(np.float32)
